# version 04
# baseline (speedup 1.0000x reference)
import numpy as np
import concourse.bass as bass
import concourse.mybir as mybir
from concourse.bass_utils import run_bass_kernel_spmd
from concourse.bass_core_rust import IndirectOffsetOnAxis

F32 = mybir.dt.float32
BF16 = mybir.dt.bfloat16
AF = mybir.ActivationFunctionType
ALU = mybir.AluOpType
AX = mybir.AxisListType

S = 2048
D = 1024
NCH = 8
NBK = 4
NSEQ = 4
NCORES = 8
NEG = -1.0e9
LN_EPS = 1e-5
RMS_EPS = 1e-6
BISECT_ITERS = 42
ETA = 2.0 ** -38

_PARTS = ("attn_a", "ffn1", "attn_b", "ffn2")

_NAME_CTR = [0]


def _nm(base):
    _NAME_CTR[0] += 1
    return f"{base}_{_NAME_CTR[0]}"



def _rope_tables():
    ROPE_BASE = 10000.0
    dim = 64
    inv_freq = 1.0 / ROPE_BASE ** (np.arange(0, dim, 2, dtype=np.float32) / dim)
    t = np.arange(S, dtype=np.float32)
    freqs = np.outer(t, inv_freq)
    cos = np.concatenate([np.cos(freqs), np.cos(freqs)], -1)[:, ::2]
    sin = np.concatenate([np.sin(freqs), np.sin(freqs)], -1)[:, ::2]
    return (np.ascontiguousarray(cos.T).astype(np.float32),
            np.ascontiguousarray(sin.T).astype(np.float32))


def _chunked_vec(v):
    return np.ascontiguousarray(np.asarray(v, np.float32).reshape(NCH, 128).T)


def _dev3(w, pc=128):
    w = np.asarray(w, np.float32)
    c = w.shape[0] // pc
    return np.ascontiguousarray(w.reshape(c, pc, w.shape[1]).transpose(1, 0, 2))


def prep_inputs(kw):
    f = lambda a: np.asarray(a, dtype=np.float32)
    out = {}
    out["a_wdqT"] = _dev3(f(kw["L0_attn_a_w_down_q"]).T)
    out["a_wdkvT"] = _dev3(f(kw["L0_attn_a_w_down_kv"]).T)
    perm = np.concatenate([np.arange(0, 64, 2), np.arange(1, 64, 2)])
    fullp = np.concatenate([hh * 64 + perm for hh in range(16)])
    out["a_wupqT"] = _dev3(f(kw["L0_attn_a_w_up_q"]).T[:, fullp])
    out["a_wupkT"] = _dev3(f(kw["L0_attn_a_w_up_k"]).T[:, fullp])
    out["a_wupvT"] = _dev3(f(kw["L0_attn_a_w_up_v"]).T)
    out["a_woT"] = _dev3(f(kw["L0_attn_a_w_out"]).T)
    out["a_lns"] = _chunked_vec(kw["L0_attn_a_ln_scale"])
    out["a_lnb"] = _chunked_vec(kw["L0_attn_a_ln_bias"])
    cosT, sinT = _rope_tables()
    tile4 = lambda a: np.ascontiguousarray(np.tile(a, (4, 1)))
    out["r_cos"] = tile4(cosT)
    out["r_sin"] = tile4(sinT)
    out["f_wgT"] = _dev3(f(kw["L0_ffn_wg"]).T)
    out["f_bg"] = np.ascontiguousarray(
        np.broadcast_to(f(kw["L0_ffn_bg"])[None, :], (128, 8)))
    WgT = [_dev3(f(kw["L0_ffn_Wgate"])[e].T) for e in range(8)]
    WuT = [_dev3(f(kw["L0_ffn_Wup"])[e].T) for e in range(8)]
    WdT = [_dev3(f(kw["L0_ffn_Wdown"])[e].T) for e in range(8)]
    out["f_WgS"] = np.ascontiguousarray(np.stack(
        [WgT[e][:, :, hb * 512:(hb + 1) * 512] for e in range(8) for hb in range(8)]))
    out["f_WuS"] = np.ascontiguousarray(np.stack(
        [WuT[e][:, :, hb * 512:(hb + 1) * 512] for e in range(8) for hb in range(8)]))
    out["f_WdS"] = np.ascontiguousarray(np.stack(
        [WdT[e][:, sl * 4:(sl + 1) * 4, :] for e in range(8) for sl in range(8)]))
    out["iotaT"] = np.ascontiguousarray(
        np.broadcast_to(np.arange(16, dtype=np.float32)[None, :] * 512.0, (8, 16)))
    out["piota"] = np.arange(128, dtype=np.float32).reshape(128, 1)
    out["b_wq"] = _dev3(f(kw["L0_attn_b_wq"]))
    out["b_wk"] = _dev3(f(kw["L0_attn_b_wk"]))
    out["b_wv"] = _dev3(f(kw["L0_attn_b_wv"]))
    out["b_wo"] = _dev3(f(kw["L0_attn_b_wo"]))
    out["b_iwq"] = _dev3(f(kw["L0_attn_b_idx_wq"]))
    out["b_iwk"] = _dev3(f(kw["L0_attn_b_idx_wk"]))
    out["b_iww"] = _dev3(f(kw["L0_attn_b_idx_ww"]))
    out["b_gw"] = _dev3(f(kw["L0_attn_b_gate_w"]))
    out["b_n1"] = _chunked_vec(kw["L0_attn_b_norm1_w"])
    out["b_n2"] = _chunked_vec(kw["L0_attn_b_norm2_w"])
    out["b_w1"] = np.ascontiguousarray(
        np.stack([_dev3(f(kw["L0_attn_b_w1"])[e]) for e in range(4)]))
    out["b_w2"] = np.ascontiguousarray(
        np.stack([_dev3(f(kw["L0_attn_b_w2"])[e]) for e in range(4)]))
    b1 = f(kw["L0_attn_b_b1"])
    b2 = f(kw["L0_attn_b_b2"])
    out["b_b1"] = np.ascontiguousarray(
        np.stack([b1[e].reshape(32, 128).T for e in range(4)]))
    out["b_b2"] = np.ascontiguousarray(
        np.stack([b2[e].reshape(8, 128).T for e in range(4)]))
    return out



_MARKS = {}


class SemCtr:
    def __init__(self, h):
        self.h = h
        self.val = 0

    def inc(self, inst, n=1):
        inst.then_inc(self.h, n)
        self.val += n
        try:
            _MARKS[inst.ins.name] = (self, self.val)
        except Exception:
            pass


class P:
    def __init__(self, nseq, parts):
        self.nseq = nseq
        self.parts = parts
        self.nc = bass.Bass("TRN2", target_bir_lowering=False)
        self.sems = {}
        self.h = {}
        self.t = {}

    def sem(self, name):
        if name not in self.sems:
            self.sems[name] = SemCtr(self.nc.semaphore(name).__enter__())
        return self.sems[name]


def nb512(nb):
    return slice(nb * 512, (nb + 1) * 512)


def edge(p, eng, ins, name="ch"):
    s = p.sem(name)
    s.inc(ins)
    eng.wait_ge(s.h, s.val)


_SER_SKIP = {"wait_ge", "engine_nop", "nop", "dma_start", "indirect_dma_start",
             "wait_op", "sem_inc"}


class SerialEngine:

    def __init__(self, p, eng, name="ch"):
        self.__dict__["_p"] = p
        self.__dict__["_e"] = eng
        self.__dict__["_s"] = p.sem(name)
        self.__dict__["_pend"] = None

    def _fence(self):
        pend = self.__dict__["_pend"]
        if pend is None:
            return
        try:
            mark = _MARKS.pop(pend.ins.name, None)
        except Exception:
            mark = None
        if mark is not None:
            sctr, val = mark
            self._e.wait_ge(sctr.h, val)
        else:
            self._s.inc(pend)
            self._e.wait_ge(self._s.h, self._s.val)
        self.__dict__["_pend"] = None

    def __getattr__(self, k):
        a = getattr(self.__dict__["_e"], k)
        if not callable(a) or k in _SER_SKIP or k.startswith("_"):
            return a
        def wrap(*args, **kw):
            self._fence()
            ins = a(*args, **kw)
            self.__dict__["_pend"] = ins
            return ins
        return wrap



def emit_consts(p):
    nc, t, h = p.nc, p.t, p.h
    w = p.sem("wg")
    t["r_neg"] = nc.gpsimd.alloc_register("r_negp")
    t["r_bc"] = nc.gpsimd.alloc_register("r_bcp")
    with nc.Block() as b:
        @b.gpsimd
        def _(g):
            g = SerialEngine(p, g)
            g.reg_mov(t["r_neg"], float(NEG))
            g.reg_mov(t["r_bc"], NT * 512 - 1)
            g.memset(t["ident"][:, :], 1.0)
            g.memset(t["identf"][:, :], 1.0)
            g.memset(t["id64"][:, :], 1.0)
            g.memset(t["ones_c"][:, :], 1.0)
            g.memset(t["ones_r"][:, :], 1.0)
            g.memset(t["sel8"][:, :], 1.0)
            g.memset(t["sel4"][:, :], 1.0)
            g.memset(t["ones128"][:, :], 1.0)
            g.memset(t["ltri"][:, :], 1.0)
            g.memset(t["l8"][:, :], 1.0)
            for val, tile in t["cpool"].items():
                g.memset(tile[:, :], float(val)).then_inc(p.sem("gp").h, 0) \
                    if False else g.memset(tile[:, :], float(val))
            gp = p.sem("gp")
            gp.inc(g.engine_nop())
            g.wait_ge(gp.h, gp.val)
            g.affine_select(t["ident"][:, :], t["ident"][:, :], [[-1, 128]],
                            ALU.is_equal, 0.0, base=0, channel_multiplier=1)
            g.affine_select(t["identf"][:, :], t["identf"][:, :], [[-1, 128]],
                            ALU.is_equal, 0.0, base=0, channel_multiplier=1)
            g.affine_select(t["id64"][0:64, :], t["id64"][0:64, :], [[-1, 64]],
                            ALU.is_equal, 0.0, base=0, channel_multiplier=1)
            g.affine_select(t["id64"][64:128, :], t["id64"][64:128, :], [[-1, 64]],
                            ALU.is_equal, 0.0, base=0, channel_multiplier=1)
            g.affine_select(t["sel8"][:, :], t["sel8"][:, :], [[-1, 8], [0, 128]],
                            ALU.is_equal, 0.0, base=0, channel_multiplier=1)
            g.affine_select(t["sel4"][:, :], t["sel4"][:, :], [[-1, 4], [0, 128]],
                            ALU.is_equal, 0.0, base=0, channel_multiplier=1)
            g.affine_select(t["ltri"][:, :], t["ltri"][:, :], [[1, 128]],
                            ALU.is_ge, 0.0, base=-1, channel_multiplier=-1)
            g.affine_select(t["l8"][:, :], t["l8"][:, :], [[1, 8]],
                            ALU.is_ge, 0.0, base=-1, channel_multiplier=-1)
            for name, dst in [("a_lns", "lns"), ("a_lnb", "lnb"), ("b_n1", "bn1"),
                              ("b_n2", "bn2"), ("f_bg", "fbg")]:
                g.dma_start(t[dst][:, :], h[name]).then_inc(w.h, 16)
                w.val += 16
            for name, dst in [("f_wgT", "fwgT"), ("b_gw", "bgw"), ("b_iww", "biww")]:
                g.dma_start(t[dst][:, :, :], h[name]).then_inc(w.h, 16)
                w.val += 16
            g.dma_start(t["bb1"][:, :, :],
                        h["b_b1"].rearrange("e p c -> p e c")).then_inc(w.h, 16)
            g.dma_start(t["bb2"][:, :, :],
                        h["b_b2"].rearrange("e p c -> p e c")).then_inc(w.h, 16)
            g.dma_start(t["iotaT"][:, :], h["iotaT"]).then_inc(w.h, 16)
            g.dma_start(t["piota"][:, :], h["piota"]).then_inc(w.h, 16)
            w.val += 64
            g.wait_ge(w.h, w.val)


def emit_wconv(p):
    nc, t, h = p.nc, p.t, p.h
    wv = p.sem("wcv")
    wz = p.sem("wcz")
    with nc.Block() as b, \
         nc.sbuf_tensor(_nm("wc_a"), [128, 8, 512], BF16) as ba, \
         nc.sbuf_tensor(_nm("wc_b"), [128, 8, 512], BF16) as bb, \
         nc.sbuf_tensor(_nm("wc_z"), [128, 4096], BF16) as bz:

        @b.gpsimd
        def _(g):
            g = SerialEngine(p, g)
            wz.inc(g.memset(bz[:, :], 0.0))
            g.wait_ge(wz.h, wz.val)
            for i in range(16):
                g.dma_start(h["xg_dram"][i * 512:(i + 1) * 512].rearrange(
                    "(a p) n -> p a n", p=128), bz.rearrange("p (a n) -> p a n", a=4)
                ).then_inc(wv.h, 16)
                wv.val += 16
            bufs = [ba, bb]
            k = 0
            for name, dst in (("f_WgS", "wg_bf"), ("f_WuS", "wu_bf")):
                for s_ in range(64):
                    buf = bufs[k % 2]
                    g.dma_start(buf[:, :, :], h[name][s_]).then_inc(wv.h, 16)
                    wv.val += 16
                    g.wait_ge(wv.h, wv.val)
                    g.dma_start(h[dst][s_], buf[:, :, :]).then_inc(wv.h, 16)
                    wv.val += 16
                    k += 1
            bd = [ba.rearrange("p a n -> p (a n)").rearrange("p (a n) -> p a n", a=4),
                  bb.rearrange("p a n -> p (a n)").rearrange("p (a n) -> p a n", a=4)]
            for s_ in range(64):
                buf = bd[k % 2]
                g.dma_start(buf[:, :, :], h["f_WdS"][s_]).then_inc(wv.h, 16)
                wv.val += 16
                g.wait_ge(wv.h, wv.val)
                g.dma_start(h["wd_bf"][s_], buf[:, :, :]).then_inc(wv.h, 16)
                wv.val += 16
                k += 1
            g.wait_ge(wv.h, wv.val)


def emit_load_x(p, sq, x_sb):
    nc = p.nc
    sx = p.sem("wx")
    with nc.Block() as b:
        @b.sync
        def _(sy):
            sy.dma_start(x_sb[:, :, :],
                         p.h["xin"][sq].rearrange("(c p) s -> p c s", p=128)
                         ).then_inc(sx.h, 16)
            sx.val += 16
            sy.wait_ge(sx.h, sx.val)


def emit_store_x(p, sq, x_sb):
    nc = p.nc
    st = p.sem("so")
    with nc.Block() as b:
        @b.sync
        def _(sy):
            sy.dma_start(p.h["xout"][sq].rearrange("(c p) s -> p c s", p=128),
                         x_sb[:, :, :]).then_inc(st.h, 16)
            st.val += 16



def emit_stats(p, src, murep, rsrep, eps, mode, mu1rep=None, rs1rep=None):
    nc, t = p.nc, p.t
    want_mean = mode in ("x", "ln")
    derived = mode in ("ln", "lnrms")
    s_src = p.sem("sts")
    s_sq = p.sem("stq")
    s_mm = p.sem("stm")
    s_row = p.sem("str")
    s_rs = p.sem("stz")
    s_rep = p.sem("rpm")
    NU = NCH * 2
    HC = S // 2

    def u_src(u):
        c, hf = divmod(u, 2)
        return src[:, c, hf * HC:(hf + 1) * HC]

    with nc.sbuf_tensor(_nm("st_rows"), [65, S], F32) as rows_t:
        r_mu = rows_t[0:1, :]
        r_var = rows_t[32:33, :]
        r_scr = rows_t[64:65, :]
        with nc.Block() as b, \
             nc.psum_tensor(_nm("ps_st"), [1, 2 * S], F32) as ps_st, \
             nc.sbuf_tensor(_nm("st_sqa"), [128, HC], F32) as sqa, \
             nc.sbuf_tensor(_nm("st_sqb"), [128, HC], F32) as sqb, \
             nc.sbuf_tensor(_nm("st_sra"), [128, HC], F32) as sra, \
             nc.sbuf_tensor(_nm("st_srb"), [128, HC], F32) as srb:

            b_src, b_sq, b_mm, b_row = s_src.val, s_sq.val, s_mm.val, s_row.val

            @b.vector
            def _(v):
                v = SerialEngine(p, v)
                if derived:
                    for u in range(NU):
                        c, hf = divmod(u, 2)
                        scr = sra if u % 2 == 0 else srb
                        if u >= 2:
                            v.wait_ge(s_mm.h, b_mm + (u - 1))
                        sl = slice(hf * HC, (hf + 1) * HC)
                        v.tensor_tensor(scr[:, :], u_src(u), mu1rep[:, sl],
                                        ALU.subtract)
                        ins = v.tensor_tensor(scr[:, :], scr[:, :], rs1rep[:, sl],
                                              ALU.mult)
                        s_src.inc(ins)
                v.wait_ge(s_mm.h, b_mm + NU)
                if want_mean:
                    v.tensor_scalar_mul(r_mu, ps_st[:, 0:S], 1.0 / D)
                s_row.inc(v.tensor_scalar_mul(r_var, ps_st[:, S:2 * S], 1.0 / D))

            @b.scalar
            def _(sc):
                sc = SerialEngine(p, sc)
                for u in range(NU):
                    sq = sqa if u % 2 == 0 else sqb
                    if derived:
                        sc.wait_ge(s_src.h, b_src + u + 1)
                        src_u = (sra if u % 2 == 0 else srb)[:, :]
                    else:
                        src_u = u_src(u)
                    if u >= 2:
                        sc.wait_ge(s_mm.h, b_mm + (u - 1))
                    s_sq.inc(sc.activation(sq[:, :], src_u, AF.Square))

            @b.tensor
            def _(tt):
                for u in range(NU):
                    c, hf = divmod(u, 2)
                    tt.wait_ge(s_sq.h, b_sq + u + 1)
                    sq = sqa if u % 2 == 0 else sqb
                    scr = (sra if u % 2 == 0 else srb) if derived else None
                    ins = None
                    for nb in range(2):
                        off = hf * HC + nb * 512
                        if want_mean:
                            rhs = scr[:, nb * 512:(nb + 1) * 512] if derived \
                                else src[:, c, off:off + 512]
                            tt.matmul(ps_st[:, off:off + 512], t["ones_c"][:, :], rhs,
                                      start=(c == 0), stop=(c == NCH - 1))
                        ins = tt.matmul(ps_st[:, S + off:S + off + 512],
                                        t["ones_c"][:, :],
                                        sq[:, nb * 512:(nb + 1) * 512],
                                        start=(c == 0), stop=(c == NCH - 1))
                    s_mm.inc(ins)

        with nc.Block() as b, nc.psum_tensor(_nm("ps_rep"), [128, 2 * S], F32) as ps_rep:
            b_rep = s_rep.val
            b_row2 = p.sem("sr2").val
            s_row2 = p.sem("sr2")

            @b.tensor
            def _(tt):
                tt.wait_ge(s_row.h, s_row.val)
                ins = None
                for nb in range(NBK):
                    if want_mean:
                        tt.matmul(ps_rep[:, nb512(nb)], t["ones_r"][0:1, :],
                                  r_mu[:, nb512(nb)], start=True, stop=True)
                    ins = tt.matmul(ps_rep[:, S + nb * 512:S + (nb + 1) * 512],
                                    t["ones_r"][32:33, :],
                                    r_var[:, nb512(nb)], start=True, stop=True)
                s_rep.inc(ins)

            @b.vector
            def _(v):
                v = SerialEngine(p, v)
                v.wait_ge(s_rep.h, b_rep + 1)
                if want_mean:
                    edge(p, v, v.tensor_copy(murep[:, :], ps_rep[:, 0:S]))
                    edge(p, v, v.tensor_tensor(rsrep[:, :], murep[:, :], murep[:, :],
                                               ALU.mult))
                    s_row2.inc(v.tensor_tensor(rsrep[:, :], ps_rep[:, S:2 * S],
                                               rsrep[:, :], ALU.subtract))
                else:
                    s_row2.inc(v.tensor_copy(rsrep[:, :], ps_rep[:, S:2 * S]))

            @b.scalar
            def _(sc):
                sc = SerialEngine(p, sc)
                sc.wait_ge(s_row2.h, b_row2 + 1)
                edge(p, sc, sc.activation(ps_rep[:, 0:S], rsrep[:, :], AF.Ln,
                                          bias=float(eps)))
                s_rs.inc(sc.activation(rsrep[:, :], ps_rep[:, 0:S], AF.Exp,
                                       scale=-0.5))


def emit_ln_apply(p, src, murep, rsrep, out_bf, mu2rep=None, rs2rep=None,
                  gamma=None, beta=None, add_inner_residual=False):
    nc = p.nc
    with nc.Block() as b, nc.sbuf_tensor(_nm("lnt"), [128, S], F32) as lnt:
        @b.vector
        def _(v):
            v = SerialEngine(p, v)
            for c in range(NCH):
                if murep is not None:
                    edge(p, v, v.tensor_tensor(lnt[:, :], src[:, c, :], murep[:, :],
                                               ALU.subtract))
                    edge(p, v, v.tensor_tensor(lnt[:, :], lnt[:, :], rsrep[:, :],
                                               ALU.mult))
                else:
                    edge(p, v, v.tensor_tensor(lnt[:, :], src[:, c, :], rsrep[:, :],
                                               ALU.mult))
                if mu2rep is not None:
                    edge(p, v, v.tensor_tensor(lnt[:, :], lnt[:, :], mu2rep[:, :],
                                               ALU.subtract))
                if rs2rep is not None:
                    edge(p, v, v.tensor_tensor(lnt[:, :], lnt[:, :], rs2rep[:, :],
                                               ALU.mult))
                if gamma is not None and beta is not None:
                    v.tensor_scalar(out_bf[:, c, :], lnt[:, :], gamma[:, c:c + 1],
                                    beta[:, c:c + 1], ALU.mult, ALU.add)
                elif gamma is not None:
                    v.tensor_single_scalar(out_bf[:, c, :], lnt[:, :],
                                           gamma[:, c:c + 1], ALU.mult)
                else:
                    v.tensor_copy(out_bf[:, c, :], lnt[:, :])
            if add_inner_residual:
                for c in range(NCH):
                    edge(p, v, v.tensor_tensor(lnt[:, :], src[:, c, :], murep[:, :],
                                               ALU.subtract))
                    edge(p, v, v.tensor_tensor(lnt[:, :], lnt[:, :], rsrep[:, :],
                                               ALU.mult))
                    edge(p, v, v.tensor_tensor(src[:, c, :], src[:, c, :], lnt[:, :],
                                               ALU.add))




def emit_attn_a(p, x_sb):
    nc, t, h = p.nc, p.t, p.h

    if True:
        cq_g = nc.sbuf_tensor(_nm("a_cq"), [128, 4, S], BF16, side="right")
        ckv_g = nc.sbuf_tensor(_nm("a_ckv"), [128, 4, S], BF16, side="right")
        if True:

            with nc.sbuf_tensor(_nm("a_xn"), [128, NCH, S], BF16) as xn_bf:
                with nc.sbuf_tensor(_nm("a_mu1"), [128, S], F32) as mu1, \
                     nc.sbuf_tensor(_nm("a_rs1"), [128, S], F32) as rs1, \
                     nc.sbuf_tensor(_nm("a_mu2"), [128, S], F32) as mu2, \
                     nc.sbuf_tensor(_nm("a_rs2"), [128, S], F32) as rs2:
                    emit_stats(p, x_sb, mu1, rs1, LN_EPS, "x")
                    emit_stats(p, x_sb, mu2, rs2, LN_EPS, "ln",
                               mu1rep=mu1, rs1rep=rs1)
                    emit_ln_apply(p, x_sb, mu1, rs1, xn_bf, mu2rep=mu2,
                                  rs2rep=rs2, gamma=t["lns"], beta=t["lnb"],
                                  add_inner_residual=True)
                cq_bf = cq_g.__enter__()
                ckv_bf = ckv_g.__enter__()

                s_w, s_m, s_e = p.sem("wg"), p.sem("cqm"), p.sem("cqe")
                with nc.Block() as b, \
                     nc.psum_tensor(_nm("ps_cq"), [128, 2, 512], F32) as ps, \
                     nc.sbuf_tensor(_nm("cq_wa"), [128, 8, 512], BF16) as wdq, \
                     nc.sbuf_tensor(_nm("cq_wb"), [128, 8, 512], BF16) as wdkv:
                    b_w, b_m, b_e = s_w.val, s_m.val, s_e.val

                    @b.gpsimd
                    def _(g):
                        g = SerialEngine(p, g)
                        g.dma_start(wdq[:, :, :], h["a_wdqT"]).then_inc(s_w.h, 16)
                        g.dma_start(wdkv[:, :, :], h["a_wdkvT"]).then_inc(s_w.h, 16)
                        s_w.val += 32

                    @b.tensor
                    def _(tt):
                        ui = 0
                        for which, slab in ((0, wdq), (1, wdkv)):
                            tt.wait_ge(s_w.h, b_w + 32)
                            for lc in range(4):
                                for nb in range(NBK):
                                    if ui >= 2:
                                        tt.wait_ge(s_e.h, b_e + ui - 1)
                                    ins = None
                                    for c in range(NCH):
                                        ins = tt.matmul(
                                            ps[:, ui % 2, :],
                                            slab[:, c, lc * 128:(lc + 1) * 128],
                                            xn_bf[:, c, nb512(nb)],
                                            start=(c == 0), stop=(c == NCH - 1))
                                    s_m.inc(ins)
                                    ui += 1

                    @b.scalar
                    def _(sc):
                        sc = SerialEngine(p, sc)
                        ui = 0
                        for which, dst in ((0, cq_bf), (1, ckv_bf)):
                            for lc in range(4):
                                for nb in range(NBK):
                                    sc.wait_ge(s_m.h, b_m + ui + 1)
                                    s_e.inc(sc.activation(dst[:, lc, nb512(nb)],
                                                          ps[:, ui % 2, :],
                                                          AF.Identity))
                                    ui += 1

            with nc.sbuf_tensor(_nm("a_ao"), [128, NCH, S], BF16) as ao_bf, \
                 nc.sbuf_tensor(_nm("a_rc"), [128, S], BF16) as rcos, \
                 nc.sbuf_tensor(_nm("a_rs_t"), [128, S], BF16) as rsin:
                sg = p.sem("wg")
                with nc.Block() as b:
                    @b.gpsimd
                    def _(g):
                        g = SerialEngine(p, g)
                        g.dma_start(rcos[:, :], h["r_cos"]).then_inc(sg.h, 16)
                        g.dma_start(rsin[:, :], h["r_sin"]).then_inc(sg.h, 16)
                        sg.val += 32
                        g.wait_ge(sg.h, sg.val)

                for hp in range(8):
                    _attn_a_hp(p, hp, cq_bf, ckv_bf, ao_bf, (rcos, rsin))

                ckv_g.__exit__(None, None, None)
                cq_g.__exit__(None, None, None)
                _attn_out_residual(p, x_sb, ao_bf, h["a_woT"], None, None,
                                   xm_bf=None)


def _attn_a_hp(p, hp, cq_bf, ckv_bf, ao_bf, ropes):
    nc, t, h = p.nc, p.t, p.h
    rcos, rsin = ropes

    with nc.sbuf_tensor(_nm("a_qr"), [128, S], BF16) as qraw, \
         nc.sbuf_tensor(_nm("a_kr"), [128, S], BF16) as kraw, \
         nc.sbuf_tensor(_nm("a_qb"), [128, S], BF16) as q_bf, \
         nc.sbuf_tensor(_nm("a_kb"), [128, S], BF16) as k_bf, \
         nc.sbuf_tensor(_nm("a_vb"), [128, S], BF16) as v_bf, \
         nc.sbuf_tensor(_nm("a_vT"), [128, 16, 128], BF16) as vT_bf, \
         nc.sbuf_tensor(_nm("a_stg"), [32, 2 * S], BF16) as stg:

        s_w, s_m, s_e = p.sem("wg"), p.sem("qkm"), p.sem("qke")
        with nc.Block() as b, \
             nc.psum_tensor(_nm("ps_qkv"), [128, 2, 512], F32) as ps, \
             nc.sbuf_tensor(_nm("a_wup"), [128, 4, 384], BF16) as wup:
            b_w, b_m, b_e = s_w.val, s_m.val, s_e.val

            @b.gpsimd
            def _(g):
                g = SerialEngine(p, g)
                g.dma_start(wup[:, :, 0:128],
                            h["a_wupqT"][:, :, hp * 128:(hp + 1) * 128]).then_inc(s_w.h, 16)
                g.dma_start(wup[:, :, 128:256],
                            h["a_wupkT"][:, :, hp * 128:(hp + 1) * 128]).then_inc(s_w.h, 16)
                g.dma_start(wup[:, :, 256:384],
                            h["a_wupvT"][:, :, hp * 128:(hp + 1) * 128]).then_inc(s_w.h, 16)
                s_w.val += 48

            @b.tensor
            def _(tt):
                tt.wait_ge(s_w.h, b_w + 48)
                ui = 0
                for which in range(3):
                    srcb = cq_bf if which == 0 else ckv_bf
                    for nb in range(NBK):
                        if ui >= 2:
                            tt.wait_ge(s_e.h, b_e + ui - 1)
                        ins = None
                        for c in range(4):
                            ins = tt.matmul(ps[:, ui % 2, :],
                                            wup[:, c, which * 128:(which + 1) * 128],
                                            srcb[:, c, nb512(nb)],
                                            start=(c == 0), stop=(c == 3))
                        s_m.inc(ins)
                        ui += 1

            @b.scalar
            def _(sc):
                sc = SerialEngine(p, sc)
                ui = 0
                for which, dst in ((0, qraw), (1, kraw), (2, v_bf)):
                    for nb in range(NBK):
                        sc.wait_ge(s_m.h, b_m + ui + 1)
                        s_e.inc(sc.activation(dst[:, nb512(nb)], ps[:, ui % 2, :],
                                              AF.Identity,
                                              scale=0.125 if which == 0 else 1.0))
                        ui += 1

        s_vm, s_ve = p.sem("vtm"), p.sem("vte")
        with nc.Block() as b, nc.psum_tensor(_nm("ps_vT"), [128, 4, 1024], BF16) as psv:
            b_vm, b_ve = s_vm.val, s_ve.val

            @b.vector
            def _(v):
                v = SerialEngine(p, v)
                for srcq, dst in ((qraw, q_bf), (kraw, k_bf)):
                    for hh in range(2):
                        B = hh * 64
                        e = srcq[B:B + 32, :]
                        o = srcq[B + 32:B + 64, :]
                        v.tensor_tensor(stg[:, 0:S], e, rcos[B:B + 32, :], ALU.mult)
                        edge(p, v, v.tensor_tensor(stg[:, S:2 * S], o,
                                                   rsin[B + 32:B + 64, :], ALU.mult))
                        edge(p, v, v.tensor_tensor(dst[B:B + 32, :], stg[:, 0:S],
                                                   stg[:, S:2 * S], ALU.subtract))
                        v.tensor_tensor(stg[:, 0:S], e, rsin[B:B + 32, :], ALU.mult)
                        edge(p, v, v.tensor_tensor(stg[:, S:2 * S], o,
                                                   rcos[B + 32:B + 64, :], ALU.mult))
                        edge(p, v, v.tensor_tensor(dst[B + 32:B + 64, :], stg[:, 0:S],
                                                   stg[:, S:2 * S], ALU.add))

            @b.tensor
            def _(tt):
                k = 0
                for sc_ in range(16):
                    for hh in range(2):
                        if k >= 4:
                            tt.wait_ge(s_ve.h, b_ve + k - 3)
                        s_vm.inc(tt.transpose(psv[:, k % 4, 0:64],
                                              v_bf[hh * 64:(hh + 1) * 64,
                                                   sc_ * 128:(sc_ + 1) * 128],
                                              t["id64"][hh * 64:(hh + 1) * 64, :]))
                        k += 1

            @b.scalar
            def _(sc):
                sc = SerialEngine(p, sc)
                k = 0
                for sc_ in range(16):
                    for hh in range(2):
                        sc.wait_ge(s_vm.h, b_vm + k + 1)
                        s_ve.inc(sc.activation(vT_bf[:, sc_, hh * 64:(hh + 1) * 64],
                                               psv[:, k % 4, 0:64], AF.Identity))
                        k += 1

        _attn_inner(p, hp, q_bf, k_bf, vT_bf, ao_bf, masked=None)


def _attn_inner(p, hp, q_bf, k_bf, vT_bf, ao_bf, masked=None):
    nc, t = p.nc, p.t
    asc, atr, aav = p.sem("asc"), p.sem("atr"), p.sem("aav")
    amx, apb, aao = p.sem("amx"), p.sem("apb"), p.sem("aao")
    aex, aat, arc = p.sem("aex"), p.sem("aat"), p.sem("arc")
    sy_m = [p.sem("wy0"), p.sem("wy1")]

    with nc.Block() as b, \
         nc.psum_tensor(_nm("ps_sc"), [128, 4, 512], F32) as ps_sc, \
         nc.psum_tensor(_nm("ps_tr"), [128, 16, 128], BF16) as ps_tr, \
         nc.psum_tensor(_nm("ps_av"), [128, 2, 512], F32) as ps_av, \
         nc.sbuf_tensor(_nm("at_exp"), [128, S], BF16) as exp_f, \
         nc.sbuf_tensor(_nm("at_pb"), [128, S], BF16) as probs, \
         nc.sbuf_tensor(_nm("at_aT"), [128, 16, 128], BF16) as attnT, \
         nc.sbuf_tensor(_nm("at_red"), [128, 6], F32) as red:
        scm_g = nc.sbuf_tensor(_nm("at_scm"), [128, S], F32) \
            if masked is not None else None
        scm = scm_g.__enter__() if scm_g is not None else None

        B_sc, B_tr, B_av = asc.val, atr.val, aav.val
        B_mx, B_pb, B_ao = amx.val, apb.val, aao.val
        B_ex, B_at, B_rc = aex.val, aat.val, arc.val
        B_sy = [sy_m[0].val, sy_m[1].val]
        niter = 32
        ps_flat = ps_sc.rearrange("p b n -> p (b n)")

        if masked is not None:
            @b.sync
            def _(sy):
                for qt in range(16):
                    if qt >= 2:
                        sy.wait_ge(amx.h, B_mx + 2 * (qt - 1))
                    masked["load"](sy, qt, qt % 2)

        @b.tensor
        def _(tt):
            def tr_av(j):
                qt, hh = divmod(j, 2)
                Bh = hh * 64
                tt.wait_ge(apb.h, B_pb + j + 1)
                if j >= 1:
                    tt.wait_ge(aat.h, B_at + j)
                ins_t = None
                for sc_ in range(16):
                    ins_t = tt.transpose(ps_tr[:, sc_, :],
                                         probs[:, sc_ * 128:(sc_ + 1) * 128],
                                         t["ident"][:, :])
                atr.inc(ins_t)
                tt.wait_ge(aat.h, B_at + j + 1)
                if j >= 2:
                    tt.wait_ge(aao.h, B_ao + j - 1)
                ins = None
                for sc_ in range(16):
                    ins = tt.matmul(ps_av[0:64, j % 2, 0:128], vT_bf[:, sc_, Bh:Bh + 64],
                                    attnT[:, sc_, :], start=(sc_ == 0), stop=(sc_ == 15))
                aav.inc(ins)

            for i in range(niter):
                qt, hh = divmod(i, 2)
                Bh = hh * 64
                if i >= 1:
                    tt.wait_ge(aex.h, B_ex + i)
                ins = None
                for nb in range(NBK):
                    ins = tt.matmul(ps_sc[:, nb, :],
                                    q_bf[Bh:Bh + 64, qt * 128:(qt + 1) * 128],
                                    k_bf[Bh:Bh + 64, nb512(nb)],
                                    start=True, stop=True)
                asc.inc(ins)
                if i >= 1:
                    tr_av(i - 1)
            tr_av(niter - 1)

        @b.vector
        def _(v):
            v = SerialEngine(p, v)
            def ao_evac(j):
                qt, hh = divmod(j, 2)
                Bh = hh * 64
                v.wait_ge(aav.h, B_av + j + 1)
                aao.inc(v.tensor_copy(ao_bf[Bh:Bh + 64, hp, qt * 128:(qt + 1) * 128],
                                      ps_av[0:64, j % 2, 0:128]))

            for i in range(niter):
                qt, hh = divmod(i, 2)
                par = i % 2
                na = red[:, par * 3 + 0:par * 3 + 1]
                sm = red[:, par * 3 + 1:par * 3 + 2]
                rc = red[:, par * 3 + 2:par * 3 + 3]
                v.wait_ge(asc.h, B_sc + i + 1)
                if masked is not None:
                    if hh == 0:
                        v.wait_ge(sy_m[qt % 2].h, B_sy[qt % 2] + 16 * (qt // 2 + 1))
                    edge(p, v, v.tensor_tensor(scm[:, :], ps_flat,
                                               masked["mask_bf"][qt % 2][:, :],
                                               ALU.add))
                    amx.inc(v.reduce_max(na, scm[:, :], axis=AX.X, negate=True))
                else:
                    amx.inc(v.reduce_max(na, ps_flat, axis=AX.X, negate=True))
                if i >= 1:
                    ao_evac(i - 1)
                v.wait_ge(aex.h, B_ex + i + 1)
                arc.inc(v.reciprocal(rc, sm))
                v.wait_ge(arc.h, arc.val)
                if i >= 1:
                    v.wait_ge(atr.h, B_tr + i)
                apb.inc(v.tensor_single_scalar(probs[:, :], exp_f[:, :], rc, ALU.mult))
            ao_evac(niter - 1)

        @b.scalar
        def _(sc):
            sc = SerialEngine(p, sc)
            for i in range(niter):
                par = i % 2
                na = red[:, par * 3 + 0:par * 3 + 1]
                sm = red[:, par * 3 + 1:par * 3 + 2]
                sc.wait_ge(amx.h, B_mx + i + 1)
                src = scm[:, :] if masked is not None else ps_flat
                aex.inc(sc.activation(exp_f[:, :], src, AF.Exp, bias=na, scale=1.0,
                                      accum_out=sm))
                sc.wait_ge(atr.h, B_tr + i + 1)
                aat.inc(sc.activation(attnT.rearrange("p c n -> p (c n)"),
                                      ps_tr.rearrange("p c n -> p (c n)"),
                                      AF.Identity))
        if scm_g is not None:
            scm_g.__exit__(None, None, None)


def _attn_out_residual(p, x_sb, ao_bf, woT_hbm, mu1, rs1, xm_bf=None):
    nc = p.nc
    s_w, s_m, s_e = p.sem("wg"), p.sem("wom"), p.sem("woe")
    with nc.Block() as b, \
         nc.psum_tensor(_nm("ps_wo"), [128, 2, 512], F32) as ps, \
         nc.sbuf_tensor(_nm("wo_wa"), [128, 8, 512], BF16) as wa, \
         nc.sbuf_tensor(_nm("wo_wb"), [128, 8, 512], BF16) as wb, \
         nc.sbuf_tensor(_nm("wo_t"), [128, 512], F32) as wt:
        b_w, b_m, b_e = s_w.val, s_m.val, s_e.val
        slabs = [wa, wb]

        @b.gpsimd
        def _(g):
            g = SerialEngine(p, g)
            for si in range(2):
                g.dma_start(slabs[si][:, :, :],
                            woT_hbm[:, :, si * 512:(si + 1) * 512]).then_inc(s_w.h, 16)
            s_w.val += 32

        @b.tensor
        def _(tt):
            ui = 0
            for si in range(2):
                tt.wait_ge(s_w.h, b_w + 32)
                for cc in range(4):
                    for nb in range(NBK):
                        if ui >= 2:
                            tt.wait_ge(s_e.h, b_e + ui - 1)
                        ins = None
                        for c in range(NCH):
                            ins = tt.matmul(ps[:, ui % 2, :],
                                            slabs[si][:, c, cc * 128:(cc + 1) * 128],
                                            ao_bf[:, c, nb512(nb)],
                                            start=(c == 0), stop=(c == NCH - 1))
                        s_m.inc(ins)
                        ui += 1

        @b.vector
        def _(v):
            v = SerialEngine(p, v)
            ui = 0
            for si in range(2):
                for cc in range(4):
                    fo = si * 4 + cc
                    for nb in range(NBK):
                        v.wait_ge(s_m.h, b_m + ui + 1)
                        xs = x_sb[:, fo, nb512(nb)]
                        if mu1 is not None:
                            edge(p, v, v.tensor_tensor(wt[:, :], xs,
                                                       mu1[:, nb512(nb)],
                                                       ALU.subtract))
                            edge(p, v, v.tensor_tensor(wt[:, :], wt[:, :],
                                                       rs1[:, nb512(nb)], ALU.mult))
                            edge(p, v, v.tensor_tensor(wt[:, :], wt[:, :],
                                                       ps[:, ui % 2, :], ALU.add))
                            if xm_bf is not None:
                                edge(p, v, v.tensor_copy(xm_bf[:, fo, nb512(nb)],
                                                         wt[:, :]))
                            s_e.inc(v.tensor_tensor(xs, xs, wt[:, :], ALU.add))
                        else:
                            s_e.inc(v.tensor_tensor(xs, xs, ps[:, ui % 2, :],
                                                    ALU.add))
                        ui += 1



def emit_gating(p, xn_bf, wgT_sb, bias_rep, nexp, wde_sb, wdtm_sb=None):
    nc, t = p.nc, p.t
    ggm, ggt = p.sem("ggm"), p.sem("ggt")
    glg, gg1, gg2, gg3, gg4 = (p.sem(x) for x in ("glg", "gg1", "gg2", "gg3", "gg4"))
    gwd, gge, gex = p.sem("gwd"), p.sem("gge"), p.sem("gex")

    with nc.Block() as b, \
         nc.psum_tensor(_nm("ps_g"), [128, 2, 512], F32) as ps, \
         nc.psum_tensor(_nm("ps_gt"), [16, 2, 512], F32) as pst, \
         nc.sbuf_tensor(_nm("g_lg"), [128, 16], F32) as lg, \
         nc.sbuf_tensor(_nm("g_l2"), [128, 16], F32) as l2, \
         nc.sbuf_tensor(_nm("g_eq"), [128, 32], F32) as eq, \
         nc.sbuf_tensor(_nm("g_red"), [128, 8], F32) as red, \
         nc.sbuf_tensor(_nm("g_wd"), [128, 16], F32) as wd:
        B_gm, B_gt, B_lg = ggm.val, ggt.val, glg.val
        B_ex, B_wd, B_ge = gex.val, gwd.val, gge.val
        B_g2 = gg2.val

        @b.tensor
        def _(tt):
            for i in range(16):
                if i >= 2:
                    tt.wait_ge(glg.h, B_lg + i - 1)
                ins = None
                for c in range(NCH):
                    ins = tt.matmul(ps[:, i % 2, 0:nexp],
                                    xn_bf[:, c, i * 128:(i + 1) * 128],
                                    wgT_sb[:, c, 0:nexp],
                                    start=(c == 0), stop=(c == NCH - 1))
                ggm.inc(ins)
                tt.wait_ge(gwd.h, B_wd + i + 1)
                if i >= 2:
                    tt.wait_ge(gge.h, B_ge + i - 1)
                ggt.inc(tt.transpose(pst[0:nexp, i % 2, 0:128], wd[:, 0:nexp],
                                     t["identf"][:, :]))

        @b.vector
        def _(v):
            v = SerialEngine(p, v)
            for i in range(16):
                m1, nm1 = red[:, 0:1], red[:, 1:2]
                m2, bex = red[:, 2:3], red[:, 3:4]
                den, rden, br = red[:, 4:5], red[:, 5:6], red[:, 6:7]
                v.wait_ge(ggm.h, B_gm + i + 1)
                if bias_rep is not None:
                    glg.inc(v.tensor_tensor(lg[:, 0:nexp], ps[:, i % 2, 0:nexp],
                                            bias_rep[:, 0:nexp], ALU.add))
                else:
                    glg.inc(v.tensor_copy(lg[:, 0:nexp], ps[:, i % 2, 0:nexp]))
                edge(p, v, v.tensor_reduce(m1, lg[:, 0:nexp], axis=AX.X,
                                           op=ALU.max))
                gg1.inc(v.tensor_scalar_mul(nm1, m1, -1.0))
                v.wait_ge(gg1.h, gg1.val)
                edge(p, v, v.tensor_single_scalar(eq[:, 0:nexp], lg[:, 0:nexp], m1,
                                                  ALU.is_ge))
                edge(p, v, v.scalar_tensor_tensor(l2[:, 0:nexp], eq[:, 0:nexp],
                                                  -1e30, lg[:, 0:nexp],
                                                  ALU.mult, ALU.add))
                gg2.inc(v.tensor_reduce(m2, l2[:, 0:nexp], axis=AX.X, op=ALU.max))
                v.wait_ge(gex.h, B_ex + i + 1)
                gg3.inc(v.tensor_scalar_add(den, bex, 1.0))
                v.wait_ge(gg3.h, gg3.val)
                gg3.inc(v.reciprocal(rden, den))
                v.wait_ge(gg3.h, gg3.val)
                gg4.inc(v.tensor_tensor(br, bex, rden, ALU.mult))
                v.wait_ge(gg4.h, gg4.val)
                edge(p, v, v.tensor_single_scalar(eq[:, 16:16 + nexp], l2[:, 0:nexp],
                                                  m2, ALU.is_ge))
                edge(p, v, v.tensor_single_scalar(eq[:, 16:16 + nexp],
                                                  eq[:, 16:16 + nexp], br, ALU.mult))
                edge(p, v, v.tensor_single_scalar(wd[:, 0:nexp], eq[:, 0:nexp],
                                                  rden, ALU.mult))
                gwd.inc(v.tensor_tensor(wd[:, 0:nexp], wd[:, 0:nexp],
                                        eq[:, 16:16 + nexp], ALU.add))
                if wdtm_sb is not None:
                    v.tensor_copy(wdtm_sb[:, i, 0:nexp], wd[:, 0:nexp])
                v.wait_ge(ggt.h, B_gt + i + 1)
                gge.inc(v.tensor_copy(wde_sb[0:nexp, i, :], pst[0:nexp, i % 2, 0:128]))

        @b.scalar
        def _(sc):
            sc = SerialEngine(p, sc)
            for i in range(16):
                nm1, m2, bex = red[:, 1:2], red[:, 2:3], red[:, 3:4]
                sc.wait_ge(gg2.h, B_g2 + i + 1)
                gex.inc(sc.activation(bex, m2, AF.Exp, bias=nm1, scale=1.0))


def emit_moe(p, x_sb, xn_bf, kind):
    nc, t = p.nc, p.t
    nexp = 8 if kind == "ffn" else 4
    sel = t["sel8"] if kind == "ffn" else t["sel4"]

    with nc.sbuf_tensor(_nm("m_wde"), [8, 16, 128], F32) as wde, \
         nc.sbuf_tensor(_nm("m_wrep"), [128, S], F32) as wrep:

        if kind == "ffn":
            emit_gating(p, xn_bf, t["fwgT"], t["fbg"], 8, wde)
        else:
            emit_gating(p, xn_bf, t["bgw"], None, 4, wde)

        wde_flat = wde.rearrange("e q n -> e (q n)")
        for e in range(nexp):
            s_rw = p.sem("mrw")
            with nc.Block() as b, nc.psum_tensor(_nm("ps_wr"), [128, S], F32) as psr:
                b_rw = s_rw.val

                @b.tensor
                def _(tt):
                    ins = None
                    for nb in range(NBK):
                        ins = tt.matmul(psr[:, nb512(nb)],
                                        sel[0:nexp, e * 128:(e + 1) * 128],
                                        wde_flat[0:nexp, nb512(nb)],
                                        start=True, stop=True)
                    s_rw.inc(ins)

                @b.vector
                def _(v):
                    v = SerialEngine(p, v)
                    v.wait_ge(s_rw.h, b_rw + 1)
                    v.tensor_copy(wrep[:, :], psr[:, :])

            _moe_expert(p, x_sb, xn_bf, wrep, e, kind)


def _moe_expert(p, x_sb, xn_bf, wrep, e, kind):
    nc, t, h = p.nc, p.t, p.h
    s_w = p.sem("wg")
    s_g = [p.sem("wga0"), p.sem("wga1")]
    s_u = [p.sem("wgu0"), p.sem("wgu1")]
    s_d = [p.sem("wgd0"), p.sem("wgd1")]
    fgm, fum, fsl, fh = p.sem("fgm"), p.sem("fum"), p.sem("fsl"), p.sem("fh")
    fdm, fde, fds = p.sem("fdm"), p.sem("fde"), p.sem("fds")

    is_ffn = kind == "ffn"
    guards = [nc.sbuf_tensor(_nm("m_h"), [128, 32, 512], BF16),
              nc.sbuf_tensor(_nm("m_wga"), [128, 8, 512], BF16),
              nc.sbuf_tensor(_nm("m_wgb"), [128, 8, 512], BF16),
              nc.sbuf_tensor(_nm("m_wd0"), [128, 4, 1024], BF16),
              nc.sbuf_tensor(_nm("m_wd1"), [128, 4, 1024], BF16),
              nc.sbuf_tensor(_nm("m_ds"), [128, 512], F32)]
    if is_ffn:
        guards += [nc.sbuf_tensor(_nm("m_wua"), [128, 8, 512], BF16),
                   nc.sbuf_tensor(_nm("m_wub"), [128, 8, 512], BF16),
                   nc.sbuf_tensor(_nm("m_sg"), [128, 4, 512], BF16)]
    objs = [g_.__enter__() for g_ in guards]
    h_bf, wga, wgb, wd0, wd1, dscr = objs[:6]
    if is_ffn:
        wua, wub, sg = objs[6:]
    else:
        wua = wub = sg = None
    if True:
        nperhb = 8 if is_ffn else 4

        for tb in range(4):
            tok = slice(tb * 512, (tb + 1) * 512)

            with nc.Block() as b, \
                 nc.psum_tensor(_nm("ps_mg"), [128, 4, 512], F32) as psg, \
                 nc.psum_tensor(_nm("ps_mu"), [128, 4, 512], F32) as psu:
                b_gm, b_um = fgm.val, fum.val
                b_sl, b_h = fsl.val, fh.val
                b_g = [s_g[0].val, s_g[1].val]
                b_u = [s_u[0].val, s_u[1].val]

                @b.gpsimd
                def _(g):
                    g = SerialEngine(p, g)
                    for hb in range(8):
                        ga = wga if hb % 2 == 0 else wgb
                        ua = wua if hb % 2 == 0 else wub
                        if hb >= 2:
                            g.wait_ge(fgm.h, b_gm + (hb - 1) * 4)
                            if is_ffn:
                                g.wait_ge(fum.h, b_um + (hb - 1) * 4)
                        src = h["f_WgT"] if is_ffn else h["b_w1"]
                        sem = s_g[hb % 2]
                        g.dma_start(ga[:, :, :],
                                    src[e][:, :, hb * 512:(hb + 1) * 512]
                                    ).then_inc(sem.h, 16)
                        sem.val += 16
                        if is_ffn:
                            sem = s_u[hb % 2]
                            g.dma_start(ua[:, :, :],
                                        h["f_WuT"][e][:, :, hb * 512:(hb + 1) * 512]
                                        ).then_inc(sem.h, 16)
                            sem.val += 16

                nload = 2 if is_ffn else 1

                @b.tensor
                def _(tt):
                    for hb in range(8):
                        tt.wait_ge(s_g[hb % 2].h, b_g[hb % 2] + 16 * (hb // 2 + 1))
                        if is_ffn:
                            tt.wait_ge(s_u[hb % 2].h, b_u[hb % 2] + 16 * (hb // 2 + 1))
                        ga = wga if hb % 2 == 0 else wgb
                        ua = wua if hb % 2 == 0 else wub
                        for mc in range(4):
                            if hb >= 1:
                                tt.wait_ge(fsl.h, b_sl + (hb - 1) * 4 + mc + 1)
                            ins = None
                            for c in range(NCH):
                                ins = tt.matmul(psg[:, mc, :],
                                                ga[:, c, mc * 128:(mc + 1) * 128],
                                                xn_bf[:, c, tok],
                                                start=(c == 0), stop=(c == NCH - 1))
                            fgm.inc(ins)
                        if is_ffn:
                            for mc in range(4):
                                if hb >= 1:
                                    tt.wait_ge(fh.h, b_h + (hb - 1) * 4 + mc + 1)
                                ins = None
                                for c in range(NCH):
                                    ins = tt.matmul(psu[:, mc, :],
                                                    ua[:, c, mc * 128:(mc + 1) * 128],
                                                    xn_bf[:, c, tok],
                                                    start=(c == 0), stop=(c == NCH - 1))
                                fum.inc(ins)

                @b.scalar
                def _(sc):
                    sc = SerialEngine(p, sc)
                    for hb in range(8):
                        for mc in range(4):
                            sc.wait_ge(fgm.h, b_gm + hb * 4 + mc + 1)
                            if is_ffn:
                                if hb >= 1:
                                    sc.wait_ge(fh.h, b_h + (hb - 1) * 4 + mc + 1)
                                fsl.inc(sc.activation(sg[:, mc, :], psg[:, mc, :],
                                                      AF.Silu))
                            else:
                                fsl.inc(sc.activation(
                                    h_bf[:, hb * 4 + mc, :], psg[:, mc, :],
                                    AF.Gelu_apprx_tanh,
                                    bias=t["bb1"][:, e, hb * 4 + mc:hb * 4 + mc + 1],
                                    scale=1.0))

                if is_ffn:
                    @b.vector
                    def _(v):
                        v = SerialEngine(p, v)
                        for hb in range(8):
                            for mc in range(4):
                                v.wait_ge(fsl.h, b_sl + hb * 4 + mc + 1)
                                v.wait_ge(fum.h, b_um + hb * 4 + mc + 1)
                                fh.inc(v.tensor_tensor(h_bf[:, hb * 4 + mc, :],
                                                       sg[:, mc, :], psu[:, mc, :],
                                                       ALU.mult))

            with nc.Block() as b, nc.psum_tensor(_nm("ps_md"), [128, 8, 512], F32) as psd:
                b_dm, b_ds = fdm.val, fds.val
                b_d = [s_d[0].val, s_d[1].val]
                wds = [wd0, wd1]
                hkey = "f_WdT" if is_ffn else "b_w2"

                @b.gpsimd
                def _(g):
                    g = SerialEngine(p, g)
                    for sl in range(8):
                        if sl >= 2:
                            g.wait_ge(fds.h, b_ds + sl - 1)
                        sem = s_d[sl % 2]
                        g.dma_start(wds[sl % 2][:, :, :],
                                    h[hkey][e][:, sl * 4:(sl + 1) * 4, :]
                                    ).then_inc(sem.h, 16)
                        sem.val += 16

                @b.tensor
                def _(tt):
                    if _T3_MODE == "g":
                        return
                    for sl in range(8):
                        tt.wait_ge(s_d[sl % 2].h, b_d[sl % 2] + 16 * (sl // 2 + 1))
                        ins = None
                        for fo in range(8):
                            for c in range(4):
                                ins = tt.matmul(psd[:, fo, :],
                                                wds[sl % 2][:, c, fo * 128:(fo + 1) * 128],
                                                h_bf[:, sl * 4 + c, :],
                                                start=(sl == 0 and c == 0),
                                                stop=(sl == 7 and c == 3))
                            if sl == 7:
                                fdm.inc(ins)
                        if sl < 7:
                            fds.inc(ins)

                @b.vector
                def _(v):
                    v = SerialEngine(p, v)
                    for fo in range(8):
                        v.wait_ge(fdm.h, b_dm + fo + 1)
                        xs = x_sb[:, fo, tok]
                        if is_ffn:
                            edge(p, v, v.tensor_tensor(dscr[:, :], psd[:, fo, :],
                                                       wrep[:, tok], ALU.mult))
                        else:
                            edge(p, v, v.scalar_tensor_tensor(
                                dscr[:, :], psd[:, fo, :],
                                t["bb2"][:, e, fo:fo + 1],
                                wrep[:, tok], ALU.add, ALU.mult))
                        fde.inc(v.tensor_tensor(xs, xs, dscr[:, :], ALU.add))
    for g_ in reversed(guards):
        g_.__exit__(None, None, None)



def emit_attn_b(p, x_sb):
    nc, t, h = p.nc, p.t, p.h

    with nc.sbuf_tensor(_nm("b_xn"), [128, NCH, S], BF16) as xn_bf:

        with nc.sbuf_tensor(_nm("b_mu1"), [128, S], F32) as mu1, \
             nc.sbuf_tensor(_nm("b_rs1"), [128, S], F32) as rs1:
            with nc.sbuf_tensor(_nm("b_rs2"), [128, S], F32) as rs2:
                emit_stats(p, x_sb, mu1, rs1, LN_EPS, "x")
                emit_stats(p, x_sb, None, rs2, RMS_EPS, "lnrms",
                           mu1rep=mu1, rs1rep=rs1)
                emit_ln_apply(p, x_sb, mu1, rs1, xn_bf, rs2rep=rs2, gamma=t["bn1"])

            _attn_b_masks(p, xn_bf)

            with nc.sbuf_tensor(_nm("b_ao"), [128, NCH, S], BF16) as ao_bf, \
                 nc.sbuf_tensor(_nm("b_mk0"), [128, S], BF16) as mk0, \
                 nc.sbuf_tensor(_nm("b_mk1"), [128, S], BF16) as mk1:
                s_wy = [p.sem("wy0"), p.sem("wy1")]

                def load_mask(sy, qt, buf):
                    dst = mk0 if buf == 0 else mk1
                    sem = s_wy[buf]
                    sy.dma_start(dst[:, :], p.h["mask_scr"][qt]).then_inc(sem.h, 16)
                    sem.val += 16

                masked = {"mask_bf": [mk0, mk1], "load": load_mask}

                for hp in range(8):
                    _attn_b_hp(p, hp, xn_bf, ao_bf, masked)

                _attn_out_residual(p, x_sb, ao_bf, h["b_wo"], mu1, rs1, xm_bf=xn_bf)

        with nc.sbuf_tensor(_nm("b_rs3"), [128, S], F32) as rs3:
            emit_stats(p, xn_bf, None, rs3, RMS_EPS, "rms")
            emit_ln_apply(p, xn_bf, None, rs3, xn_bf, gamma=t["bn2"])
        emit_moe(p, x_sb, xn_bf, "moe4")


def _attn_b_hp(p, hp, xn_bf, ao_bf, masked):
    nc, t, h = p.nc, p.t, p.h
    s_w, s_m, s_e = p.sem("wg"), p.sem("qkm"), p.sem("qke")

    with nc.sbuf_tensor(_nm("b_qb"), [128, S], BF16) as q_bf, \
         nc.sbuf_tensor(_nm("b_kb"), [128, S], BF16) as k_bf, \
         nc.sbuf_tensor(_nm("b_vb"), [128, S], BF16) as v_bf, \
         nc.sbuf_tensor(_nm("b_vT"), [128, 16, 128], BF16) as vT_bf:

        with nc.Block() as b, \
             nc.psum_tensor(_nm("ps_qkv"), [128, 2, 512], F32) as ps, \
             nc.sbuf_tensor(_nm("b_wqkv"), [128, 8, 384], BF16) as wqkv:
            b_w, b_m, b_e = s_w.val, s_m.val, s_e.val

            @b.gpsimd
            def _(g):
                g = SerialEngine(p, g)
                g.dma_start(wqkv[:, :, 0:128],
                            h["b_wq"][:, :, hp * 128:(hp + 1) * 128]).then_inc(s_w.h, 16)
                g.dma_start(wqkv[:, :, 128:256],
                            h["b_wk"][:, :, hp * 128:(hp + 1) * 128]).then_inc(s_w.h, 16)
                g.dma_start(wqkv[:, :, 256:384],
                            h["b_wv"][:, :, hp * 128:(hp + 1) * 128]).then_inc(s_w.h, 16)
                s_w.val += 48

            @b.tensor
            def _(tt):
                tt.wait_ge(s_w.h, b_w + 48)
                ui = 0
                for which in range(3):
                    for nb in range(NBK):
                        if ui >= 2:
                            tt.wait_ge(s_e.h, b_e + ui - 1)
                        ins = None
                        for c in range(NCH):
                            ins = tt.matmul(ps[:, ui % 2, :],
                                            wqkv[:, c, which * 128:(which + 1) * 128],
                                            xn_bf[:, c, nb512(nb)],
                                            start=(c == 0), stop=(c == NCH - 1))
                        s_m.inc(ins)
                        ui += 1

            @b.scalar
            def _(sc):
                sc = SerialEngine(p, sc)
                ui = 0
                for which, dst in ((0, q_bf), (1, k_bf), (2, v_bf)):
                    for nb in range(NBK):
                        sc.wait_ge(s_m.h, b_m + ui + 1)
                        s_e.inc(sc.activation(dst[:, nb512(nb)], ps[:, ui % 2, :],
                                              AF.Identity,
                                              scale=0.125 if which == 0 else 1.0))
                        ui += 1

        s_vm, s_ve = p.sem("vtm"), p.sem("vte")
        with nc.Block() as b, nc.psum_tensor(_nm("ps_vT"), [128, 4, 1024], BF16) as psv:
            b_vm, b_ve = s_vm.val, s_ve.val

            @b.tensor
            def _(tt):
                k = 0
                for sc_ in range(16):
                    for hh in range(2):
                        if k >= 4:
                            tt.wait_ge(s_ve.h, b_ve + k - 3)
                        s_vm.inc(tt.transpose(psv[:, k % 4, 0:64],
                                              v_bf[hh * 64:(hh + 1) * 64,
                                                   sc_ * 128:(sc_ + 1) * 128],
                                              t["id64"][hh * 64:(hh + 1) * 64, :]))
                        k += 1

            @b.scalar
            def _(sc):
                sc = SerialEngine(p, sc)
                k = 0
                for sc_ in range(16):
                    for hh in range(2):
                        sc.wait_ge(s_vm.h, b_vm + k + 1)
                        s_ve.inc(sc.activation(vT_bf[:, sc_, hh * 64:(hh + 1) * 64],
                                               psv[:, k % 4, 0:64], AF.Identity))
                        k += 1

        _attn_inner(p, hp, q_bf, k_bf, vT_bf, ao_bf, masked=masked)


def _attn_b_masks(p, xn_bf):
    nc, t, h = p.nc, p.t, p.h
    s_bm, s_br, s_bs = p.sem("bim"), p.sem("bir"), p.sem("bis")
    s_ba, s_bg, s_bn = p.sem("bia"), p.sem("big"), p.sem("bin")
    s_tr, s_ct, s_lo = p.sem("btr"), p.sem("bct"), p.sem("blo")
    s_mk, s_so = p.sem("bmk"), p.sem("so")
    s_w, s_pm, s_pe = p.sem("wg"), p.sem("bpm"), p.sem("bpe")

    with nc.sbuf_tensor(_nm("b_iq"), [64, 4, S], BF16) as iq_bf, \
         nc.sbuf_tensor(_nm("b_ik"), [64, S], BF16) as ik_bf, \
         nc.sbuf_tensor(_nm("b_iwt"), [128, 16, 4], F32) as iw_tm, \
         nc.sbuf_tensor(_nm("b_iota"), [128, S], F32) as iotaEta:
        with nc.Block() as b0:
            @b0.gpsimd
            def _(g):
                g = SerialEngine(p, g)
                gp = p.sem("gp")
                gp.inc(g.iota(iotaEta[:, :], [[1, S]], base=1, channel_multiplier=0,
                              allow_small_or_imprecise_dtypes=True))
                g.wait_ge(gp.h, gp.val)
                g.tensor_scalar_mul(iotaEta[:, :], iotaEta[:, :], float(ETA))

        with nc.Block() as b, \
             nc.psum_tensor(_nm("ps_iq"), [64, 2, 512], F32) as psq, \
             nc.psum_tensor(_nm("ps_iw"), [128, 2, 512], F32) as psw, \
             nc.sbuf_tensor(_nm("b_iwq"), [128, 8, 256], BF16) as wiq, \
             nc.sbuf_tensor(_nm("b_iwk"), [128, 8, 64], BF16) as wik:
            b_w, b_pm, b_pe = s_w.val, s_pm.val, s_pe.val

            @b.gpsimd
            def _(g):
                g = SerialEngine(p, g)
                g.dma_start(wiq[:, :, :], h["b_iwq"]).then_inc(s_w.h, 16)
                g.dma_start(wik[:, :, :], h["b_iwk"]).then_inc(s_w.h, 16)
                s_w.val += 32

            @b.tensor
            def _(tt):
                tt.wait_ge(s_w.h, b_w + 32)
                ui = 0
                for hi in range(4):
                    for nb in range(NBK):
                        if ui >= 2:
                            tt.wait_ge(s_pe.h, b_pe + ui - 1)
                        ins = None
                        for c in range(NCH):
                            ins = tt.matmul(psq[:, ui % 2, :],
                                            wiq[:, c, hi * 64:(hi + 1) * 64],
                                            xn_bf[:, c, nb512(nb)],
                                            start=(c == 0), stop=(c == NCH - 1))
                        s_pm.inc(ins)
                        ui += 1
                for nb in range(NBK):
                    if ui >= 2:
                        tt.wait_ge(s_pe.h, b_pe + ui - 1)
                    ins = None
                    for c in range(NCH):
                        ins = tt.matmul(psq[:, ui % 2, :], wik[:, c, :],
                                        xn_bf[:, c, nb512(nb)],
                                        start=(c == 0), stop=(c == NCH - 1))
                    s_pm.inc(ins)
                    ui += 1
                for i in range(16):
                    if i >= 2:
                        tt.wait_ge(s_pe.h, b_pe + ui - 1)
                    ins = None
                    for c in range(NCH):
                        ins = tt.matmul(psw[:, i % 2, 0:4],
                                        xn_bf[:, c, i * 128:(i + 1) * 128],
                                        t["biww"][:, c, :],
                                        start=(c == 0), stop=(c == NCH - 1))
                    s_pm.inc(ins)
                    ui += 1

            @b.scalar
            def _(sc):
                sc = SerialEngine(p, sc)
                ui = 0
                for hi in range(4):
                    for nb in range(NBK):
                        sc.wait_ge(s_pm.h, b_pm + ui + 1)
                        s_pe.inc(sc.activation(iq_bf[:, hi, nb512(nb)],
                                               psq[:, ui % 2, :], AF.Identity))
                        ui += 1
                for nb in range(NBK):
                    sc.wait_ge(s_pm.h, b_pm + ui + 1)
                    s_pe.inc(sc.activation(ik_bf[:, nb512(nb)], psq[:, ui % 2, :],
                                           AF.Identity))
                    ui += 1
                for i in range(16):
                    sc.wait_ge(s_pm.h, b_pm + ui + 1)
                    s_pe.inc(sc.activation(iw_tm[:, i, :], psw[:, i % 2, 0:4],
                                           AF.Identity))
                    ui += 1

        with nc.Block() as b, nc.sbuf_tensor(_nm("b_mtmp"), [128, S], BF16) as mtmp:
            @b.gpsimd
            def _(g):
                g = SerialEngine(p, g)
                gp = p.sem("gp")
                gso = p.sem("gso")
                for qt in range(4):
                    gp.inc(g.memset(mtmp[:, :], 0.0))
                    g.wait_ge(gp.h, gp.val)
                    gp.inc(g.affine_select(mtmp[:, :], mtmp[:, :], [[-1, S]],
                                           ALU.is_ge, t["r_neg"], base=qt * 128,
                                           channel_multiplier=1))
                    g.wait_ge(gp.h, gp.val)
                    g.dma_start(h["mask_scr"][qt], mtmp[:, :]).then_inc(gso.h, 16)
                    gso.val += 16
                    g.wait_ge(gso.h, gso.val)

        for qp in range(6):
            qts = (4 + 2 * qp, 5 + 2 * qp)
            _bisect_pair(p, qts, iq_bf, ik_bf, iw_tm, iotaEta)


def _bisect_pair(p, qts, iq_bf, ik_bf, iw_tm, iotaEta):
    nc, t, h = p.nc, p.t, p.h
    s_bm, s_br, s_bs = p.sem("bim"), p.sem("bir"), p.sem("bis")
    s_ba, s_bg, s_bn = p.sem("bia"), p.sem("big"), p.sem("bin")
    s_tr, s_ct, s_lo = p.sem("btr"), p.sem("bct"), p.sem("blo")
    s_mk, s_so = p.sem("bmk"), p.sem("so")

    with nc.Block() as b, \
         nc.psum_tensor(_nm("ps_il"), [128, 4, 512], F32) as psl, \
         nc.sbuf_tensor(_nm("bi_r"), [128, S], F32) as relu_s, \
         nc.sbuf_tensor(_nm("bi_s0"), [128, S], F32) as isc0, \
         nc.sbuf_tensor(_nm("bi_s1"), [128, S], F32) as isc1, \
         nc.sbuf_tensor(_nm("bi_sg"), [128, S], F32) as sgn_s, \
         nc.sbuf_tensor(_nm("bi_mk0"), [128, S], BF16) as mk_bf0, \
         nc.sbuf_tensor(_nm("bi_mk1"), [128, S], BF16) as mk_bf1, \
         nc.sbuf_tensor(_nm("bi_st"), [128, 12], F32) as bst:
        B_bm, B_br, B_bs = s_bm.val, s_br.val, s_bs.val
        B_ba, B_bg, B_bn = s_ba.val, s_bg.val, s_bn.val
        B_tr, B_ct, B_lo = s_tr.val, s_ct.val, s_lo.val
        B_mk = s_mk.val
        iscs = [isc0, isc1]
        mks = [mk_bf0, mk_bf1]

        @b.tensor
        def _(tt):
            for j, qt in enumerate(qts):
                for hi in range(4):
                    k = j * 4 + hi
                    if k >= 1:
                        tt.wait_ge(s_br.h, B_br + k)
                    ins = None
                    for nb in range(NBK):
                        ins = tt.matmul(psl[:, nb, :],
                                        iq_bf[:, hi, qt * 128:(qt + 1) * 128],
                                        ik_bf[:, nb512(nb)], start=True, stop=True)
                    s_bm.inc(ins)

        @b.scalar
        def _(sc):
            sc = SerialEngine(p, sc)
            psl_flat = psl.rearrange("p b n -> p (b n)")
            for j, qt in enumerate(qts):
                for hi in range(4):
                    k = j * 4 + hi
                    sc.wait_ge(s_bm.h, B_bm + k + 1)
                    if k >= 1:
                        sc.wait_ge(s_bs.h, B_bs + k)
                    s_br.inc(sc.activation(relu_s[:, :], psl_flat, AF.Relu))
            for it in range(BISECT_ITERS):
                for j in range(2):
                    sc.wait_ge(s_tr.h, B_tr + it * 2 + j + 1)
                    s_ct.inc(sc.activation(sgn_s[:, :], iscs[j][:, :], AF.Sign,
                                           bias=bst[:, 6 + j:7 + j], scale=1.0,
                                           accum_out=bst[:, 2 + j:3 + j]))

        @b.gpsimd
        def _(g):
            g = SerialEngine(p, g)
            for j, qt in enumerate(qts):
                g.wait_ge(s_ba.h, B_ba + j + 1)
                g.affine_select(iscs[j][:, :], iscs[j][:, :], [[-1, S]], ALU.is_ge,
                                t["r_neg"], base=qt * 128, channel_multiplier=1)
                s_bg.inc(g.engine_nop())

        @b.vector
        def _(v):
            v = SerialEngine(p, v)
            for j, qt in enumerate(qts):
                for hi in range(4):
                    k = j * 4 + hi
                    v.wait_ge(s_br.h, B_br + k + 1)
                    wcol = iw_tm[:, qt, hi:hi + 1]
                    if hi == 0:
                        ins = v.tensor_single_scalar(iscs[j][:, :], relu_s[:, :],
                                                     wcol, ALU.mult)
                    else:
                        ins = v.scalar_tensor_tensor(iscs[j][:, :], relu_s[:, :],
                                                     wcol, iscs[j][:, :],
                                                     ALU.mult, ALU.add)
                    s_bs.inc(ins)
                s_ba.inc(v.reduce_max(bst[:, 0 + j:1 + j], iscs[j][:, :], axis=AX.X,
                                      apply_absolute_value=True))
                v.wait_ge(s_bg.h, B_bg + j + 1)
                s_bn.inc(v.tensor_scalar_add(bst[:, 4 + j:5 + j],
                                             bst[:, 0 + j:1 + j], 1.0))
                v.wait_ge(s_bn.h, s_bn.val)
                s_bn.inc(v.reciprocal(bst[:, 4 + j:5 + j], bst[:, 4 + j:5 + j]))
                v.wait_ge(s_bn.h, s_bn.val)
                edge(p, v, v.tensor_single_scalar(iscs[j][:, :], iscs[j][:, :],
                                                  bst[:, 4 + j:5 + j], ALU.mult))
                edge(p, v, v.tensor_single_scalar(sgn_s[:, :], iscs[j][:, :], 0.0,
                                                  ALU.is_equal))
                edge(p, v, v.tensor_tensor(sgn_s[:, :], sgn_s[:, :], iotaEta[:, :],
                                           ALU.mult))
                v.tensor_tensor(iscs[j][:, :], iscs[j][:, :], sgn_s[:, :],
                                ALU.subtract)
                v.memset(bst[:, 8 + j:9 + j], 1.1)
            for it in range(BISECT_ITERS):
                step = 2.2 * (2.0 ** -(it + 1))
                for j in range(2):
                    s_tr.inc(v.tensor_scalar_add(bst[:, 6 + j:7 + j],
                                                 bst[:, 8 + j:9 + j], -step))
                for j in range(2):
                    v.wait_ge(s_ct.h, B_ct + it * 2 + j + 1)
                    s_lo.inc(v.tensor_single_scalar(bst[:, 10 + j:11 + j],
                                                    bst[:, 2 + j:3 + j], -1024.0,
                                                    ALU.is_ge))
                    v.wait_ge(s_lo.h, s_lo.val)
                    s_lo.inc(v.scalar_tensor_tensor(bst[:, 8 + j:9 + j],
                                                    bst[:, 10 + j:11 + j], -step,
                                                    bst[:, 8 + j:9 + j],
                                                    ALU.mult, ALU.add))
            for j, qt in enumerate(qts):
                s_mk.inc(v.tensor_scalar_mul(bst[:, 4 + j:5 + j],
                                             bst[:, 8 + j:9 + j], -1.0))
                v.wait_ge(s_mk.h, s_mk.val)
                edge(p, v, v.tensor_single_scalar(sgn_s[:, :], iscs[j][:, :],
                                                  bst[:, 4 + j:5 + j], ALU.is_lt))
                s_mk.inc(v.tensor_single_scalar(mks[j][:, :], sgn_s[:, :], NEG,
                                                ALU.mult))

        @b.sync
        def _(sy):
            mso = p.sem("mso")
            for j, qt in enumerate(qts):
                sy.wait_ge(s_mk.h, B_mk + (j + 1) * 2)
                sy.dma_start(h["mask_scr"][qt], mks[j][:, :]).then_inc(mso.h, 16)
                mso.val += 16
            sy.wait_ge(mso.h, mso.val)



def build_program(nseq=NSEQ, parts=_PARTS):
    p = P(nseq, parts)
    nc = p.nc
    p.h["xin"] = nc.dram_tensor("xin", [nseq, D, S], F32, kind="ExternalInput").ap()
    p.h["xout"] = nc.dram_tensor("xout", [nseq, D, S], F32, kind="ExternalOutput").ap()
    p.h["mask_scr"] = nc.dram_tensor("mask_scr", [16, 128, S], BF16,
                                     kind="Internal").ap()
    p.h["wg_bf"] = nc.dram_tensor("wg_bf", [64, 128, 8, 512], BF16,
                                  kind="Internal").ap()
    p.h["wu_bf"] = nc.dram_tensor("wu_bf", [64, 128, 8, 512], BF16,
                                  kind="Internal").ap()
    p.h["wd_bf"] = nc.dram_tensor("wd_bf", [64, 128, 4, 1024], BF16,
                                  kind="Internal").ap()
    p.h["xg_dram"] = nc.dram_tensor("xg_dram", [16 * 512, 1024], BF16,
                                    kind="Internal").ap()
    p.h["os_dram"] = nc.dram_tensor("os_dram", [16 * 512, 1024], BF16,
                                    kind="Internal").ap()

    shapes = {
        "a_wdqT": [128, 8, 512], "a_wdkvT": [128, 8, 512],
        "a_wupqT": [128, 4, 1024], "a_wupkT": [128, 4, 1024],
        "a_wupvT": [128, 4, 1024], "a_woT": [128, 8, 1024],
        "a_lns": [128, 8], "a_lnb": [128, 8],
        "r_cos": [128, S], "r_sin": [128, S],
        "f_wgT": [128, 8, 8], "f_bg": [128, 8],
        "f_WgS": [64, 128, 8, 512], "f_WuS": [64, 128, 8, 512],
        "f_WdS": [64, 128, 4, 1024], "iotaT": [8, 16], "piota": [128, 1],
        "b_wq": [128, 8, 1024], "b_wk": [128, 8, 1024], "b_wv": [128, 8, 1024],
        "b_wo": [128, 8, 1024],
        "b_iwq": [128, 8, 256], "b_iwk": [128, 8, 64], "b_iww": [128, 8, 4],
        "b_gw": [128, 8, 4], "b_n1": [128, 8], "b_n2": [128, 8],
        "b_w1": [4, 128, 8, 4096], "b_w2": [4, 128, 32, 1024],
        "b_b1": [4, 128, 32], "b_b2": [4, 128, 8],
    }
    for name, shape in shapes.items():
        p.h[name] = nc.dram_tensor(name, shape, F32, kind="ExternalInput").ap()

    t = p.t
    t["ident"] = nc.sbuf_tensor(_nm("c_ident"), [128, 128], BF16).__enter__()
    t["id64"] = nc.sbuf_tensor(_nm("c_id64"), [128, 64], BF16).__enter__()
    t["identf"] = nc.sbuf_tensor(_nm("c_idf"), [128, 128], F32).__enter__()
    t["ones_c"] = nc.sbuf_tensor(_nm("c_onc"), [128, 1], F32).__enter__()
    t["ones_r"] = nc.sbuf_tensor(_nm("c_onr"), [65, 128], F32).__enter__()
    t["sel8"] = nc.sbuf_tensor(_nm("c_sel8"), [8, 8 * 128], F32).__enter__()
    t["sel4"] = nc.sbuf_tensor(_nm("c_sel4"), [4, 4 * 128], F32).__enter__()
    t["lns"] = nc.sbuf_tensor(_nm("c_lns"), [128, 8], F32).__enter__()
    t["lnb"] = nc.sbuf_tensor(_nm("c_lnb"), [128, 8], F32).__enter__()
    t["bn1"] = nc.sbuf_tensor(_nm("c_bn1"), [128, 8], F32).__enter__()
    t["bn2"] = nc.sbuf_tensor(_nm("c_bn2"), [128, 8], F32).__enter__()
    t["fbg"] = nc.sbuf_tensor(_nm("c_fbg"), [128, 8], F32).__enter__()
    t["fwgT"] = nc.sbuf_tensor(_nm("c_fwgT"), [128, 8, 8], BF16).__enter__()
    t["bgw"] = nc.sbuf_tensor(_nm("c_bgw"), [128, 8, 4], BF16).__enter__()
    t["biww"] = nc.sbuf_tensor(_nm("c_biww"), [128, 8, 4], BF16).__enter__()
    t["bb1"] = nc.sbuf_tensor(_nm("c_bb1"), [128, 4, 32], F32).__enter__()
    t["ones128"] = nc.sbuf_tensor(_nm("c_o128"), [128, 128], BF16).__enter__()
    t["ltri"] = nc.sbuf_tensor(_nm("c_ltri"), [128, 128], BF16).__enter__()
    t["l8"] = nc.sbuf_tensor(_nm("c_l8"), [8, 8], BF16).__enter__()
    t["iotaT"] = nc.sbuf_tensor(_nm("c_iotaT"), [8, 16], F32).__enter__()
    t["piota"] = nc.sbuf_tensor(_nm("c_piota"), [128, 1], F32).__enter__()
    t["bb2"] = nc.sbuf_tensor(_nm("c_bb2"), [128, 4, 8], F32).__enter__()
    t["cpool"] = {}
    for vi, val in enumerate((LN_EPS, RMS_EPS, -0.5, 0.125, -1.0, 512.0)):
        tile = nc.sbuf_tensor(_nm(f"c_v{vi}"), [128, 1], F32).__enter__()
        nc.const_aps.register(F32, float(val), tile[:, :])
        t["cpool"][float(val)] = tile

    x_sb = nc.sbuf_tensor(_nm("x_sb"), [128, NCH, S], F32).__enter__()

    emit_consts(p)
    if "ffn1" in parts or "ffn2" in parts:
        emit_wconv(p)
    for sq in range(nseq):
        emit_load_x(p, sq, x_sb)
        if "attn_a" in p.parts:
            emit_attn_a(p, x_sb)
        if "ffn1" in p.parts:
            _emit_ffn_call(p, x_sb)
        if "attn_b" in p.parts:
            emit_attn_b(p, x_sb)
        if "ffn2" in p.parts:
            _emit_ffn_call(p, x_sb)
        emit_store_x(p, sq, x_sb)

    st = p.sem("so")
    with nc.Block() as b:
        @b.sync
        def _(sy):
            sy.wait_ge(st.h, st.val)
    return p


NT = 16


def emit_ffn_sparse(p, x_sb, xn_bf, xn_guard):
    nc, t, h = p.nc, p.t, p.h

    with nc.sbuf_tensor(_nm("s_wde"), [8, 16, 128], F32) as wde, \
         nc.sbuf_tensor(_nm("s_wdtm"), [128, 16, 8], F32) as wdtm, \
         nc.sbuf_tensor(_nm("s_spos"), [128, 16, 8], mybir.dt.int32) as spos_i, \
         nc.sbuf_tensor(_nm("s_idxg"), [128, NT, 8], mybir.dt.int32) as idxg_i:
        emit_gating(p, xn_bf, t["fwgT"], t["fbg"], 8, wde, wdtm_sb=wdtm)
        _ffn_routing(p, wde, wdtm, spos_i, idxg_i)
        _ffn_scatter(p, xn_bf, spos_i)
        xn_guard.__exit__(None, None, None)
        _ffn_tiles(p, idxg_i)
        _ffn_combine(p, x_sb, spos_i, wdtm)


def _ffn_routing(p, wde, wdtm, spos_i, idxg_i):
    nc, t, h = p.nc, p.t, p.h
    rpe, rdv, rac = p.sem("rpe"), p.sem("rdv"), p.sem("rac")

    with nc.sbuf_tensor(_nm("s_sel"), [8, S], BF16) as sel01, \
         nc.sbuf_tensor(_nm("s_selT"), [128, 16, 8], BF16) as selT, \
         nc.sbuf_tensor(_nm("s_pos"), [128, 16, 8], F32) as pos_tm, \
         nc.sbuf_tensor(_nm("s_r8"), [8, 8], F32) as r8, \
         nc.sbuf_tensor(_nm("s_ri8"), [8, 8], mybir.dt.int32) as ri8, \
         nc.sbuf_tensor(_nm("s_rb8"), [8, 1], BF16) as rb8, \
         nc.sbuf_tensor(_nm("s_cmp"), [8, 16], F32) as cmp16, \
         nc.sbuf_tensor(_nm("s_ter"), [1, 2 * NT], F32) as te_f, \
         nc.sbuf_tensor(_nm("s_brep"), [128, 8], F32) as b_rep, \
         nc.sbuf_tensor(_nm("s_m01"), [128, 8], F32) as m01:
        wde_flat = wde.rearrange("e q n -> e (q n)")

        with nc.Block() as b, \
             nc.psum_tensor(_nm("ps_r1"), [128, 2, 1024], BF16) as pss, \
             nc.psum_tensor(_nm("ps_r1b"), [128, 2, 512], F32) as psb:
            B_pe, B_dv, B_ac = rpe.val, rdv.val, rac.val

            @b.vector
            def _(v):
                v = SerialEngine(p, v)
                rdv.inc(v.tensor_single_scalar(sel01[:, :], wde_flat, 0.0, ALU.is_gt))
                v.reduce_sum(r8[:, 0:1], sel01[:, :], axis=AX.X)
                v.tensor_copy(ri8[:, 0:1], r8[:, 0:1])
                v.tensor_scalar_add(ri8[:, 1:2], ri8[:, 0:1], 511)
                v.tensor_single_scalar(ri8[:, 2:3], ri8[:, 1:2], 9,
                                       ALU.logical_shift_right)
                rdv.inc(v.tensor_copy(rb8[:, 0:1], ri8[:, 2:3]))

            @b.tensor
            def _(tt):
                tt.wait_ge(rdv.h, B_dv + 1)
                for i in range(16):
                    if i >= 2:
                        tt.wait_ge(rac.h, B_ac + i - 1)
                    rpe.inc(tt.transpose(pss[0:128, i % 2, 0:8],
                                         sel01[:, i * 128:(i + 1) * 128],
                                         t["ident"][0:8, 0:8]))
                tt.wait_ge(rdv.h, B_dv + 2)
                rpe.inc(tt.matmul(psb[0:8, 0, 0:1], t["l8"][:, :], rb8[:, 0:1],
                                  start=True, stop=True))

            @b.scalar
            def _(sc):
                sc = SerialEngine(p, sc)
                for i in range(16):
                    sc.wait_ge(rpe.h, B_pe + i + 1)
                    rac.inc(sc.activation(selT[:, i, :], pss[0:128, i % 2, 0:8],
                                          AF.Identity))
                sc.wait_ge(rpe.h, B_pe + 17)
                rac.inc(sc.activation(r8[:, 4:5], psb[0:8, 0, 0:1], AF.Identity,
                                      scale=512.0))

        with nc.Block() as b, \
             nc.psum_tensor(_nm("ps_r2"), [128, 4, 512], F32) as ps2, \
             nc.sbuf_tensor(_nm("s_ter"), [128, NT], F32) as ter_sb, \
             nc.sbuf_tensor(_nm("s_tmp"), [128, NT], F32) as tmp_sb:
            B_pe, B_dv, B_ac = rpe.val, rdv.val, rac.val

            @b.tensor
            def _(tt):
                rpe.inc(tt.transpose(ps2[0:1, 0, 0:8], r8[:, 4:5],
                                     t["identf"][0:8, 0:8]))
                tt.wait_ge(rac.h, B_ac + 1)
                rpe.inc(tt.matmul(ps2[:, 1, 0:8], t["ones_r"][0:1, :],
                                  te_f[:, NT:NT + 8], start=True, stop=True))
                tt.wait_ge(rdv.h, B_dv + 1)
                rpe.inc(tt.matmul(ps2[0:1, 2, 0:NT], t["ones_c"][0:8, 0:1],
                                  cmp16[:, :], start=True, stop=True))
                tt.wait_ge(rac.h, B_ac + 3)
                rpe.inc(tt.matmul(ps2[:, 3, 0:NT], t["ones_r"][0:1, :],
                                  te_f[:, 0:NT], start=True, stop=True))

            @b.scalar
            def _(sc):
                sc = SerialEngine(p, sc)
                sc.wait_ge(rpe.h, B_pe + 1)
                rac.inc(sc.activation(te_f[:, NT:NT + 8], ps2[0:1, 0, 0:8],
                                      AF.Identity))
                sc.wait_ge(rpe.h, B_pe + 2)
                rac.inc(sc.activation(b_rep[:, :], ps2[:, 1, 0:8], AF.Identity))
                sc.wait_ge(rpe.h, B_pe + 3)
                rac.inc(sc.activation(te_f[:, 0:NT], ps2[0:1, 2, 0:NT], AF.Identity,
                                      bias=-1.0))
                sc.wait_ge(rpe.h, B_pe + 4)
                rac.inc(sc.activation(ter_sb[:, :], ps2[:, 3, 0:NT], AF.Identity))

            @b.vector
            def _(v):
                v = SerialEngine(p, v)
                rdv.inc(v.tensor_single_scalar(cmp16[:, :], t["iotaT"][:, :],
                                               r8[:, 4:5], ALU.is_ge))
                v.wait_ge(rac.h, B_ac + 4)
                for hb in range(8):
                    v.tensor_scalar(tmp_sb[:, :], ter_sb[:, :], 1024.0,
                                    float(hb * 128), ALU.mult, ALU.add)
                    v.tensor_single_scalar(tmp_sb[:, :], tmp_sb[:, :],
                                           t["piota"][:, 0:1], ALU.add)
                    rdv.inc(v.tensor_copy(idxg_i[:, :, hb], tmp_sb[:, :]))

        with nc.Block() as b, \
             nc.psum_tensor(_nm("ps_r3"), [128, 2, 512], F32) as ps3:
            B_pe, B_dv, B_ac = rpe.val, rdv.val, rac.val

            @b.tensor
            def _(tt):
                for i in range(16):
                    if i >= 2:
                        tt.wait_ge(rac.h, B_ac + i - 1)
                    ins = None
                    for j in range(i + 1):
                        lh = t["ltri"] if j == i else t["ones128"]
                        ins = tt.matmul(ps3[:, i % 2, 0:8], lh[:, :], selT[:, j, :],
                                        start=(j == 0), stop=(j == i))
                    rpe.inc(ins)

            @b.scalar
            def _(sc):
                sc = SerialEngine(p, sc)
                for i in range(16):
                    sc.wait_ge(rpe.h, B_pe + i + 1)
                    rac.inc(sc.activation(pos_tm[:, i, :], ps3[:, i % 2, 0:8],
                                          AF.Identity))

            @b.vector
            def _(v):
                v = SerialEngine(p, v)
                for i in range(16):
                    v.wait_ge(rac.h, B_ac + i + 1)
                    v.tensor_tensor(pos_tm[:, i, :], pos_tm[:, i, :], b_rep[:, :],
                                    ALU.add)
                    v.tensor_single_scalar(m01[:, :], wdtm[:, i, :], 0.0, ALU.is_le)
                    v.scalar_tensor_tensor(pos_tm[:, i, :], m01[:, :], 1e9,
                                           pos_tm[:, i, :], ALU.mult, ALU.add)
                    rdv.inc(v.tensor_copy(spos_i[:, i, :], pos_tm[:, i, :]))


def _ffn_scatter(p, xn_bf, spos_i):
    nc, t, h = p.nc, p.t, p.h
    spe, sac = p.sem("spe"), p.sem("sac")
    sdm = [p.sem("sdm0"), p.sem("sdm1")]
    with nc.Block() as b, \
         nc.psum_tensor(_nm("ps_xt"), [128, 2, 8, 128], BF16) as pst, \
         nc.sbuf_tensor(_nm("s_xtm"), [128, 2, 1024], BF16) as xtm:
        B_pe, B_ac = spe.val, sac.val
        B_dm = [sdm[0].val, sdm[1].val]

        @b.tensor
        def _(tt):
            for i in range(16):
                if i >= 2:
                    tt.wait_ge(sac.h, B_ac + i - 1)
                ins = None
                for c in range(NCH):
                    ins = tt.transpose(pst[:, i % 2, c, :],
                                       xn_bf[:, c, i * 128:(i + 1) * 128],
                                       t["ident"][:, :])
                spe.inc(ins)

        @b.scalar
        def _(sc):
            for i in range(16):
                sc.wait_ge(spe.h, B_pe + i + 1)
                if i >= 2:
                    sc.wait_ge(sdm[i % 2].h, B_dm[i % 2] + 128 * (i // 2))
                sac.inc(sc.activation(xtm[:, i % 2, :],
                                      pst[:, i % 2, :, :].rearrange("p c n -> p (c n)"),
                                      AF.Identity))

        @b.gpsimd
        def _(g):
            for i in range(16):
                g.wait_ge(sac.h, B_ac + i + 1)
                for e in range(8):
                    g.indirect_dma_start(
                        h["xg_dram"][:, :],
                        IndirectOffsetOnAxis(spos_i[:, i, e:e + 1], 0),
                        xtm[:, i % 2, :], None,
                        bounds_check=t["r_bc"], oob_is_err=False,
                    ).then_inc(sdm[i % 2].h, 16)
                    sdm[i % 2].val += 16
            g.wait_ge(sdm[0].h, sdm[0].val)
            g.wait_ge(sdm[1].h, sdm[1].val)


_TILES_PHASES = 4
_T3_MODE = "full"


def _ffn_tiles(p, idxg_i):
    nc, t, h = p.nc, p.t, p.h
    fgm, fum, fsl, fh = p.sem("fgm"), p.sem("fum"), p.sem("fsl"), p.sem("fh")
    fdm, fds = p.sem("fdm"), p.sem("fds")
    s_g = [p.sem("wga0"), p.sem("wga1")]
    s_u = [p.sem("wgu0"), p.sem("wgu1")]
    s_d = [p.sem("wgd0"), p.sem("wgd1")]
    s_x = p.sem("wxg")
    tpe, tac = p.sem("tpe"), p.sem("tac")

    with nc.sbuf_tensor(_nm("t_h"), [128, 32, 512], BF16) as h_bf, \
         nc.sbuf_tensor(_nm("t_wga"), [128, 8, 512], BF16) as wga, \
         nc.sbuf_tensor(_nm("t_wgb"), [128, 8, 512], BF16) as wgb, \
         nc.sbuf_tensor(_nm("t_wua"), [128, 8, 512], BF16) as wua, \
         nc.sbuf_tensor(_nm("t_wub"), [128, 8, 512], BF16) as wub, \
         nc.sbuf_tensor(_nm("t_wd0"), [128, 4, 1024], BF16) as wd0, \
         nc.sbuf_tensor(_nm("t_wd1"), [128, 4, 1024], BF16) as wd1, \
         nc.sbuf_tensor(_nm("t_xf"), [128, 8, 512], BF16) as xg_f, \
         nc.sbuf_tensor(_nm("t_xtm"), [128, 4, 1024], BF16) as xg_tm, \
         nc.sbuf_tensor(_nm("t_sg"), [128, 4, 512], BF16) as sg, \
         nc.sbuf_tensor(_nm("t_of"), [128, 8, 512], BF16) as oslot_f:
        wds = [wd0, wd1]
        gbuf = [(wga, wua), (wgb, wub)]

        for ti in range(NT):
            with nc.Block() as b, \
                 nc.psum_tensor(_nm("ps_ti"), [128, 2, 8, 128], BF16) as psi:
                B_x, B_pe, B_ac = s_x.val, tpe.val, tac.val

                @b.gpsimd
                def _(g):
                    g.dma_start(xg_tm[:, :, :],
                                h["xg_dram"][ti * 512:(ti + 1) * 512].rearrange(
                                    "(a p) n -> p a n", p=128)).then_inc(s_x.h, 16)
                    s_x.val += 16
                    for hb in range(2):
                        g.indirect_dma_start(
                            gbuf[hb][0].rearrange("p c n -> p (c n)"), None,
                            h["wg_bf"].rearrange("s p c n -> (s p) (c n)"),
                            IndirectOffsetOnAxis(idxg_i[:, ti, hb:hb + 1], 0),
                        ).then_inc(s_g[hb].h, 16)
                        s_g[hb].val += 16
                        g.indirect_dma_start(
                            gbuf[hb][1].rearrange("p c n -> p (c n)"), None,
                            h["wu_bf"].rearrange("s p c n -> (s p) (c n)"),
                            IndirectOffsetOnAxis(idxg_i[:, ti, hb:hb + 1], 0),
                        ).then_inc(s_u[hb].h, 16)
                        s_u[hb].val += 16

                @b.tensor
                def _(tt):
                    tt.wait_ge(s_x.h, B_x + 16)
                    for a in range(4):
                        if a >= 2:
                            tt.wait_ge(tac.h, B_ac + a - 1)
                        ins = None
                        for c in range(NCH):
                            ins = tt.transpose(psi[:, a % 2, c, :],
                                               xg_tm[:, a, c * 128:(c + 1) * 128],
                                               t["ident"][:, :])
                        tpe.inc(ins)

                @b.scalar
                def _(sc):
                    for a in range(4):
                        sc.wait_ge(tpe.h, B_pe + a + 1)
                        tac.inc(sc.activation(
                            xg_f[:, :, a * 128:(a + 1) * 128],
                            psi[:, a % 2, :, :], AF.Identity))

            if _TILES_PHASES < 2:
                continue
            with nc.Block() as b, \
                 nc.psum_tensor(_nm("ps_tg"), [128, 4, 512], F32) as psg, \
                 nc.psum_tensor(_nm("ps_tu"), [128, 4, 512], F32) as psu:
                b_gm, b_um, b_sl, b_h = fgm.val, fum.val, fsl.val, fh.val
                b_g = [s_g[0].val, s_g[1].val]
                b_u = [s_u[0].val, s_u[1].val]

                @b.gpsimd
                def _(g):
                    for hb in range(2, 8):
                        g.wait_ge(fgm.h, b_gm + hb - 1)
                        g.indirect_dma_start(
                            gbuf[hb % 2][0].rearrange("p c n -> p (c n)"), None,
                            h["wg_bf"].rearrange("s p c n -> (s p) (c n)"),
                            IndirectOffsetOnAxis(idxg_i[:, ti, hb:hb + 1], 0),
                        ).then_inc(s_g[hb % 2].h, 16)
                        s_g[hb % 2].val += 16
                        g.wait_ge(fum.h, b_um + hb - 1)
                        g.indirect_dma_start(
                            gbuf[hb % 2][1].rearrange("p c n -> p (c n)"), None,
                            h["wu_bf"].rearrange("s p c n -> (s p) (c n)"),
                            IndirectOffsetOnAxis(idxg_i[:, ti, hb:hb + 1], 0),
                        ).then_inc(s_u[hb % 2].h, 16)
                        s_u[hb % 2].val += 16

                @b.tensor
                def _(tt):
                    for hb in range(8):
                        tt.wait_ge(s_g[hb % 2].h, b_g[hb % 2] + 16 * (hb // 2))
                        if hb >= 1:
                            tt.wait_ge(fsl.h, b_sl + hb)
                        ins = None
                        for mc in range(4):
                            for c in range(NCH):
                                ins = tt.matmul(psg[:, mc, :],
                                                gbuf[hb % 2][0][:, c, mc * 128:(mc + 1) * 128],
                                                xg_f[:, c, :],
                                                start=(c == 0), stop=(c == NCH - 1))
                        fgm.inc(ins)
                        tt.wait_ge(s_u[hb % 2].h, b_u[hb % 2] + 16 * (hb // 2))
                        if hb >= 1:
                            tt.wait_ge(fh.h, b_h + hb)
                        ins = None
                        for mc in range(4):
                            for c in range(NCH):
                                ins = tt.matmul(psu[:, mc, :],
                                                gbuf[hb % 2][1][:, c, mc * 128:(mc + 1) * 128],
                                                xg_f[:, c, :],
                                                start=(c == 0), stop=(c == NCH - 1))
                        fum.inc(ins)

                @b.scalar
                def _(sc):
                    for hb in range(8):
                        sc.wait_ge(fgm.h, b_gm + hb + 1)
                        if hb >= 1:
                            sc.wait_ge(fh.h, b_h + hb)
                        fsl.inc(sc.activation(sg.rearrange("p c n -> p (c n)"),
                                              psg.rearrange("p c n -> p (c n)"),
                                              AF.Silu))

                @b.vector
                def _(v):
                    for hb in range(8):
                        v.wait_ge(fsl.h, b_sl + hb + 1)
                        v.wait_ge(fum.h, b_um + hb + 1)
                        fh.inc(v.tensor_tensor(
                            h_bf[:, hb * 4:(hb + 1) * 4, :].rearrange("p c n -> p (c n)"),
                            sg.rearrange("p c n -> p (c n)"),
                            psu.rearrange("p c n -> p (c n)"), ALU.mult))

            if _TILES_PHASES < 3:
                continue
            with nc.Block() as b, \
                 nc.psum_tensor(_nm("ps_td"), [128, 8, 512], F32) as psd:
                b_dm, b_ds = fdm.val, fds.val
                b_d = [s_d[0].val, s_d[1].val]

                @b.gpsimd
                def _(g):
                    for sl in range(8):
                        if sl >= 2 and _T3_MODE != "g":
                            g.wait_ge(fds.h, b_ds + sl - 1)
                        g.indirect_dma_start(
                            wds[sl % 2].rearrange("p c n -> p (c n)"), None,
                            h["wd_bf"].rearrange("s p c n -> (s p) (c n)"),
                            IndirectOffsetOnAxis(idxg_i[:, ti, sl:sl + 1], 0),
                        ).then_inc(s_d[sl % 2].h, 16)
                        s_d[sl % 2].val += 16

                @b.tensor
                def _(tt):
                    if _T3_MODE == "g":
                        return
                    for sl in range(8):
                        tt.wait_ge(s_d[sl % 2].h, b_d[sl % 2] + 16 * (sl // 2 + 1))
                        for fo in range(8):
                            ins = None
                            for c in range(4):
                                ins = tt.matmul(psd[:, fo, :],
                                                wds[sl % 2][:, c, fo * 128:(fo + 1) * 128],
                                                h_bf[:, sl * 4 + c, :],
                                                start=(sl == 0 and c == 0),
                                                stop=(sl == 7 and c == 3))
                            if sl == 7:
                                fdm.inc(ins)
                        if sl < 7:
                            fds.inc(ins)

                @b.scalar
                def _(sc):
                    if _T3_MODE in ("g", "gm"):
                        return
                    for fo in range(8):
                        sc.wait_ge(fdm.h, b_dm + fo + 1)
                        sc.activation(oslot_f[:, fo, :], psd[:, fo, :], AF.Identity)

            if _TILES_PHASES < 4:
                continue
            with nc.Block() as b, \
                 nc.psum_tensor(_nm("ps_to"), [128, 2, 8, 128], BF16) as pso:
                B_pe, B_ac, B_x = tpe.val, tac.val, s_x.val

                @b.tensor
                def _(tt):
                    for a in range(4):
                        if a >= 2:
                            tt.wait_ge(tac.h, B_ac + a - 1)
                        ins = None
                        for c in range(NCH):
                            ins = tt.transpose(pso[:, a % 2, c, :],
                                               oslot_f[:, c, a * 128:(a + 1) * 128],
                                               t["ident"][:, :])
                        tpe.inc(ins)

                @b.scalar
                def _(sc):
                    for a in range(4):
                        sc.wait_ge(tpe.h, B_pe + a + 1)
                        tac.inc(sc.activation(xg_tm[:, a, :],
                                              pso[:, a % 2, :, :].rearrange(
                                                  "p c n -> p (c n)"),
                                              AF.Identity))

                @b.gpsimd
                def _(g):
                    g.wait_ge(tac.h, B_ac + 4)
                    g.dma_start(h["os_dram"][ti * 512:(ti + 1) * 512].rearrange(
                        "(a p) n -> p a n", p=128),
                        xg_tm[:, :, :]).then_inc(s_x.h, 16)
                    s_x.val += 16
                    g.wait_ge(s_x.h, s_x.val)


def _ffn_combine(p, x_sb, spos_i, wdtm):
    nc, t, h = p.nc, p.t, p.h
    cdm = [p.sem("cdm0"), p.sem("cdm1")]
    cac_, cyb = p.sem("cac"), p.sem("cyb")
    cpe, cxa, cgp = p.sem("cpe"), p.sem("cxa"), p.sem("cgp")
    with nc.Block() as b, \
         nc.psum_tensor(_nm("ps_cb"), [128, 2, 8, 128], BF16) as psc, \
         nc.sbuf_tensor(_nm("c_gt"), [128, 2, 1024], BF16) as gtile, \
         nc.sbuf_tensor(_nm("c_y"), [128, 1024], F32) as y_tm, \
         nc.sbuf_tensor(_nm("c_yb"), [128, 1024], BF16) as y_bf:
        B_dm = [cdm[0].val, cdm[1].val]
        B_ac, B_yb = cac_.val, cyb.val
        B_pe, B_xa, B_gp = cpe.val, cxa.val, cgp.val

        @b.gpsimd
        def _(g):
            cgp.inc(g.memset(gtile[:, :, :], 0.0))
            g.wait_ge(cgp.h, cgp.val)
            k = 0
            for i in range(16):
                for e in range(8):
                    if k >= 2:
                        g.wait_ge(cac_.h, B_ac + k - 1)
                    g.indirect_dma_start(
                        gtile[:, k % 2, :], None,
                        h["os_dram"][:, :],
                        IndirectOffsetOnAxis(spos_i[:, i, e:e + 1], 0),
                        bounds_check=t["r_bc"], oob_is_err=False,
                    ).then_inc(cdm[k % 2].h, 16)
                    cdm[k % 2].val += 16
                    k += 1

        @b.vector
        def _(v):
            v = SerialEngine(p, v)
            k = 0
            for i in range(16):
                for e in range(8):
                    v.wait_ge(cdm[k % 2].h, B_dm[k % 2] + 16 * (k // 2 + 1))
                    wcol = wdtm[:, i, e:e + 1]
                    if e == 0:
                        ins = v.tensor_single_scalar(y_tm[:, :], gtile[:, k % 2, :],
                                                     wcol, ALU.mult)
                    else:
                        ins = v.scalar_tensor_tensor(y_tm[:, :], gtile[:, k % 2, :],
                                                     wcol, y_tm[:, :],
                                                     ALU.mult, ALU.add)
                    cac_.inc(ins)
                    k += 1
                cyb.inc(v.tensor_copy(y_bf[:, :], y_tm[:, :]))
                v.wait_ge(cpe.h, B_pe + i + 1)
                cxa.inc(v.tensor_tensor(
                    x_sb[:, :, i * 128:(i + 1) * 128],
                    x_sb[:, :, i * 128:(i + 1) * 128],
                    psc[:, i % 2, :, :], ALU.add))

        @b.tensor
        def _(tt):
            for i in range(16):
                tt.wait_ge(cyb.h, B_yb + i + 1)
                if i >= 2:
                    tt.wait_ge(cxa.h, B_xa + i - 1)
                ins = None
                for c in range(NCH):
                    ins = tt.transpose(psc[:, i % 2, c, :],
                                       y_bf[:, c * 128:(c + 1) * 128],
                                       t["ident"][:, :])
                cpe.inc(ins)


def _emit_ffn_call(p, x_sb):
    nc, t = p.nc, p.t
    xn_g = nc.sbuf_tensor(_nm("f_xn"), [128, NCH, S], BF16, side="right")
    xn_bf = xn_g.__enter__()
    with nc.sbuf_tensor(_nm("f_mu1"), [128, S], F32) as mu1, \
         nc.sbuf_tensor(_nm("f_rs1"), [128, S], F32) as rs1:
        emit_stats(p, x_sb, mu1, rs1, LN_EPS, "x")
        emit_ln_apply(p, x_sb, mu1, rs1, xn_bf)
    emit_ffn_sparse(p, x_sb, xn_bf, xn_g)



_CACHE = {}


def kernel(**inputs):
    weights = prep_inputs(inputs)
    x = np.asarray(inputs["L0_attn_a_x"], np.float32)
    xt = np.ascontiguousarray(x.transpose(0, 2, 1))

    key = ("full", NSEQ, _PARTS)
    if key not in _CACHE:
        _CACHE[key] = build_program(NSEQ, _PARTS)
    p = _CACHE[key]

    in_maps = []
    for core in range(NCORES):
        m = {"xin": np.ascontiguousarray(xt[core * NSEQ:(core + 1) * NSEQ])}
        m.update(weights)
        in_maps.append(m)
    res = run_bass_kernel_spmd(p.nc, in_maps, core_ids=list(range(NCORES)))
    out = np.concatenate([r["xout"] for r in res.results], axis=0)
    return np.ascontiguousarray(out.transpose(0, 2, 1))


# version 05 (final)
# speedup vs baseline: 1.0144x; 1.0144x over previous
import numpy as np
import concourse.bass as bass
import concourse.mybir as mybir
from concourse.bass_utils import run_bass_kernel_spmd
from concourse.bass_core_rust import IndirectOffsetOnAxis

F32 = mybir.dt.float32
BF16 = mybir.dt.bfloat16
AF = mybir.ActivationFunctionType
ALU = mybir.AluOpType
AX = mybir.AxisListType

S = 2048
D = 1024
NCH = 8
NBK = 4
NSEQ = 4
NCORES = 8
NEG = -1.0e9
LN_EPS = 1e-5
RMS_EPS = 1e-6
BISECT_ITERS = 42
ETA = 2.0 ** -38

_PARTS = ("attn_a", "ffn1", "attn_b", "ffn2")

_NAME_CTR = [0]


def _nm(base):
    _NAME_CTR[0] += 1
    return f"{base}_{_NAME_CTR[0]}"



def _rope_tables():
    ROPE_BASE = 10000.0
    dim = 64
    inv_freq = 1.0 / ROPE_BASE ** (np.arange(0, dim, 2, dtype=np.float32) / dim)
    t = np.arange(S, dtype=np.float32)
    freqs = np.outer(t, inv_freq)
    cos = np.concatenate([np.cos(freqs), np.cos(freqs)], -1)[:, ::2]
    sin = np.concatenate([np.sin(freqs), np.sin(freqs)], -1)[:, ::2]
    return (np.ascontiguousarray(cos.T).astype(np.float32),
            np.ascontiguousarray(sin.T).astype(np.float32))


def _chunked_vec(v):
    return np.ascontiguousarray(np.asarray(v, np.float32).reshape(NCH, 128).T)


def _dev3(w, pc=128):
    w = np.asarray(w, np.float32)
    c = w.shape[0] // pc
    return np.ascontiguousarray(w.reshape(c, pc, w.shape[1]).transpose(1, 0, 2))


def prep_inputs(kw):
    f = lambda a: np.asarray(a, dtype=np.float32)
    out = {}
    out["a_wdqT"] = _dev3(f(kw["L0_attn_a_w_down_q"]).T)
    out["a_wdkvT"] = _dev3(f(kw["L0_attn_a_w_down_kv"]).T)
    perm = np.concatenate([np.arange(0, 64, 2), np.arange(1, 64, 2)])
    fullp = np.concatenate([hh * 64 + perm for hh in range(16)])
    out["a_wupqT"] = _dev3(f(kw["L0_attn_a_w_up_q"]).T[:, fullp])
    out["a_wupkT"] = _dev3(f(kw["L0_attn_a_w_up_k"]).T[:, fullp])
    out["a_wupvT"] = _dev3(f(kw["L0_attn_a_w_up_v"]).T)
    out["a_woT"] = _dev3(f(kw["L0_attn_a_w_out"]).T)
    out["a_lns"] = _chunked_vec(kw["L0_attn_a_ln_scale"])
    out["a_lnb"] = _chunked_vec(kw["L0_attn_a_ln_bias"])
    cosT, sinT = _rope_tables()
    tile4 = lambda a: np.ascontiguousarray(np.tile(a, (4, 1)))
    out["r_cos"] = tile4(cosT)
    out["r_sin"] = tile4(sinT)
    out["f_wgT"] = _dev3(f(kw["L0_ffn_wg"]).T)
    out["f_bg"] = np.ascontiguousarray(
        np.broadcast_to(f(kw["L0_ffn_bg"])[None, :], (128, 8)))
    WgT = [_dev3(f(kw["L0_ffn_Wgate"])[e].T) for e in range(8)]
    WuT = [_dev3(f(kw["L0_ffn_Wup"])[e].T) for e in range(8)]
    WdT = [_dev3(f(kw["L0_ffn_Wdown"])[e].T) for e in range(8)]
    out["f_WgS"] = np.ascontiguousarray(np.stack(
        [WgT[e][:, :, hb * 512:(hb + 1) * 512] for e in range(8) for hb in range(8)]))
    out["f_WuS"] = np.ascontiguousarray(np.stack(
        [WuT[e][:, :, hb * 512:(hb + 1) * 512] for e in range(8) for hb in range(8)]))
    out["f_WdS"] = np.ascontiguousarray(np.stack(
        [WdT[e][:, sl * 4:(sl + 1) * 4, :] for e in range(8) for sl in range(8)]))
    out["iotaT"] = np.ascontiguousarray(
        np.broadcast_to(np.arange(16, dtype=np.float32)[None, :] * 512.0, (8, 16)))
    out["piota"] = np.arange(128, dtype=np.float32).reshape(128, 1)
    out["b_wq"] = _dev3(f(kw["L0_attn_b_wq"]))
    out["b_wk"] = _dev3(f(kw["L0_attn_b_wk"]))
    out["b_wv"] = _dev3(f(kw["L0_attn_b_wv"]))
    out["b_wo"] = _dev3(f(kw["L0_attn_b_wo"]))
    out["b_iwq"] = _dev3(f(kw["L0_attn_b_idx_wq"]))
    out["b_iwk"] = _dev3(f(kw["L0_attn_b_idx_wk"]))
    out["b_iww"] = _dev3(f(kw["L0_attn_b_idx_ww"]))
    out["b_gw"] = _dev3(f(kw["L0_attn_b_gate_w"]))
    out["b_n1"] = _chunked_vec(kw["L0_attn_b_norm1_w"])
    out["b_n2"] = _chunked_vec(kw["L0_attn_b_norm2_w"])
    out["b_w1"] = np.ascontiguousarray(
        np.stack([_dev3(f(kw["L0_attn_b_w1"])[e]) for e in range(4)]))
    out["b_w2"] = np.ascontiguousarray(
        np.stack([_dev3(f(kw["L0_attn_b_w2"])[e]) for e in range(4)]))
    b1 = f(kw["L0_attn_b_b1"])
    b2 = f(kw["L0_attn_b_b2"])
    out["b_b1"] = np.ascontiguousarray(
        np.stack([b1[e].reshape(32, 128).T for e in range(4)]))
    out["b_b2"] = np.ascontiguousarray(
        np.stack([b2[e].reshape(8, 128).T for e in range(4)]))
    return out



_MARKS = {}


class SemCtr:
    def __init__(self, h):
        self.h = h
        self.val = 0

    def inc(self, inst, n=1):
        inst.then_inc(self.h, n)
        self.val += n
        try:
            _MARKS[inst.ins.name] = (self, self.val)
        except Exception:
            pass


class P:
    def __init__(self, nseq, parts):
        self.nseq = nseq
        self.parts = parts
        self.nc = bass.Bass("TRN2", target_bir_lowering=False)
        self.sems = {}
        self.h = {}
        self.t = {}

    def sem(self, name):
        if name not in self.sems:
            self.sems[name] = SemCtr(self.nc.semaphore(name).__enter__())
        return self.sems[name]


def nb512(nb):
    return slice(nb * 512, (nb + 1) * 512)


def edge(p, eng, ins, name="ch"):
    s = p.sem(name)
    s.inc(ins)
    eng.wait_ge(s.h, s.val)


_SER_SKIP = {"wait_ge", "engine_nop", "nop", "dma_start", "indirect_dma_start",
             "wait_op", "sem_inc"}


class SerialEngine:

    def __init__(self, p, eng, name="ch"):
        self.__dict__["_p"] = p
        self.__dict__["_e"] = eng
        self.__dict__["_s"] = p.sem(name)
        self.__dict__["_pend"] = None

    def _fence(self):
        pend = self.__dict__["_pend"]
        if pend is None:
            return
        try:
            mark = _MARKS.pop(pend.ins.name, None)
        except Exception:
            mark = None
        if mark is not None:
            sctr, val = mark
            self._e.wait_ge(sctr.h, val)
        else:
            self._s.inc(pend)
            self._e.wait_ge(self._s.h, self._s.val)
        self.__dict__["_pend"] = None

    def __getattr__(self, k):
        a = getattr(self.__dict__["_e"], k)
        if not callable(a) or k in _SER_SKIP or k.startswith("_"):
            return a
        def wrap(*args, **kw):
            self._fence()
            ins = a(*args, **kw)
            self.__dict__["_pend"] = ins
            return ins
        return wrap



def emit_consts(p):
    nc, t, h = p.nc, p.t, p.h
    w = p.sem("wg")
    t["r_neg"] = nc.gpsimd.alloc_register("r_negp")
    t["r_bc"] = nc.gpsimd.alloc_register("r_bcp")
    with nc.Block() as b:
        @b.gpsimd
        def _(g):
            g = SerialEngine(p, g)
            g.reg_mov(t["r_neg"], float(NEG))
            g.reg_mov(t["r_bc"], NT * 512 - 1)
            g.memset(t["ident"][:, :], 1.0)
            g.memset(t["identf"][:, :], 1.0)
            g.memset(t["id64"][:, :], 1.0)
            g.memset(t["ones_c"][:, :], 1.0)
            g.memset(t["ones_r"][:, :], 1.0)
            g.memset(t["sel8"][:, :], 1.0)
            g.memset(t["sel4"][:, :], 1.0)
            g.memset(t["ones128"][:, :], 1.0)
            g.memset(t["ltri"][:, :], 1.0)
            g.memset(t["l8"][:, :], 1.0)
            for val, tile in t["cpool"].items():
                g.memset(tile[:, :], float(val)).then_inc(p.sem("gp").h, 0) \
                    if False else g.memset(tile[:, :], float(val))
            gp = p.sem("gp")
            gp.inc(g.engine_nop())
            g.wait_ge(gp.h, gp.val)
            g.affine_select(t["ident"][:, :], t["ident"][:, :], [[-1, 128]],
                            ALU.is_equal, 0.0, base=0, channel_multiplier=1)
            g.affine_select(t["identf"][:, :], t["identf"][:, :], [[-1, 128]],
                            ALU.is_equal, 0.0, base=0, channel_multiplier=1)
            g.affine_select(t["id64"][0:64, :], t["id64"][0:64, :], [[-1, 64]],
                            ALU.is_equal, 0.0, base=0, channel_multiplier=1)
            g.affine_select(t["id64"][64:128, :], t["id64"][64:128, :], [[-1, 64]],
                            ALU.is_equal, 0.0, base=0, channel_multiplier=1)
            g.affine_select(t["sel8"][:, :], t["sel8"][:, :], [[-1, 8], [0, 128]],
                            ALU.is_equal, 0.0, base=0, channel_multiplier=1)
            g.affine_select(t["sel4"][:, :], t["sel4"][:, :], [[-1, 4], [0, 128]],
                            ALU.is_equal, 0.0, base=0, channel_multiplier=1)
            g.affine_select(t["ltri"][:, :], t["ltri"][:, :], [[1, 128]],
                            ALU.is_ge, 0.0, base=-1, channel_multiplier=-1)
            g.affine_select(t["l8"][:, :], t["l8"][:, :], [[1, 8]],
                            ALU.is_ge, 0.0, base=-1, channel_multiplier=-1)
            for name, dst in [("a_lns", "lns"), ("a_lnb", "lnb"), ("b_n1", "bn1"),
                              ("b_n2", "bn2"), ("f_bg", "fbg")]:
                g.dma_start(t[dst][:, :], h[name]).then_inc(w.h, 16)
                w.val += 16
            for name, dst in [("f_wgT", "fwgT"), ("b_gw", "bgw"), ("b_iww", "biww")]:
                g.dma_start(t[dst][:, :, :], h[name]).then_inc(w.h, 16)
                w.val += 16
            g.dma_start(t["bb1"][:, :, :],
                        h["b_b1"].rearrange("e p c -> p e c")).then_inc(w.h, 16)
            g.dma_start(t["bb2"][:, :, :],
                        h["b_b2"].rearrange("e p c -> p e c")).then_inc(w.h, 16)
            g.dma_start(t["iotaT"][:, :], h["iotaT"]).then_inc(w.h, 16)
            g.dma_start(t["piota"][:, :], h["piota"]).then_inc(w.h, 16)
            w.val += 64
            g.wait_ge(w.h, w.val)


def emit_wconv(p):
    nc, t, h = p.nc, p.t, p.h
    wz = p.sem("wcz")
    wq = p.sem("wcq")
    wv = p.sem("wcv")
    with nc.Block() as b, \
         nc.sbuf_tensor(_nm("wc_z"), [128, 4096], BF16) as bz:

        @b.gpsimd
        def _(g):
            wz.inc(g.memset(bz[:, :], 0.0))
            g.wait_ge(wz.h, wz.val)
            for i in range(16):
                g.dma_start(h["xg_dram"][i * 512:(i + 1) * 512].rearrange(
                    "(a p) n -> p a n", p=128), bz.rearrange("p (a n) -> p a n", a=4)
                ).then_inc(wq.h, 16)
                wq.val += 16
            for name, dst in (("f_WgS", "wg_bf"), ("f_WuS", "wu_bf"),
                              ("f_WdS", "wd_bf")):
                g.dma_start(h[dst][:, :, :, :], h[name][:, :, :, :]).then_inc(wv.h, 16)
                wv.val += 16
            t["wconv_done"] = [(wv.h, wv.val)]
            g.wait_ge(wq.h, wq.val)


def emit_load_x(p, sq, x_sb):
    nc = p.nc
    sx = p.sem("wx")
    with nc.Block() as b:
        @b.sync
        def _(sy):
            sy.dma_start(x_sb[:, :, :],
                         p.h["xin"][sq].rearrange("(c p) s -> p c s", p=128)
                         ).then_inc(sx.h, 16)
            sx.val += 16
            sy.wait_ge(sx.h, sx.val)


def emit_store_x(p, sq, x_sb):
    nc = p.nc
    st = p.sem("so")
    with nc.Block() as b:
        @b.sync
        def _(sy):
            sy.dma_start(p.h["xout"][sq].rearrange("(c p) s -> p c s", p=128),
                         x_sb[:, :, :]).then_inc(st.h, 16)
            st.val += 16



def emit_stats(p, src, murep, rsrep, eps, mode, mu1rep=None, rs1rep=None):
    nc, t = p.nc, p.t
    want_mean = mode in ("x", "ln")
    derived = mode in ("ln", "lnrms")
    s_src = p.sem("sts")
    s_sq = p.sem("stq")
    s_mm = p.sem("stm")
    s_row = p.sem("str")
    s_rs = p.sem("stz")
    s_rep = p.sem("rpm")
    NU = NCH * 2
    HC = S // 2

    def u_src(u):
        c, hf = divmod(u, 2)
        return src[:, c, hf * HC:(hf + 1) * HC]

    with nc.sbuf_tensor(_nm("st_rows"), [65, S], F32) as rows_t:
        r_mu = rows_t[0:1, :]
        r_var = rows_t[32:33, :]
        r_scr = rows_t[64:65, :]
        with nc.Block() as b, \
             nc.psum_tensor(_nm("ps_st"), [1, 2 * S], F32) as ps_st, \
             nc.sbuf_tensor(_nm("st_sqa"), [128, HC], F32) as sqa, \
             nc.sbuf_tensor(_nm("st_sqb"), [128, HC], F32) as sqb, \
             nc.sbuf_tensor(_nm("st_sra"), [128, HC], F32) as sra, \
             nc.sbuf_tensor(_nm("st_srb"), [128, HC], F32) as srb:

            b_src, b_sq, b_mm, b_row = s_src.val, s_sq.val, s_mm.val, s_row.val

            @b.vector
            def _(v):
                v = SerialEngine(p, v)
                if derived:
                    for u in range(NU):
                        c, hf = divmod(u, 2)
                        scr = sra if u % 2 == 0 else srb
                        if u >= 2:
                            v.wait_ge(s_mm.h, b_mm + (u - 1))
                        sl = slice(hf * HC, (hf + 1) * HC)
                        v.tensor_tensor(scr[:, :], u_src(u), mu1rep[:, sl],
                                        ALU.subtract)
                        ins = v.tensor_tensor(scr[:, :], scr[:, :], rs1rep[:, sl],
                                              ALU.mult)
                        s_src.inc(ins)
                v.wait_ge(s_mm.h, b_mm + NU)
                if want_mean:
                    v.tensor_scalar_mul(r_mu, ps_st[:, 0:S], 1.0 / D)
                s_row.inc(v.tensor_scalar_mul(r_var, ps_st[:, S:2 * S], 1.0 / D))

            @b.scalar
            def _(sc):
                sc = SerialEngine(p, sc)
                for u in range(NU):
                    sq = sqa if u % 2 == 0 else sqb
                    if derived:
                        sc.wait_ge(s_src.h, b_src + u + 1)
                        src_u = (sra if u % 2 == 0 else srb)[:, :]
                    else:
                        src_u = u_src(u)
                    if u >= 2:
                        sc.wait_ge(s_mm.h, b_mm + (u - 1))
                    s_sq.inc(sc.activation(sq[:, :], src_u, AF.Square))

            @b.tensor
            def _(tt):
                for u in range(NU):
                    c, hf = divmod(u, 2)
                    tt.wait_ge(s_sq.h, b_sq + u + 1)
                    sq = sqa if u % 2 == 0 else sqb
                    scr = (sra if u % 2 == 0 else srb) if derived else None
                    ins = None
                    for nb in range(2):
                        off = hf * HC + nb * 512
                        if want_mean:
                            rhs = scr[:, nb * 512:(nb + 1) * 512] if derived \
                                else src[:, c, off:off + 512]
                            tt.matmul(ps_st[:, off:off + 512], t["ones_c"][:, :], rhs,
                                      start=(c == 0), stop=(c == NCH - 1))
                        ins = tt.matmul(ps_st[:, S + off:S + off + 512],
                                        t["ones_c"][:, :],
                                        sq[:, nb * 512:(nb + 1) * 512],
                                        start=(c == 0), stop=(c == NCH - 1))
                    s_mm.inc(ins)

        with nc.Block() as b, nc.psum_tensor(_nm("ps_rep"), [128, 2 * S], F32) as ps_rep:
            b_rep = s_rep.val
            b_row2 = p.sem("sr2").val
            s_row2 = p.sem("sr2")

            @b.tensor
            def _(tt):
                tt.wait_ge(s_row.h, s_row.val)
                ins = None
                for nb in range(NBK):
                    if want_mean:
                        tt.matmul(ps_rep[:, nb512(nb)], t["ones_r"][0:1, :],
                                  r_mu[:, nb512(nb)], start=True, stop=True)
                    ins = tt.matmul(ps_rep[:, S + nb * 512:S + (nb + 1) * 512],
                                    t["ones_r"][32:33, :],
                                    r_var[:, nb512(nb)], start=True, stop=True)
                s_rep.inc(ins)

            @b.vector
            def _(v):
                v = SerialEngine(p, v)
                v.wait_ge(s_rep.h, b_rep + 1)
                if want_mean:
                    edge(p, v, v.tensor_copy(murep[:, :], ps_rep[:, 0:S]))
                    edge(p, v, v.tensor_tensor(rsrep[:, :], murep[:, :], murep[:, :],
                                               ALU.mult))
                    s_row2.inc(v.tensor_tensor(rsrep[:, :], ps_rep[:, S:2 * S],
                                               rsrep[:, :], ALU.subtract))
                else:
                    s_row2.inc(v.tensor_copy(rsrep[:, :], ps_rep[:, S:2 * S]))

            @b.scalar
            def _(sc):
                sc = SerialEngine(p, sc)
                sc.wait_ge(s_row2.h, b_row2 + 1)
                edge(p, sc, sc.activation(ps_rep[:, 0:S], rsrep[:, :], AF.Ln,
                                          bias=float(eps)))
                s_rs.inc(sc.activation(rsrep[:, :], ps_rep[:, 0:S], AF.Exp,
                                       scale=-0.5))


def emit_ln_apply(p, src, murep, rsrep, out_bf, mu2rep=None, rs2rep=None,
                  gamma=None, beta=None, add_inner_residual=False):
    nc = p.nc
    with nc.Block() as b, nc.sbuf_tensor(_nm("lnt"), [128, S], F32) as lnt:
        @b.vector
        def _(v):
            v = SerialEngine(p, v)
            for c in range(NCH):
                if murep is not None:
                    edge(p, v, v.tensor_tensor(lnt[:, :], src[:, c, :], murep[:, :],
                                               ALU.subtract))
                    edge(p, v, v.tensor_tensor(lnt[:, :], lnt[:, :], rsrep[:, :],
                                               ALU.mult))
                else:
                    edge(p, v, v.tensor_tensor(lnt[:, :], src[:, c, :], rsrep[:, :],
                                               ALU.mult))
                if mu2rep is not None:
                    edge(p, v, v.tensor_tensor(lnt[:, :], lnt[:, :], mu2rep[:, :],
                                               ALU.subtract))
                if rs2rep is not None:
                    edge(p, v, v.tensor_tensor(lnt[:, :], lnt[:, :], rs2rep[:, :],
                                               ALU.mult))
                if gamma is not None and beta is not None:
                    v.tensor_scalar(out_bf[:, c, :], lnt[:, :], gamma[:, c:c + 1],
                                    beta[:, c:c + 1], ALU.mult, ALU.add)
                elif gamma is not None:
                    v.tensor_single_scalar(out_bf[:, c, :], lnt[:, :],
                                           gamma[:, c:c + 1], ALU.mult)
                else:
                    v.tensor_copy(out_bf[:, c, :], lnt[:, :])
            if add_inner_residual:
                for c in range(NCH):
                    edge(p, v, v.tensor_tensor(lnt[:, :], src[:, c, :], murep[:, :],
                                               ALU.subtract))
                    edge(p, v, v.tensor_tensor(lnt[:, :], lnt[:, :], rsrep[:, :],
                                               ALU.mult))
                    edge(p, v, v.tensor_tensor(src[:, c, :], src[:, c, :], lnt[:, :],
                                               ALU.add))




def emit_attn_a(p, x_sb):
    nc, t, h = p.nc, p.t, p.h

    if True:
        cq_g = nc.sbuf_tensor(_nm("a_cq"), [128, 4, S], BF16, side="right")
        ckv_g = nc.sbuf_tensor(_nm("a_ckv"), [128, 4, S], BF16, side="right")
        if True:

            with nc.sbuf_tensor(_nm("a_xn"), [128, NCH, S], BF16) as xn_bf:
                with nc.sbuf_tensor(_nm("a_mu1"), [128, S], F32) as mu1, \
                     nc.sbuf_tensor(_nm("a_rs1"), [128, S], F32) as rs1, \
                     nc.sbuf_tensor(_nm("a_mu2"), [128, S], F32) as mu2, \
                     nc.sbuf_tensor(_nm("a_rs2"), [128, S], F32) as rs2:
                    emit_stats(p, x_sb, mu1, rs1, LN_EPS, "x")
                    emit_stats(p, x_sb, mu2, rs2, LN_EPS, "ln",
                               mu1rep=mu1, rs1rep=rs1)
                    emit_ln_apply(p, x_sb, mu1, rs1, xn_bf, mu2rep=mu2,
                                  rs2rep=rs2, gamma=t["lns"], beta=t["lnb"],
                                  add_inner_residual=True)
                cq_bf = cq_g.__enter__()
                ckv_bf = ckv_g.__enter__()

                s_w, s_m, s_e = p.sem("wg"), p.sem("cqm"), p.sem("cqe")
                with nc.Block() as b, \
                     nc.psum_tensor(_nm("ps_cq"), [128, 2, 512], F32) as ps, \
                     nc.sbuf_tensor(_nm("cq_wa"), [128, 8, 512], BF16) as wdq, \
                     nc.sbuf_tensor(_nm("cq_wb"), [128, 8, 512], BF16) as wdkv:
                    b_w, b_m, b_e = s_w.val, s_m.val, s_e.val

                    @b.gpsimd
                    def _(g):
                        g = SerialEngine(p, g)
                        g.dma_start(wdq[:, :, :], h["a_wdqT"]).then_inc(s_w.h, 16)
                        g.dma_start(wdkv[:, :, :], h["a_wdkvT"]).then_inc(s_w.h, 16)
                        s_w.val += 32

                    @b.tensor
                    def _(tt):
                        ui = 0
                        for which, slab in ((0, wdq), (1, wdkv)):
                            tt.wait_ge(s_w.h, b_w + 32)
                            for lc in range(4):
                                for nb in range(NBK):
                                    if ui >= 2:
                                        tt.wait_ge(s_e.h, b_e + ui - 1)
                                    ins = None
                                    for c in range(NCH):
                                        ins = tt.matmul(
                                            ps[:, ui % 2, :],
                                            slab[:, c, lc * 128:(lc + 1) * 128],
                                            xn_bf[:, c, nb512(nb)],
                                            start=(c == 0), stop=(c == NCH - 1))
                                    s_m.inc(ins)
                                    ui += 1

                    @b.scalar
                    def _(sc):
                        sc = SerialEngine(p, sc)
                        ui = 0
                        for which, dst in ((0, cq_bf), (1, ckv_bf)):
                            for lc in range(4):
                                for nb in range(NBK):
                                    sc.wait_ge(s_m.h, b_m + ui + 1)
                                    s_e.inc(sc.activation(dst[:, lc, nb512(nb)],
                                                          ps[:, ui % 2, :],
                                                          AF.Identity))
                                    ui += 1

            with nc.sbuf_tensor(_nm("a_ao"), [128, NCH, S], BF16) as ao_bf, \
                 nc.sbuf_tensor(_nm("a_rc"), [128, S], BF16) as rcos, \
                 nc.sbuf_tensor(_nm("a_rs_t"), [128, S], BF16) as rsin:
                sg = p.sem("wg")
                with nc.Block() as b:
                    @b.gpsimd
                    def _(g):
                        g = SerialEngine(p, g)
                        g.dma_start(rcos[:, :], h["r_cos"]).then_inc(sg.h, 16)
                        g.dma_start(rsin[:, :], h["r_sin"]).then_inc(sg.h, 16)
                        sg.val += 32
                        g.wait_ge(sg.h, sg.val)

                for hp in range(8):
                    _attn_a_hp(p, hp, cq_bf, ckv_bf, ao_bf, (rcos, rsin))

                ckv_g.__exit__(None, None, None)
                cq_g.__exit__(None, None, None)
                _attn_out_residual(p, x_sb, ao_bf, h["a_woT"], None, None,
                                   xm_bf=None)


def _attn_a_hp(p, hp, cq_bf, ckv_bf, ao_bf, ropes):
    nc, t, h = p.nc, p.t, p.h
    rcos, rsin = ropes

    with nc.sbuf_tensor(_nm("a_qr"), [128, S], BF16) as qraw, \
         nc.sbuf_tensor(_nm("a_kr"), [128, S], BF16) as kraw, \
         nc.sbuf_tensor(_nm("a_qb"), [128, S], BF16) as q_bf, \
         nc.sbuf_tensor(_nm("a_kb"), [128, S], BF16) as k_bf, \
         nc.sbuf_tensor(_nm("a_vb"), [128, S], BF16) as v_bf, \
         nc.sbuf_tensor(_nm("a_vT"), [128, 16, 128], BF16) as vT_bf, \
         nc.sbuf_tensor(_nm("a_stg"), [32, 2 * S], BF16) as stg:

        s_w, s_m, s_e = p.sem("wg"), p.sem("qkm"), p.sem("qke")
        with nc.Block() as b, \
             nc.psum_tensor(_nm("ps_qkv"), [128, 2, 512], F32) as ps, \
             nc.sbuf_tensor(_nm("a_wup"), [128, 4, 384], BF16) as wup:
            b_w, b_m, b_e = s_w.val, s_m.val, s_e.val

            @b.gpsimd
            def _(g):
                g = SerialEngine(p, g)
                g.dma_start(wup[:, :, 0:128],
                            h["a_wupqT"][:, :, hp * 128:(hp + 1) * 128]).then_inc(s_w.h, 16)
                g.dma_start(wup[:, :, 128:256],
                            h["a_wupkT"][:, :, hp * 128:(hp + 1) * 128]).then_inc(s_w.h, 16)
                g.dma_start(wup[:, :, 256:384],
                            h["a_wupvT"][:, :, hp * 128:(hp + 1) * 128]).then_inc(s_w.h, 16)
                s_w.val += 48

            @b.tensor
            def _(tt):
                tt.wait_ge(s_w.h, b_w + 48)
                ui = 0
                for which in range(3):
                    srcb = cq_bf if which == 0 else ckv_bf
                    for nb in range(NBK):
                        if ui >= 2:
                            tt.wait_ge(s_e.h, b_e + ui - 1)
                        ins = None
                        for c in range(4):
                            ins = tt.matmul(ps[:, ui % 2, :],
                                            wup[:, c, which * 128:(which + 1) * 128],
                                            srcb[:, c, nb512(nb)],
                                            start=(c == 0), stop=(c == 3))
                        s_m.inc(ins)
                        ui += 1

            @b.scalar
            def _(sc):
                sc = SerialEngine(p, sc)
                ui = 0
                for which, dst in ((0, qraw), (1, kraw), (2, v_bf)):
                    for nb in range(NBK):
                        sc.wait_ge(s_m.h, b_m + ui + 1)
                        s_e.inc(sc.activation(dst[:, nb512(nb)], ps[:, ui % 2, :],
                                              AF.Identity,
                                              scale=0.125 if which == 0 else 1.0))
                        ui += 1

        s_vm, s_ve = p.sem("vtm"), p.sem("vte")
        with nc.Block() as b, nc.psum_tensor(_nm("ps_vT"), [128, 4, 1024], BF16) as psv:
            b_vm, b_ve = s_vm.val, s_ve.val

            @b.vector
            def _(v):
                v = SerialEngine(p, v)
                for srcq, dst in ((qraw, q_bf), (kraw, k_bf)):
                    for hh in range(2):
                        B = hh * 64
                        e = srcq[B:B + 32, :]
                        o = srcq[B + 32:B + 64, :]
                        v.tensor_tensor(stg[:, 0:S], e, rcos[B:B + 32, :], ALU.mult)
                        edge(p, v, v.tensor_tensor(stg[:, S:2 * S], o,
                                                   rsin[B + 32:B + 64, :], ALU.mult))
                        edge(p, v, v.tensor_tensor(dst[B:B + 32, :], stg[:, 0:S],
                                                   stg[:, S:2 * S], ALU.subtract))
                        v.tensor_tensor(stg[:, 0:S], e, rsin[B:B + 32, :], ALU.mult)
                        edge(p, v, v.tensor_tensor(stg[:, S:2 * S], o,
                                                   rcos[B + 32:B + 64, :], ALU.mult))
                        edge(p, v, v.tensor_tensor(dst[B + 32:B + 64, :], stg[:, 0:S],
                                                   stg[:, S:2 * S], ALU.add))

            @b.tensor
            def _(tt):
                k = 0
                for sc_ in range(16):
                    for hh in range(2):
                        if k >= 4:
                            tt.wait_ge(s_ve.h, b_ve + k - 3)
                        s_vm.inc(tt.transpose(psv[:, k % 4, 0:64],
                                              v_bf[hh * 64:(hh + 1) * 64,
                                                   sc_ * 128:(sc_ + 1) * 128],
                                              t["id64"][hh * 64:(hh + 1) * 64, :]))
                        k += 1

            @b.scalar
            def _(sc):
                sc = SerialEngine(p, sc)
                k = 0
                for sc_ in range(16):
                    for hh in range(2):
                        sc.wait_ge(s_vm.h, b_vm + k + 1)
                        s_ve.inc(sc.activation(vT_bf[:, sc_, hh * 64:(hh + 1) * 64],
                                               psv[:, k % 4, 0:64], AF.Identity))
                        k += 1

        _attn_inner(p, hp, q_bf, k_bf, vT_bf, ao_bf, masked=None)


def _attn_inner(p, hp, q_bf, k_bf, vT_bf, ao_bf, masked=None):
    nc, t = p.nc, p.t
    asc, atr, aav = p.sem("asc"), p.sem("atr"), p.sem("aav")
    amx, apb, aao = p.sem("amx"), p.sem("apb"), p.sem("aao")
    aex, aat, arc = p.sem("aex"), p.sem("aat"), p.sem("arc")
    sy_m = [p.sem("wy0"), p.sem("wy1")]

    with nc.Block() as b, \
         nc.psum_tensor(_nm("ps_sc"), [128, 4, 512], F32) as ps_sc, \
         nc.psum_tensor(_nm("ps_tr"), [128, 16, 128], BF16) as ps_tr, \
         nc.psum_tensor(_nm("ps_av"), [128, 2, 512], F32) as ps_av, \
         nc.sbuf_tensor(_nm("at_exp"), [128, S], BF16) as exp_f, \
         nc.sbuf_tensor(_nm("at_pb"), [128, S], BF16) as probs, \
         nc.sbuf_tensor(_nm("at_aT"), [128, 16, 128], BF16) as attnT, \
         nc.sbuf_tensor(_nm("at_red"), [128, 6], F32) as red:
        scm_g = nc.sbuf_tensor(_nm("at_scm"), [128, S], F32) \
            if masked is not None else None
        scm = scm_g.__enter__() if scm_g is not None else None

        B_sc, B_tr, B_av = asc.val, atr.val, aav.val
        B_mx, B_pb, B_ao = amx.val, apb.val, aao.val
        B_ex, B_at, B_rc = aex.val, aat.val, arc.val
        B_sy = [sy_m[0].val, sy_m[1].val]
        niter = 32
        ps_flat = ps_sc.rearrange("p b n -> p (b n)")

        if masked is not None:
            @b.sync
            def _(sy):
                for qt in range(16):
                    if qt >= 2:
                        sy.wait_ge(amx.h, B_mx + 2 * (qt - 1))
                    masked["load"](sy, qt, qt % 2)

        @b.tensor
        def _(tt):
            def tr_av(j):
                qt, hh = divmod(j, 2)
                Bh = hh * 64
                tt.wait_ge(apb.h, B_pb + j + 1)
                if j >= 1:
                    tt.wait_ge(aat.h, B_at + j)
                ins_t = None
                for sc_ in range(16):
                    ins_t = tt.transpose(ps_tr[:, sc_, :],
                                         probs[:, sc_ * 128:(sc_ + 1) * 128],
                                         t["ident"][:, :])
                atr.inc(ins_t)
                tt.wait_ge(aat.h, B_at + j + 1)
                if j >= 2:
                    tt.wait_ge(aao.h, B_ao + j - 1)
                ins = None
                for sc_ in range(16):
                    ins = tt.matmul(ps_av[0:64, j % 2, 0:128], vT_bf[:, sc_, Bh:Bh + 64],
                                    attnT[:, sc_, :], start=(sc_ == 0), stop=(sc_ == 15))
                aav.inc(ins)

            for i in range(niter):
                qt, hh = divmod(i, 2)
                Bh = hh * 64
                if i >= 1:
                    tt.wait_ge(aex.h, B_ex + i)
                ins = None
                for nb in range(NBK):
                    ins = tt.matmul(ps_sc[:, nb, :],
                                    q_bf[Bh:Bh + 64, qt * 128:(qt + 1) * 128],
                                    k_bf[Bh:Bh + 64, nb512(nb)],
                                    start=True, stop=True)
                asc.inc(ins)
                if i >= 1:
                    tr_av(i - 1)
            tr_av(niter - 1)

        @b.vector
        def _(v):
            v = SerialEngine(p, v)
            def ao_evac(j):
                qt, hh = divmod(j, 2)
                Bh = hh * 64
                v.wait_ge(aav.h, B_av + j + 1)
                aao.inc(v.tensor_copy(ao_bf[Bh:Bh + 64, hp, qt * 128:(qt + 1) * 128],
                                      ps_av[0:64, j % 2, 0:128]))

            for i in range(niter):
                qt, hh = divmod(i, 2)
                par = i % 2
                na = red[:, par * 3 + 0:par * 3 + 1]
                sm = red[:, par * 3 + 1:par * 3 + 2]
                rc = red[:, par * 3 + 2:par * 3 + 3]
                v.wait_ge(asc.h, B_sc + i + 1)
                if masked is not None:
                    if hh == 0:
                        v.wait_ge(sy_m[qt % 2].h, B_sy[qt % 2] + 16 * (qt // 2 + 1))
                    edge(p, v, v.tensor_tensor(scm[:, :], ps_flat,
                                               masked["mask_bf"][qt % 2][:, :],
                                               ALU.add))
                    amx.inc(v.reduce_max(na, scm[:, :], axis=AX.X, negate=True))
                else:
                    amx.inc(v.reduce_max(na, ps_flat, axis=AX.X, negate=True))
                if i >= 1:
                    ao_evac(i - 1)
                v.wait_ge(aex.h, B_ex + i + 1)
                arc.inc(v.reciprocal(rc, sm))
                v.wait_ge(arc.h, arc.val)
                if i >= 1:
                    v.wait_ge(atr.h, B_tr + i)
                apb.inc(v.tensor_single_scalar(probs[:, :], exp_f[:, :], rc, ALU.mult))
            ao_evac(niter - 1)

        @b.scalar
        def _(sc):
            sc = SerialEngine(p, sc)
            for i in range(niter):
                par = i % 2
                na = red[:, par * 3 + 0:par * 3 + 1]
                sm = red[:, par * 3 + 1:par * 3 + 2]
                sc.wait_ge(amx.h, B_mx + i + 1)
                src = scm[:, :] if masked is not None else ps_flat
                aex.inc(sc.activation(exp_f[:, :], src, AF.Exp, bias=na, scale=1.0,
                                      accum_out=sm))
                sc.wait_ge(atr.h, B_tr + i + 1)
                aat.inc(sc.activation(attnT.rearrange("p c n -> p (c n)"),
                                      ps_tr.rearrange("p c n -> p (c n)"),
                                      AF.Identity))
        if scm_g is not None:
            scm_g.__exit__(None, None, None)


def _attn_out_residual(p, x_sb, ao_bf, woT_hbm, mu1, rs1, xm_bf=None):
    nc = p.nc
    s_w, s_m, s_e = p.sem("wg"), p.sem("wom"), p.sem("woe")
    with nc.Block() as b, \
         nc.psum_tensor(_nm("ps_wo"), [128, 2, 512], F32) as ps, \
         nc.sbuf_tensor(_nm("wo_wa"), [128, 8, 512], BF16) as wa, \
         nc.sbuf_tensor(_nm("wo_wb"), [128, 8, 512], BF16) as wb, \
         nc.sbuf_tensor(_nm("wo_t"), [128, 512], F32) as wt:
        b_w, b_m, b_e = s_w.val, s_m.val, s_e.val
        slabs = [wa, wb]

        @b.gpsimd
        def _(g):
            g = SerialEngine(p, g)
            for si in range(2):
                g.dma_start(slabs[si][:, :, :],
                            woT_hbm[:, :, si * 512:(si + 1) * 512]).then_inc(s_w.h, 16)
            s_w.val += 32

        @b.tensor
        def _(tt):
            ui = 0
            for si in range(2):
                tt.wait_ge(s_w.h, b_w + 32)
                for cc in range(4):
                    for nb in range(NBK):
                        if ui >= 2:
                            tt.wait_ge(s_e.h, b_e + ui - 1)
                        ins = None
                        for c in range(NCH):
                            ins = tt.matmul(ps[:, ui % 2, :],
                                            slabs[si][:, c, cc * 128:(cc + 1) * 128],
                                            ao_bf[:, c, nb512(nb)],
                                            start=(c == 0), stop=(c == NCH - 1))
                        s_m.inc(ins)
                        ui += 1

        @b.vector
        def _(v):
            v = SerialEngine(p, v)
            ui = 0
            for si in range(2):
                for cc in range(4):
                    fo = si * 4 + cc
                    for nb in range(NBK):
                        v.wait_ge(s_m.h, b_m + ui + 1)
                        xs = x_sb[:, fo, nb512(nb)]
                        if mu1 is not None:
                            edge(p, v, v.tensor_tensor(wt[:, :], xs,
                                                       mu1[:, nb512(nb)],
                                                       ALU.subtract))
                            edge(p, v, v.tensor_tensor(wt[:, :], wt[:, :],
                                                       rs1[:, nb512(nb)], ALU.mult))
                            edge(p, v, v.tensor_tensor(wt[:, :], wt[:, :],
                                                       ps[:, ui % 2, :], ALU.add))
                            if xm_bf is not None:
                                edge(p, v, v.tensor_copy(xm_bf[:, fo, nb512(nb)],
                                                         wt[:, :]))
                            s_e.inc(v.tensor_tensor(xs, xs, wt[:, :], ALU.add))
                        else:
                            s_e.inc(v.tensor_tensor(xs, xs, ps[:, ui % 2, :],
                                                    ALU.add))
                        ui += 1



def emit_gating(p, xn_bf, wgT_sb, bias_rep, nexp, wde_sb, wdtm_sb=None):
    nc, t = p.nc, p.t
    ggm, ggt = p.sem("ggm"), p.sem("ggt")
    glg, gg1, gg2, gg3, gg4 = (p.sem(x) for x in ("glg", "gg1", "gg2", "gg3", "gg4"))
    gwd, gge, gex = p.sem("gwd"), p.sem("gge"), p.sem("gex")

    with nc.Block() as b, \
         nc.psum_tensor(_nm("ps_g"), [128, 2, 512], F32) as ps, \
         nc.psum_tensor(_nm("ps_gt"), [16, 2, 512], F32) as pst, \
         nc.sbuf_tensor(_nm("g_lg"), [128, 16], F32) as lg, \
         nc.sbuf_tensor(_nm("g_l2"), [128, 16], F32) as l2, \
         nc.sbuf_tensor(_nm("g_eq"), [128, 32], F32) as eq, \
         nc.sbuf_tensor(_nm("g_red"), [128, 8], F32) as red, \
         nc.sbuf_tensor(_nm("g_wd"), [128, 16], F32) as wd:
        B_gm, B_gt, B_lg = ggm.val, ggt.val, glg.val
        B_ex, B_wd, B_ge = gex.val, gwd.val, gge.val
        B_g2 = gg2.val

        @b.tensor
        def _(tt):
            for i in range(16):
                if i >= 2:
                    tt.wait_ge(glg.h, B_lg + i - 1)
                ins = None
                for c in range(NCH):
                    ins = tt.matmul(ps[:, i % 2, 0:nexp],
                                    xn_bf[:, c, i * 128:(i + 1) * 128],
                                    wgT_sb[:, c, 0:nexp],
                                    start=(c == 0), stop=(c == NCH - 1))
                ggm.inc(ins)
                tt.wait_ge(gwd.h, B_wd + i + 1)
                if i >= 2:
                    tt.wait_ge(gge.h, B_ge + i - 1)
                ggt.inc(tt.transpose(pst[0:nexp, i % 2, 0:128], wd[:, 0:nexp],
                                     t["identf"][:, :]))

        @b.vector
        def _(v):
            v = SerialEngine(p, v)
            for i in range(16):
                m1, nm1 = red[:, 0:1], red[:, 1:2]
                m2, bex = red[:, 2:3], red[:, 3:4]
                den, rden, br = red[:, 4:5], red[:, 5:6], red[:, 6:7]
                v.wait_ge(ggm.h, B_gm + i + 1)
                if bias_rep is not None:
                    glg.inc(v.tensor_tensor(lg[:, 0:nexp], ps[:, i % 2, 0:nexp],
                                            bias_rep[:, 0:nexp], ALU.add))
                else:
                    glg.inc(v.tensor_copy(lg[:, 0:nexp], ps[:, i % 2, 0:nexp]))
                edge(p, v, v.tensor_reduce(m1, lg[:, 0:nexp], axis=AX.X,
                                           op=ALU.max))
                gg1.inc(v.tensor_scalar_mul(nm1, m1, -1.0))
                v.wait_ge(gg1.h, gg1.val)
                edge(p, v, v.tensor_single_scalar(eq[:, 0:nexp], lg[:, 0:nexp], m1,
                                                  ALU.is_ge))
                edge(p, v, v.scalar_tensor_tensor(l2[:, 0:nexp], eq[:, 0:nexp],
                                                  -1e30, lg[:, 0:nexp],
                                                  ALU.mult, ALU.add))
                gg2.inc(v.tensor_reduce(m2, l2[:, 0:nexp], axis=AX.X, op=ALU.max))
                v.wait_ge(gex.h, B_ex + i + 1)
                gg3.inc(v.tensor_scalar_add(den, bex, 1.0))
                v.wait_ge(gg3.h, gg3.val)
                gg3.inc(v.reciprocal(rden, den))
                v.wait_ge(gg3.h, gg3.val)
                gg4.inc(v.tensor_tensor(br, bex, rden, ALU.mult))
                v.wait_ge(gg4.h, gg4.val)
                edge(p, v, v.tensor_single_scalar(eq[:, 16:16 + nexp], l2[:, 0:nexp],
                                                  m2, ALU.is_ge))
                edge(p, v, v.tensor_single_scalar(eq[:, 16:16 + nexp],
                                                  eq[:, 16:16 + nexp], br, ALU.mult))
                edge(p, v, v.tensor_single_scalar(wd[:, 0:nexp], eq[:, 0:nexp],
                                                  rden, ALU.mult))
                gwd.inc(v.tensor_tensor(wd[:, 0:nexp], wd[:, 0:nexp],
                                        eq[:, 16:16 + nexp], ALU.add))
                if wdtm_sb is not None:
                    v.tensor_copy(wdtm_sb[:, i, 0:nexp], wd[:, 0:nexp])
                v.wait_ge(ggt.h, B_gt + i + 1)
                gge.inc(v.tensor_copy(wde_sb[0:nexp, i, :], pst[0:nexp, i % 2, 0:128]))

        @b.scalar
        def _(sc):
            sc = SerialEngine(p, sc)
            for i in range(16):
                nm1, m2, bex = red[:, 1:2], red[:, 2:3], red[:, 3:4]
                sc.wait_ge(gg2.h, B_g2 + i + 1)
                gex.inc(sc.activation(bex, m2, AF.Exp, bias=nm1, scale=1.0))


def emit_moe(p, x_sb, xn_bf, kind):
    nc, t = p.nc, p.t
    nexp = 8 if kind == "ffn" else 4
    sel = t["sel8"] if kind == "ffn" else t["sel4"]

    with nc.sbuf_tensor(_nm("m_wde"), [8, 16, 128], F32) as wde, \
         nc.sbuf_tensor(_nm("m_wrep"), [128, S], F32) as wrep:

        if kind == "ffn":
            emit_gating(p, xn_bf, t["fwgT"], t["fbg"], 8, wde)
        else:
            emit_gating(p, xn_bf, t["bgw"], None, 4, wde)

        wde_flat = wde.rearrange("e q n -> e (q n)")
        for e in range(nexp):
            s_rw = p.sem("mrw")
            with nc.Block() as b, nc.psum_tensor(_nm("ps_wr"), [128, S], F32) as psr:
                b_rw = s_rw.val

                @b.tensor
                def _(tt):
                    ins = None
                    for nb in range(NBK):
                        ins = tt.matmul(psr[:, nb512(nb)],
                                        sel[0:nexp, e * 128:(e + 1) * 128],
                                        wde_flat[0:nexp, nb512(nb)],
                                        start=True, stop=True)
                    s_rw.inc(ins)

                @b.vector
                def _(v):
                    v = SerialEngine(p, v)
                    v.wait_ge(s_rw.h, b_rw + 1)
                    v.tensor_copy(wrep[:, :], psr[:, :])

            _moe_expert(p, x_sb, xn_bf, wrep, e, kind)


def _moe_expert(p, x_sb, xn_bf, wrep, e, kind):
    nc, t, h = p.nc, p.t, p.h
    s_w = p.sem("wg")
    s_g = [p.sem("wga0"), p.sem("wga1")]
    s_u = [p.sem("wgu0"), p.sem("wgu1")]
    s_d = [p.sem("wgd0"), p.sem("wgd1")]
    fgm, fum, fsl, fh = p.sem("fgm"), p.sem("fum"), p.sem("fsl"), p.sem("fh")
    fdm, fde, fds = p.sem("fdm"), p.sem("fde"), p.sem("fds")

    is_ffn = kind == "ffn"
    guards = [nc.sbuf_tensor(_nm("m_h"), [128, 32, 512], BF16),
              nc.sbuf_tensor(_nm("m_wga"), [128, 8, 512], BF16),
              nc.sbuf_tensor(_nm("m_wgb"), [128, 8, 512], BF16),
              nc.sbuf_tensor(_nm("m_wd0"), [128, 4, 1024], BF16),
              nc.sbuf_tensor(_nm("m_wd1"), [128, 4, 1024], BF16),
              nc.sbuf_tensor(_nm("m_ds"), [128, 512], F32)]
    if is_ffn:
        guards += [nc.sbuf_tensor(_nm("m_wua"), [128, 8, 512], BF16),
                   nc.sbuf_tensor(_nm("m_wub"), [128, 8, 512], BF16),
                   nc.sbuf_tensor(_nm("m_sg"), [128, 4, 512], BF16)]
    objs = [g_.__enter__() for g_ in guards]
    h_bf, wga, wgb, wd0, wd1, dscr = objs[:6]
    if is_ffn:
        wua, wub, sg = objs[6:]
    else:
        wua = wub = sg = None
    if True:
        nperhb = 8 if is_ffn else 4

        for tb in range(4):
            tok = slice(tb * 512, (tb + 1) * 512)

            with nc.Block() as b, \
                 nc.psum_tensor(_nm("ps_mg"), [128, 4, 512], F32) as psg, \
                 nc.psum_tensor(_nm("ps_mu"), [128, 4, 512], F32) as psu:
                b_gm, b_um = fgm.val, fum.val
                b_sl, b_h = fsl.val, fh.val
                b_g = [s_g[0].val, s_g[1].val]
                b_u = [s_u[0].val, s_u[1].val]

                @b.gpsimd
                def _(g):
                    g = SerialEngine(p, g)
                    for hb in range(8):
                        ga = wga if hb % 2 == 0 else wgb
                        ua = wua if hb % 2 == 0 else wub
                        if hb >= 2:
                            g.wait_ge(fgm.h, b_gm + (hb - 1) * 4)
                            if is_ffn:
                                g.wait_ge(fum.h, b_um + (hb - 1) * 4)
                        src = h["f_WgT"] if is_ffn else h["b_w1"]
                        sem = s_g[hb % 2]
                        g.dma_start(ga[:, :, :],
                                    src[e][:, :, hb * 512:(hb + 1) * 512]
                                    ).then_inc(sem.h, 16)
                        sem.val += 16
                        if is_ffn:
                            sem = s_u[hb % 2]
                            g.dma_start(ua[:, :, :],
                                        h["f_WuT"][e][:, :, hb * 512:(hb + 1) * 512]
                                        ).then_inc(sem.h, 16)
                            sem.val += 16

                nload = 2 if is_ffn else 1

                @b.tensor
                def _(tt):
                    for hb in range(8):
                        tt.wait_ge(s_g[hb % 2].h, b_g[hb % 2] + 16 * (hb // 2 + 1))
                        if is_ffn:
                            tt.wait_ge(s_u[hb % 2].h, b_u[hb % 2] + 16 * (hb // 2 + 1))
                        ga = wga if hb % 2 == 0 else wgb
                        ua = wua if hb % 2 == 0 else wub
                        for mc in range(4):
                            if hb >= 1:
                                tt.wait_ge(fsl.h, b_sl + (hb - 1) * 4 + mc + 1)
                            ins = None
                            for c in range(NCH):
                                ins = tt.matmul(psg[:, mc, :],
                                                ga[:, c, mc * 128:(mc + 1) * 128],
                                                xn_bf[:, c, tok],
                                                start=(c == 0), stop=(c == NCH - 1))
                            fgm.inc(ins)
                        if is_ffn:
                            for mc in range(4):
                                if hb >= 1:
                                    tt.wait_ge(fh.h, b_h + (hb - 1) * 4 + mc + 1)
                                ins = None
                                for c in range(NCH):
                                    ins = tt.matmul(psu[:, mc, :],
                                                    ua[:, c, mc * 128:(mc + 1) * 128],
                                                    xn_bf[:, c, tok],
                                                    start=(c == 0), stop=(c == NCH - 1))
                                fum.inc(ins)

                @b.scalar
                def _(sc):
                    sc = SerialEngine(p, sc)
                    for hb in range(8):
                        for mc in range(4):
                            sc.wait_ge(fgm.h, b_gm + hb * 4 + mc + 1)
                            if is_ffn:
                                if hb >= 1:
                                    sc.wait_ge(fh.h, b_h + (hb - 1) * 4 + mc + 1)
                                fsl.inc(sc.activation(sg[:, mc, :], psg[:, mc, :],
                                                      AF.Silu))
                            else:
                                fsl.inc(sc.activation(
                                    h_bf[:, hb * 4 + mc, :], psg[:, mc, :],
                                    AF.Gelu_apprx_tanh,
                                    bias=t["bb1"][:, e, hb * 4 + mc:hb * 4 + mc + 1],
                                    scale=1.0))

                if is_ffn:
                    @b.vector
                    def _(v):
                        v = SerialEngine(p, v)
                        for hb in range(8):
                            for mc in range(4):
                                v.wait_ge(fsl.h, b_sl + hb * 4 + mc + 1)
                                v.wait_ge(fum.h, b_um + hb * 4 + mc + 1)
                                fh.inc(v.tensor_tensor(h_bf[:, hb * 4 + mc, :],
                                                       sg[:, mc, :], psu[:, mc, :],
                                                       ALU.mult))

            with nc.Block() as b, nc.psum_tensor(_nm("ps_md"), [128, 8, 512], F32) as psd:
                b_dm, b_ds = fdm.val, fds.val
                b_d = [s_d[0].val, s_d[1].val]
                wds = [wd0, wd1]
                hkey = "f_WdT" if is_ffn else "b_w2"

                @b.gpsimd
                def _(g):
                    g = SerialEngine(p, g)
                    for sl in range(8):
                        if sl >= 2:
                            g.wait_ge(fds.h, b_ds + sl - 1)
                        sem = s_d[sl % 2]
                        g.dma_start(wds[sl % 2][:, :, :],
                                    h[hkey][e][:, sl * 4:(sl + 1) * 4, :]
                                    ).then_inc(sem.h, 16)
                        sem.val += 16

                @b.tensor
                def _(tt):
                    if _T3_MODE == "g":
                        return
                    for sl in range(8):
                        tt.wait_ge(s_d[sl % 2].h, b_d[sl % 2] + 16 * (sl // 2 + 1))
                        ins = None
                        for fo in range(8):
                            for c in range(4):
                                ins = tt.matmul(psd[:, fo, :],
                                                wds[sl % 2][:, c, fo * 128:(fo + 1) * 128],
                                                h_bf[:, sl * 4 + c, :],
                                                start=(sl == 0 and c == 0),
                                                stop=(sl == 7 and c == 3))
                            if sl == 7:
                                fdm.inc(ins)
                        if sl < 7:
                            fds.inc(ins)

                @b.vector
                def _(v):
                    v = SerialEngine(p, v)
                    for fo in range(8):
                        v.wait_ge(fdm.h, b_dm + fo + 1)
                        xs = x_sb[:, fo, tok]
                        if is_ffn:
                            edge(p, v, v.tensor_tensor(dscr[:, :], psd[:, fo, :],
                                                       wrep[:, tok], ALU.mult))
                        else:
                            edge(p, v, v.scalar_tensor_tensor(
                                dscr[:, :], psd[:, fo, :],
                                t["bb2"][:, e, fo:fo + 1],
                                wrep[:, tok], ALU.add, ALU.mult))
                        fde.inc(v.tensor_tensor(xs, xs, dscr[:, :], ALU.add))
    for g_ in reversed(guards):
        g_.__exit__(None, None, None)



def emit_attn_b(p, x_sb):
    nc, t, h = p.nc, p.t, p.h

    with nc.sbuf_tensor(_nm("b_xn"), [128, NCH, S], BF16) as xn_bf:

        with nc.sbuf_tensor(_nm("b_mu1"), [128, S], F32) as mu1, \
             nc.sbuf_tensor(_nm("b_rs1"), [128, S], F32) as rs1:
            with nc.sbuf_tensor(_nm("b_rs2"), [128, S], F32) as rs2:
                emit_stats(p, x_sb, mu1, rs1, LN_EPS, "x")
                emit_stats(p, x_sb, None, rs2, RMS_EPS, "lnrms",
                           mu1rep=mu1, rs1rep=rs1)
                emit_ln_apply(p, x_sb, mu1, rs1, xn_bf, rs2rep=rs2, gamma=t["bn1"])

            _attn_b_masks(p, xn_bf)

            with nc.sbuf_tensor(_nm("b_ao"), [128, NCH, S], BF16) as ao_bf, \
                 nc.sbuf_tensor(_nm("b_mk0"), [128, S], BF16) as mk0, \
                 nc.sbuf_tensor(_nm("b_mk1"), [128, S], BF16) as mk1:
                s_wy = [p.sem("wy0"), p.sem("wy1")]

                def load_mask(sy, qt, buf):
                    dst = mk0 if buf == 0 else mk1
                    sem = s_wy[buf]
                    sy.dma_start(dst[:, :], p.h["mask_scr"][qt]).then_inc(sem.h, 16)
                    sem.val += 16

                masked = {"mask_bf": [mk0, mk1], "load": load_mask}

                for hp in range(8):
                    _attn_b_hp(p, hp, xn_bf, ao_bf, masked)

                _attn_out_residual(p, x_sb, ao_bf, h["b_wo"], mu1, rs1, xm_bf=xn_bf)

        with nc.sbuf_tensor(_nm("b_rs3"), [128, S], F32) as rs3:
            emit_stats(p, xn_bf, None, rs3, RMS_EPS, "rms")
            emit_ln_apply(p, xn_bf, None, rs3, xn_bf, gamma=t["bn2"])
        emit_moe(p, x_sb, xn_bf, "moe4")


def _attn_b_hp(p, hp, xn_bf, ao_bf, masked):
    nc, t, h = p.nc, p.t, p.h
    s_w, s_m, s_e = p.sem("wg"), p.sem("qkm"), p.sem("qke")

    with nc.sbuf_tensor(_nm("b_qb"), [128, S], BF16) as q_bf, \
         nc.sbuf_tensor(_nm("b_kb"), [128, S], BF16) as k_bf, \
         nc.sbuf_tensor(_nm("b_vb"), [128, S], BF16) as v_bf, \
         nc.sbuf_tensor(_nm("b_vT"), [128, 16, 128], BF16) as vT_bf:

        with nc.Block() as b, \
             nc.psum_tensor(_nm("ps_qkv"), [128, 2, 512], F32) as ps, \
             nc.sbuf_tensor(_nm("b_wqkv"), [128, 8, 384], BF16) as wqkv:
            b_w, b_m, b_e = s_w.val, s_m.val, s_e.val

            @b.gpsimd
            def _(g):
                g = SerialEngine(p, g)
                g.dma_start(wqkv[:, :, 0:128],
                            h["b_wq"][:, :, hp * 128:(hp + 1) * 128]).then_inc(s_w.h, 16)
                g.dma_start(wqkv[:, :, 128:256],
                            h["b_wk"][:, :, hp * 128:(hp + 1) * 128]).then_inc(s_w.h, 16)
                g.dma_start(wqkv[:, :, 256:384],
                            h["b_wv"][:, :, hp * 128:(hp + 1) * 128]).then_inc(s_w.h, 16)
                s_w.val += 48

            @b.tensor
            def _(tt):
                tt.wait_ge(s_w.h, b_w + 48)
                ui = 0
                for which in range(3):
                    for nb in range(NBK):
                        if ui >= 2:
                            tt.wait_ge(s_e.h, b_e + ui - 1)
                        ins = None
                        for c in range(NCH):
                            ins = tt.matmul(ps[:, ui % 2, :],
                                            wqkv[:, c, which * 128:(which + 1) * 128],
                                            xn_bf[:, c, nb512(nb)],
                                            start=(c == 0), stop=(c == NCH - 1))
                        s_m.inc(ins)
                        ui += 1

            @b.scalar
            def _(sc):
                sc = SerialEngine(p, sc)
                ui = 0
                for which, dst in ((0, q_bf), (1, k_bf), (2, v_bf)):
                    for nb in range(NBK):
                        sc.wait_ge(s_m.h, b_m + ui + 1)
                        s_e.inc(sc.activation(dst[:, nb512(nb)], ps[:, ui % 2, :],
                                              AF.Identity,
                                              scale=0.125 if which == 0 else 1.0))
                        ui += 1

        s_vm, s_ve = p.sem("vtm"), p.sem("vte")
        with nc.Block() as b, nc.psum_tensor(_nm("ps_vT"), [128, 4, 1024], BF16) as psv:
            b_vm, b_ve = s_vm.val, s_ve.val

            @b.tensor
            def _(tt):
                k = 0
                for sc_ in range(16):
                    for hh in range(2):
                        if k >= 4:
                            tt.wait_ge(s_ve.h, b_ve + k - 3)
                        s_vm.inc(tt.transpose(psv[:, k % 4, 0:64],
                                              v_bf[hh * 64:(hh + 1) * 64,
                                                   sc_ * 128:(sc_ + 1) * 128],
                                              t["id64"][hh * 64:(hh + 1) * 64, :]))
                        k += 1

            @b.scalar
            def _(sc):
                sc = SerialEngine(p, sc)
                k = 0
                for sc_ in range(16):
                    for hh in range(2):
                        sc.wait_ge(s_vm.h, b_vm + k + 1)
                        s_ve.inc(sc.activation(vT_bf[:, sc_, hh * 64:(hh + 1) * 64],
                                               psv[:, k % 4, 0:64], AF.Identity))
                        k += 1

        _attn_inner(p, hp, q_bf, k_bf, vT_bf, ao_bf, masked=masked)


def _attn_b_masks(p, xn_bf):
    nc, t, h = p.nc, p.t, p.h
    s_bm, s_br, s_bs = p.sem("bim"), p.sem("bir"), p.sem("bis")
    s_ba, s_bg, s_bn = p.sem("bia"), p.sem("big"), p.sem("bin")
    s_tr, s_ct, s_lo = p.sem("btr"), p.sem("bct"), p.sem("blo")
    s_mk, s_so = p.sem("bmk"), p.sem("so")
    s_w, s_pm, s_pe = p.sem("wg"), p.sem("bpm"), p.sem("bpe")

    with nc.sbuf_tensor(_nm("b_iq"), [64, 4, S], BF16) as iq_bf, \
         nc.sbuf_tensor(_nm("b_ik"), [64, S], BF16) as ik_bf, \
         nc.sbuf_tensor(_nm("b_iwt"), [128, 16, 4], F32) as iw_tm, \
         nc.sbuf_tensor(_nm("b_iota"), [128, S], F32) as iotaEta:
        with nc.Block() as b0:
            @b0.gpsimd
            def _(g):
                g = SerialEngine(p, g)
                gp = p.sem("gp")
                gp.inc(g.iota(iotaEta[:, :], [[1, S]], base=1, channel_multiplier=0,
                              allow_small_or_imprecise_dtypes=True))
                g.wait_ge(gp.h, gp.val)
                g.tensor_scalar_mul(iotaEta[:, :], iotaEta[:, :], float(ETA))

        with nc.Block() as b, \
             nc.psum_tensor(_nm("ps_iq"), [64, 2, 512], F32) as psq, \
             nc.psum_tensor(_nm("ps_iw"), [128, 2, 512], F32) as psw, \
             nc.sbuf_tensor(_nm("b_iwq"), [128, 8, 256], BF16) as wiq, \
             nc.sbuf_tensor(_nm("b_iwk"), [128, 8, 64], BF16) as wik:
            b_w, b_pm, b_pe = s_w.val, s_pm.val, s_pe.val

            @b.gpsimd
            def _(g):
                g = SerialEngine(p, g)
                g.dma_start(wiq[:, :, :], h["b_iwq"]).then_inc(s_w.h, 16)
                g.dma_start(wik[:, :, :], h["b_iwk"]).then_inc(s_w.h, 16)
                s_w.val += 32

            @b.tensor
            def _(tt):
                tt.wait_ge(s_w.h, b_w + 32)
                ui = 0
                for hi in range(4):
                    for nb in range(NBK):
                        if ui >= 2:
                            tt.wait_ge(s_pe.h, b_pe + ui - 1)
                        ins = None
                        for c in range(NCH):
                            ins = tt.matmul(psq[:, ui % 2, :],
                                            wiq[:, c, hi * 64:(hi + 1) * 64],
                                            xn_bf[:, c, nb512(nb)],
                                            start=(c == 0), stop=(c == NCH - 1))
                        s_pm.inc(ins)
                        ui += 1
                for nb in range(NBK):
                    if ui >= 2:
                        tt.wait_ge(s_pe.h, b_pe + ui - 1)
                    ins = None
                    for c in range(NCH):
                        ins = tt.matmul(psq[:, ui % 2, :], wik[:, c, :],
                                        xn_bf[:, c, nb512(nb)],
                                        start=(c == 0), stop=(c == NCH - 1))
                    s_pm.inc(ins)
                    ui += 1
                for i in range(16):
                    if i >= 2:
                        tt.wait_ge(s_pe.h, b_pe + ui - 1)
                    ins = None
                    for c in range(NCH):
                        ins = tt.matmul(psw[:, i % 2, 0:4],
                                        xn_bf[:, c, i * 128:(i + 1) * 128],
                                        t["biww"][:, c, :],
                                        start=(c == 0), stop=(c == NCH - 1))
                    s_pm.inc(ins)
                    ui += 1

            @b.scalar
            def _(sc):
                sc = SerialEngine(p, sc)
                ui = 0
                for hi in range(4):
                    for nb in range(NBK):
                        sc.wait_ge(s_pm.h, b_pm + ui + 1)
                        s_pe.inc(sc.activation(iq_bf[:, hi, nb512(nb)],
                                               psq[:, ui % 2, :], AF.Identity))
                        ui += 1
                for nb in range(NBK):
                    sc.wait_ge(s_pm.h, b_pm + ui + 1)
                    s_pe.inc(sc.activation(ik_bf[:, nb512(nb)], psq[:, ui % 2, :],
                                           AF.Identity))
                    ui += 1
                for i in range(16):
                    sc.wait_ge(s_pm.h, b_pm + ui + 1)
                    s_pe.inc(sc.activation(iw_tm[:, i, :], psw[:, i % 2, 0:4],
                                           AF.Identity))
                    ui += 1

        with nc.Block() as b, nc.sbuf_tensor(_nm("b_mtmp"), [128, S], BF16) as mtmp:
            @b.gpsimd
            def _(g):
                g = SerialEngine(p, g)
                gp = p.sem("gp")
                gso = p.sem("gso")
                for qt in range(4):
                    gp.inc(g.memset(mtmp[:, :], 0.0))
                    g.wait_ge(gp.h, gp.val)
                    gp.inc(g.affine_select(mtmp[:, :], mtmp[:, :], [[-1, S]],
                                           ALU.is_ge, t["r_neg"], base=qt * 128,
                                           channel_multiplier=1))
                    g.wait_ge(gp.h, gp.val)
                    g.dma_start(h["mask_scr"][qt], mtmp[:, :]).then_inc(gso.h, 16)
                    gso.val += 16
                    g.wait_ge(gso.h, gso.val)

        for qp in range(6):
            qts = (4 + 2 * qp, 5 + 2 * qp)
            _bisect_pair(p, qts, iq_bf, ik_bf, iw_tm, iotaEta)


def _bisect_pair(p, qts, iq_bf, ik_bf, iw_tm, iotaEta):
    nc, t, h = p.nc, p.t, p.h
    s_bm, s_br, s_bs = p.sem("bim"), p.sem("bir"), p.sem("bis")
    s_ba, s_bg, s_bn = p.sem("bia"), p.sem("big"), p.sem("bin")
    s_tr, s_ct, s_lo = p.sem("btr"), p.sem("bct"), p.sem("blo")
    s_mk, s_so = p.sem("bmk"), p.sem("so")

    with nc.Block() as b, \
         nc.psum_tensor(_nm("ps_il"), [128, 4, 512], F32) as psl, \
         nc.sbuf_tensor(_nm("bi_r"), [128, S], F32) as relu_s, \
         nc.sbuf_tensor(_nm("bi_s0"), [128, S], F32) as isc0, \
         nc.sbuf_tensor(_nm("bi_s1"), [128, S], F32) as isc1, \
         nc.sbuf_tensor(_nm("bi_sg"), [128, S], F32) as sgn_s, \
         nc.sbuf_tensor(_nm("bi_mk0"), [128, S], BF16) as mk_bf0, \
         nc.sbuf_tensor(_nm("bi_mk1"), [128, S], BF16) as mk_bf1, \
         nc.sbuf_tensor(_nm("bi_st"), [128, 12], F32) as bst:
        B_bm, B_br, B_bs = s_bm.val, s_br.val, s_bs.val
        B_ba, B_bg, B_bn = s_ba.val, s_bg.val, s_bn.val
        B_tr, B_ct, B_lo = s_tr.val, s_ct.val, s_lo.val
        B_mk = s_mk.val
        iscs = [isc0, isc1]
        mks = [mk_bf0, mk_bf1]

        @b.tensor
        def _(tt):
            for j, qt in enumerate(qts):
                for hi in range(4):
                    k = j * 4 + hi
                    if k >= 1:
                        tt.wait_ge(s_br.h, B_br + k)
                    ins = None
                    for nb in range(NBK):
                        ins = tt.matmul(psl[:, nb, :],
                                        iq_bf[:, hi, qt * 128:(qt + 1) * 128],
                                        ik_bf[:, nb512(nb)], start=True, stop=True)
                    s_bm.inc(ins)

        @b.scalar
        def _(sc):
            sc = SerialEngine(p, sc)
            psl_flat = psl.rearrange("p b n -> p (b n)")
            for j, qt in enumerate(qts):
                for hi in range(4):
                    k = j * 4 + hi
                    sc.wait_ge(s_bm.h, B_bm + k + 1)
                    if k >= 1:
                        sc.wait_ge(s_bs.h, B_bs + k)
                    s_br.inc(sc.activation(relu_s[:, :], psl_flat, AF.Relu))
            for it in range(BISECT_ITERS):
                for j in range(2):
                    sc.wait_ge(s_tr.h, B_tr + it * 2 + j + 1)
                    s_ct.inc(sc.activation(sgn_s[:, :], iscs[j][:, :], AF.Sign,
                                           bias=bst[:, 6 + j:7 + j], scale=1.0,
                                           accum_out=bst[:, 2 + j:3 + j]))

        @b.gpsimd
        def _(g):
            g = SerialEngine(p, g)
            for j, qt in enumerate(qts):
                g.wait_ge(s_ba.h, B_ba + j + 1)
                g.affine_select(iscs[j][:, :], iscs[j][:, :], [[-1, S]], ALU.is_ge,
                                t["r_neg"], base=qt * 128, channel_multiplier=1)
                s_bg.inc(g.engine_nop())

        @b.vector
        def _(v):
            v = SerialEngine(p, v)
            for j, qt in enumerate(qts):
                for hi in range(4):
                    k = j * 4 + hi
                    v.wait_ge(s_br.h, B_br + k + 1)
                    wcol = iw_tm[:, qt, hi:hi + 1]
                    if hi == 0:
                        ins = v.tensor_single_scalar(iscs[j][:, :], relu_s[:, :],
                                                     wcol, ALU.mult)
                    else:
                        ins = v.scalar_tensor_tensor(iscs[j][:, :], relu_s[:, :],
                                                     wcol, iscs[j][:, :],
                                                     ALU.mult, ALU.add)
                    s_bs.inc(ins)
                s_ba.inc(v.reduce_max(bst[:, 0 + j:1 + j], iscs[j][:, :], axis=AX.X,
                                      apply_absolute_value=True))
                v.wait_ge(s_bg.h, B_bg + j + 1)
                s_bn.inc(v.tensor_scalar_add(bst[:, 4 + j:5 + j],
                                             bst[:, 0 + j:1 + j], 1.0))
                v.wait_ge(s_bn.h, s_bn.val)
                s_bn.inc(v.reciprocal(bst[:, 4 + j:5 + j], bst[:, 4 + j:5 + j]))
                v.wait_ge(s_bn.h, s_bn.val)
                edge(p, v, v.tensor_single_scalar(iscs[j][:, :], iscs[j][:, :],
                                                  bst[:, 4 + j:5 + j], ALU.mult))
                edge(p, v, v.tensor_single_scalar(sgn_s[:, :], iscs[j][:, :], 0.0,
                                                  ALU.is_equal))
                edge(p, v, v.tensor_tensor(sgn_s[:, :], sgn_s[:, :], iotaEta[:, :],
                                           ALU.mult))
                v.tensor_tensor(iscs[j][:, :], iscs[j][:, :], sgn_s[:, :],
                                ALU.subtract)
                v.memset(bst[:, 8 + j:9 + j], 1.1)
            for it in range(BISECT_ITERS):
                step = 2.2 * (2.0 ** -(it + 1))
                for j in range(2):
                    s_tr.inc(v.tensor_scalar_add(bst[:, 6 + j:7 + j],
                                                 bst[:, 8 + j:9 + j], -step))
                for j in range(2):
                    v.wait_ge(s_ct.h, B_ct + it * 2 + j + 1)
                    s_lo.inc(v.tensor_single_scalar(bst[:, 10 + j:11 + j],
                                                    bst[:, 2 + j:3 + j], -1024.0,
                                                    ALU.is_ge))
                    v.wait_ge(s_lo.h, s_lo.val)
                    s_lo.inc(v.scalar_tensor_tensor(bst[:, 8 + j:9 + j],
                                                    bst[:, 10 + j:11 + j], -step,
                                                    bst[:, 8 + j:9 + j],
                                                    ALU.mult, ALU.add))
            for j, qt in enumerate(qts):
                s_mk.inc(v.tensor_scalar_mul(bst[:, 4 + j:5 + j],
                                             bst[:, 8 + j:9 + j], -1.0))
                v.wait_ge(s_mk.h, s_mk.val)
                edge(p, v, v.tensor_single_scalar(sgn_s[:, :], iscs[j][:, :],
                                                  bst[:, 4 + j:5 + j], ALU.is_lt))
                s_mk.inc(v.tensor_single_scalar(mks[j][:, :], sgn_s[:, :], NEG,
                                                ALU.mult))

        @b.sync
        def _(sy):
            mso = p.sem("mso")
            for j, qt in enumerate(qts):
                sy.wait_ge(s_mk.h, B_mk + (j + 1) * 2)
                sy.dma_start(h["mask_scr"][qt], mks[j][:, :]).then_inc(mso.h, 16)
                mso.val += 16
            sy.wait_ge(mso.h, mso.val)



def build_program(nseq=NSEQ, parts=_PARTS):
    p = P(nseq, parts)
    nc = p.nc
    p.h["xin"] = nc.dram_tensor("xin", [nseq, D, S], F32, kind="ExternalInput").ap()
    p.h["xout"] = nc.dram_tensor("xout", [nseq, D, S], F32, kind="ExternalOutput").ap()
    p.h["mask_scr"] = nc.dram_tensor("mask_scr", [16, 128, S], BF16,
                                     kind="Internal").ap()
    p.h["wg_bf"] = nc.dram_tensor("wg_bf", [64, 128, 8, 512], BF16,
                                  kind="Internal").ap()
    p.h["wu_bf"] = nc.dram_tensor("wu_bf", [64, 128, 8, 512], BF16,
                                  kind="Internal").ap()
    p.h["wd_bf"] = nc.dram_tensor("wd_bf", [64, 128, 4, 1024], BF16,
                                  kind="Internal").ap()
    p.h["xg_dram"] = nc.dram_tensor("xg_dram", [16 * 512, 1024], BF16,
                                    kind="Internal").ap()
    p.h["os_dram"] = nc.dram_tensor("os_dram", [16 * 512, 1024], BF16,
                                    kind="Internal").ap()

    shapes = {
        "a_wdqT": [128, 8, 512], "a_wdkvT": [128, 8, 512],
        "a_wupqT": [128, 4, 1024], "a_wupkT": [128, 4, 1024],
        "a_wupvT": [128, 4, 1024], "a_woT": [128, 8, 1024],
        "a_lns": [128, 8], "a_lnb": [128, 8],
        "r_cos": [128, S], "r_sin": [128, S],
        "f_wgT": [128, 8, 8], "f_bg": [128, 8],
        "f_WgS": [64, 128, 8, 512], "f_WuS": [64, 128, 8, 512],
        "f_WdS": [64, 128, 4, 1024], "iotaT": [8, 16], "piota": [128, 1],
        "b_wq": [128, 8, 1024], "b_wk": [128, 8, 1024], "b_wv": [128, 8, 1024],
        "b_wo": [128, 8, 1024],
        "b_iwq": [128, 8, 256], "b_iwk": [128, 8, 64], "b_iww": [128, 8, 4],
        "b_gw": [128, 8, 4], "b_n1": [128, 8], "b_n2": [128, 8],
        "b_w1": [4, 128, 8, 4096], "b_w2": [4, 128, 32, 1024],
        "b_b1": [4, 128, 32], "b_b2": [4, 128, 8],
    }
    for name, shape in shapes.items():
        p.h[name] = nc.dram_tensor(name, shape, F32, kind="ExternalInput").ap()

    t = p.t
    t["ident"] = nc.sbuf_tensor(_nm("c_ident"), [128, 128], BF16).__enter__()
    t["id64"] = nc.sbuf_tensor(_nm("c_id64"), [128, 64], BF16).__enter__()
    t["identf"] = nc.sbuf_tensor(_nm("c_idf"), [128, 128], F32).__enter__()
    t["ones_c"] = nc.sbuf_tensor(_nm("c_onc"), [128, 1], F32).__enter__()
    t["ones_r"] = nc.sbuf_tensor(_nm("c_onr"), [65, 128], F32).__enter__()
    t["sel8"] = nc.sbuf_tensor(_nm("c_sel8"), [8, 8 * 128], F32).__enter__()
    t["sel4"] = nc.sbuf_tensor(_nm("c_sel4"), [4, 4 * 128], F32).__enter__()
    t["lns"] = nc.sbuf_tensor(_nm("c_lns"), [128, 8], F32).__enter__()
    t["lnb"] = nc.sbuf_tensor(_nm("c_lnb"), [128, 8], F32).__enter__()
    t["bn1"] = nc.sbuf_tensor(_nm("c_bn1"), [128, 8], F32).__enter__()
    t["bn2"] = nc.sbuf_tensor(_nm("c_bn2"), [128, 8], F32).__enter__()
    t["fbg"] = nc.sbuf_tensor(_nm("c_fbg"), [128, 8], F32).__enter__()
    t["fwgT"] = nc.sbuf_tensor(_nm("c_fwgT"), [128, 8, 8], BF16).__enter__()
    t["bgw"] = nc.sbuf_tensor(_nm("c_bgw"), [128, 8, 4], BF16).__enter__()
    t["biww"] = nc.sbuf_tensor(_nm("c_biww"), [128, 8, 4], BF16).__enter__()
    t["bb1"] = nc.sbuf_tensor(_nm("c_bb1"), [128, 4, 32], F32).__enter__()
    t["ones128"] = nc.sbuf_tensor(_nm("c_o128"), [128, 128], BF16).__enter__()
    t["ltri"] = nc.sbuf_tensor(_nm("c_ltri"), [128, 128], BF16).__enter__()
    t["l8"] = nc.sbuf_tensor(_nm("c_l8"), [8, 8], BF16).__enter__()
    t["iotaT"] = nc.sbuf_tensor(_nm("c_iotaT"), [8, 16], F32).__enter__()
    t["piota"] = nc.sbuf_tensor(_nm("c_piota"), [128, 1], F32).__enter__()
    t["bb2"] = nc.sbuf_tensor(_nm("c_bb2"), [128, 4, 8], F32).__enter__()
    t["cpool"] = {}
    for vi, val in enumerate((LN_EPS, RMS_EPS, -0.5, 0.125, -1.0, 512.0)):
        tile = nc.sbuf_tensor(_nm(f"c_v{vi}"), [128, 1], F32).__enter__()
        nc.const_aps.register(F32, float(val), tile[:, :])
        t["cpool"][float(val)] = tile

    x_sb = nc.sbuf_tensor(_nm("x_sb"), [128, NCH, S], F32).__enter__()

    emit_consts(p)
    if "ffn1" in parts or "ffn2" in parts:
        emit_wconv(p)
    for sq in range(nseq):
        emit_load_x(p, sq, x_sb)
        if "attn_a" in p.parts:
            emit_attn_a(p, x_sb)
        if "ffn1" in p.parts:
            _emit_ffn_call(p, x_sb)
        if "attn_b" in p.parts:
            emit_attn_b(p, x_sb)
        if "ffn2" in p.parts:
            _emit_ffn_call(p, x_sb)
        emit_store_x(p, sq, x_sb)

    st = p.sem("so")
    with nc.Block() as b:
        @b.sync
        def _(sy):
            sy.wait_ge(st.h, st.val)
    return p


NT = 16


def emit_ffn_sparse(p, x_sb, xn_bf, xn_guard):
    nc, t, h = p.nc, p.t, p.h

    with nc.sbuf_tensor(_nm("s_wde"), [8, 16, 128], F32) as wde, \
         nc.sbuf_tensor(_nm("s_wdtm"), [128, 16, 8], F32) as wdtm, \
         nc.sbuf_tensor(_nm("s_spos"), [128, 16, 8], mybir.dt.int32) as spos_i, \
         nc.sbuf_tensor(_nm("s_idxg"), [128, NT, 8], mybir.dt.int32) as idxg_i:
        emit_gating(p, xn_bf, t["fwgT"], t["fbg"], 8, wde, wdtm_sb=wdtm)
        _ffn_routing(p, wde, wdtm, spos_i, idxg_i)
        _ffn_scatter(p, xn_bf, spos_i)
        xn_guard.__exit__(None, None, None)
        _ffn_tiles(p, idxg_i)
        _ffn_combine(p, x_sb, spos_i, wdtm)


def _ffn_routing(p, wde, wdtm, spos_i, idxg_i):
    nc, t, h = p.nc, p.t, p.h
    rpe, rdv, rac = p.sem("rpe"), p.sem("rdv"), p.sem("rac")

    with nc.sbuf_tensor(_nm("s_sel"), [8, S], BF16) as sel01, \
         nc.sbuf_tensor(_nm("s_selT"), [128, 16, 8], BF16) as selT, \
         nc.sbuf_tensor(_nm("s_pos"), [128, 16, 8], F32) as pos_tm, \
         nc.sbuf_tensor(_nm("s_r8"), [8, 8], F32) as r8, \
         nc.sbuf_tensor(_nm("s_ri8"), [8, 8], mybir.dt.int32) as ri8, \
         nc.sbuf_tensor(_nm("s_rb8"), [8, 1], BF16) as rb8, \
         nc.sbuf_tensor(_nm("s_cmp"), [8, 16], F32) as cmp16, \
         nc.sbuf_tensor(_nm("s_ter"), [1, 2 * NT], F32) as te_f, \
         nc.sbuf_tensor(_nm("s_brep"), [128, 8], F32) as b_rep, \
         nc.sbuf_tensor(_nm("s_m01"), [128, 8], F32) as m01:
        wde_flat = wde.rearrange("e q n -> e (q n)")

        with nc.Block() as b, \
             nc.psum_tensor(_nm("ps_r1"), [128, 2, 1024], BF16) as pss, \
             nc.psum_tensor(_nm("ps_r1b"), [128, 2, 512], F32) as psb:
            B_pe, B_dv, B_ac = rpe.val, rdv.val, rac.val

            @b.vector
            def _(v):
                v = SerialEngine(p, v)
                rdv.inc(v.tensor_single_scalar(sel01[:, :], wde_flat, 0.0, ALU.is_gt))
                v.reduce_sum(r8[:, 0:1], sel01[:, :], axis=AX.X)
                v.tensor_copy(ri8[:, 0:1], r8[:, 0:1])
                v.tensor_scalar_add(ri8[:, 1:2], ri8[:, 0:1], 511)
                v.tensor_single_scalar(ri8[:, 2:3], ri8[:, 1:2], 9,
                                       ALU.logical_shift_right)
                rdv.inc(v.tensor_copy(rb8[:, 0:1], ri8[:, 2:3]))

            @b.tensor
            def _(tt):
                tt.wait_ge(rdv.h, B_dv + 1)
                for i in range(16):
                    if i >= 2:
                        tt.wait_ge(rac.h, B_ac + i - 1)
                    rpe.inc(tt.transpose(pss[0:128, i % 2, 0:8],
                                         sel01[:, i * 128:(i + 1) * 128],
                                         t["ident"][0:8, 0:8]))
                tt.wait_ge(rdv.h, B_dv + 2)
                rpe.inc(tt.matmul(psb[0:8, 0, 0:1], t["l8"][:, :], rb8[:, 0:1],
                                  start=True, stop=True))

            @b.scalar
            def _(sc):
                sc = SerialEngine(p, sc)
                for i in range(16):
                    sc.wait_ge(rpe.h, B_pe + i + 1)
                    rac.inc(sc.activation(selT[:, i, :], pss[0:128, i % 2, 0:8],
                                          AF.Identity))
                sc.wait_ge(rpe.h, B_pe + 17)
                rac.inc(sc.activation(r8[:, 4:5], psb[0:8, 0, 0:1], AF.Identity,
                                      scale=512.0))

        with nc.Block() as b, \
             nc.psum_tensor(_nm("ps_r2"), [128, 4, 512], F32) as ps2, \
             nc.sbuf_tensor(_nm("s_ter"), [128, NT], F32) as ter_sb, \
             nc.sbuf_tensor(_nm("s_tmp"), [128, NT], F32) as tmp_sb:
            B_pe, B_dv, B_ac = rpe.val, rdv.val, rac.val

            @b.tensor
            def _(tt):
                rpe.inc(tt.transpose(ps2[0:1, 0, 0:8], r8[:, 4:5],
                                     t["identf"][0:8, 0:8]))
                tt.wait_ge(rac.h, B_ac + 1)
                rpe.inc(tt.matmul(ps2[:, 1, 0:8], t["ones_r"][0:1, :],
                                  te_f[:, NT:NT + 8], start=True, stop=True))
                tt.wait_ge(rdv.h, B_dv + 1)
                rpe.inc(tt.matmul(ps2[0:1, 2, 0:NT], t["ones_c"][0:8, 0:1],
                                  cmp16[:, :], start=True, stop=True))
                tt.wait_ge(rac.h, B_ac + 3)
                rpe.inc(tt.matmul(ps2[:, 3, 0:NT], t["ones_r"][0:1, :],
                                  te_f[:, 0:NT], start=True, stop=True))

            @b.scalar
            def _(sc):
                sc = SerialEngine(p, sc)
                sc.wait_ge(rpe.h, B_pe + 1)
                rac.inc(sc.activation(te_f[:, NT:NT + 8], ps2[0:1, 0, 0:8],
                                      AF.Identity))
                sc.wait_ge(rpe.h, B_pe + 2)
                rac.inc(sc.activation(b_rep[:, :], ps2[:, 1, 0:8], AF.Identity))
                sc.wait_ge(rpe.h, B_pe + 3)
                rac.inc(sc.activation(te_f[:, 0:NT], ps2[0:1, 2, 0:NT], AF.Identity,
                                      bias=-1.0))
                sc.wait_ge(rpe.h, B_pe + 4)
                rac.inc(sc.activation(ter_sb[:, :], ps2[:, 3, 0:NT], AF.Identity))

            @b.vector
            def _(v):
                v = SerialEngine(p, v)
                rdv.inc(v.tensor_single_scalar(cmp16[:, :], t["iotaT"][:, :],
                                               r8[:, 4:5], ALU.is_ge))
                v.wait_ge(rac.h, B_ac + 4)
                for hb in range(8):
                    v.tensor_scalar(tmp_sb[:, :], ter_sb[:, :], 1024.0,
                                    float(hb * 128), ALU.mult, ALU.add)
                    v.tensor_single_scalar(tmp_sb[:, :], tmp_sb[:, :],
                                           t["piota"][:, 0:1], ALU.add)
                    rdv.inc(v.tensor_copy(idxg_i[:, :, hb], tmp_sb[:, :]))

        with nc.Block() as b, \
             nc.psum_tensor(_nm("ps_r3"), [128, 2, 512], F32) as ps3:
            B_pe, B_dv, B_ac = rpe.val, rdv.val, rac.val

            @b.tensor
            def _(tt):
                for i in range(16):
                    if i >= 2:
                        tt.wait_ge(rac.h, B_ac + i - 1)
                    ins = None
                    for j in range(i + 1):
                        lh = t["ltri"] if j == i else t["ones128"]
                        ins = tt.matmul(ps3[:, i % 2, 0:8], lh[:, :], selT[:, j, :],
                                        start=(j == 0), stop=(j == i))
                    rpe.inc(ins)

            @b.scalar
            def _(sc):
                sc = SerialEngine(p, sc)
                for i in range(16):
                    sc.wait_ge(rpe.h, B_pe + i + 1)
                    rac.inc(sc.activation(pos_tm[:, i, :], ps3[:, i % 2, 0:8],
                                          AF.Identity))

            @b.vector
            def _(v):
                v = SerialEngine(p, v)
                for i in range(16):
                    v.wait_ge(rac.h, B_ac + i + 1)
                    v.tensor_tensor(pos_tm[:, i, :], pos_tm[:, i, :], b_rep[:, :],
                                    ALU.add)
                    v.tensor_single_scalar(m01[:, :], wdtm[:, i, :], 0.0, ALU.is_le)
                    v.scalar_tensor_tensor(pos_tm[:, i, :], m01[:, :], 1e9,
                                           pos_tm[:, i, :], ALU.mult, ALU.add)
                    rdv.inc(v.tensor_copy(spos_i[:, i, :], pos_tm[:, i, :]))


def _ffn_scatter(p, xn_bf, spos_i):
    nc, t, h = p.nc, p.t, p.h
    spe, sac = p.sem("spe"), p.sem("sac")
    sdm = [p.sem("sdm0"), p.sem("sdm1")]
    with nc.Block() as b, \
         nc.psum_tensor(_nm("ps_xt"), [128, 2, 8, 128], BF16) as pst, \
         nc.sbuf_tensor(_nm("s_xtm"), [128, 2, 1024], BF16) as xtm:
        B_pe, B_ac = spe.val, sac.val
        B_dm = [sdm[0].val, sdm[1].val]

        @b.tensor
        def _(tt):
            for i in range(16):
                if i >= 2:
                    tt.wait_ge(sac.h, B_ac + i - 1)
                ins = None
                for c in range(NCH):
                    ins = tt.transpose(pst[:, i % 2, c, :],
                                       xn_bf[:, c, i * 128:(i + 1) * 128],
                                       t["ident"][:, :])
                spe.inc(ins)

        @b.scalar
        def _(sc):
            for i in range(16):
                sc.wait_ge(spe.h, B_pe + i + 1)
                if i >= 2:
                    sc.wait_ge(sdm[i % 2].h, B_dm[i % 2] + 128 * (i // 2))
                sac.inc(sc.activation(xtm[:, i % 2, :],
                                      pst[:, i % 2, :, :].rearrange("p c n -> p (c n)"),
                                      AF.Identity))

        @b.gpsimd
        def _(g):
            for i in range(16):
                g.wait_ge(sac.h, B_ac + i + 1)
                for e in range(8):
                    g.indirect_dma_start(
                        h["xg_dram"][:, :],
                        IndirectOffsetOnAxis(spos_i[:, i, e:e + 1], 0),
                        xtm[:, i % 2, :], None,
                        bounds_check=t["r_bc"], oob_is_err=False,
                    ).then_inc(sdm[i % 2].h, 16)
                    sdm[i % 2].val += 16
            g.wait_ge(sdm[0].h, sdm[0].val)
            g.wait_ge(sdm[1].h, sdm[1].val)


_TILES_PHASES = 4
_T3_MODE = "full"


def _ffn_tiles(p, idxg_i):
    nc, t, h = p.nc, p.t, p.h
    fgm, fum, fsl, fh = p.sem("fgm"), p.sem("fum"), p.sem("fsl"), p.sem("fh")
    fdm, fds = p.sem("fdm"), p.sem("fds")
    s_g = [p.sem("wga0"), p.sem("wga1")]
    s_u = [p.sem("wgu0"), p.sem("wgu1")]
    s_d = [p.sem("wgd0"), p.sem("wgd1")]
    s_x = p.sem("wxg")
    tpe, tac = p.sem("tpe"), p.sem("tac")

    with nc.sbuf_tensor(_nm("t_h"), [128, 32, 512], BF16) as h_bf, \
         nc.sbuf_tensor(_nm("t_wga"), [128, 8, 512], BF16) as wga, \
         nc.sbuf_tensor(_nm("t_wgb"), [128, 8, 512], BF16) as wgb, \
         nc.sbuf_tensor(_nm("t_wua"), [128, 8, 512], BF16) as wua, \
         nc.sbuf_tensor(_nm("t_wub"), [128, 8, 512], BF16) as wub, \
         nc.sbuf_tensor(_nm("t_wd0"), [128, 4, 1024], BF16) as wd0, \
         nc.sbuf_tensor(_nm("t_wd1"), [128, 4, 1024], BF16) as wd1, \
         nc.sbuf_tensor(_nm("t_xf"), [128, 8, 512], BF16) as xg_f, \
         nc.sbuf_tensor(_nm("t_xtm"), [128, 4, 1024], BF16) as xg_tm, \
         nc.sbuf_tensor(_nm("t_sg"), [128, 4, 512], BF16) as sg, \
         nc.sbuf_tensor(_nm("t_of"), [128, 8, 512], BF16) as oslot_f:
        wds = [wd0, wd1]
        gbuf = [(wga, wua), (wgb, wub)]

        for ti in range(NT):
            with nc.Block() as b, \
                 nc.psum_tensor(_nm("ps_ti"), [128, 2, 8, 128], BF16) as psi:
                B_x, B_pe, B_ac = s_x.val, tpe.val, tac.val

                @b.gpsimd
                def _(g):
                    for sh, sv in t.pop("wconv_done", ()):
                        g.wait_ge(sh, sv)
                    g.dma_start(xg_tm[:, :, :],
                                h["xg_dram"][ti * 512:(ti + 1) * 512].rearrange(
                                    "(a p) n -> p a n", p=128)).then_inc(s_x.h, 16)
                    s_x.val += 16
                    for hb in range(2):
                        g.indirect_dma_start(
                            gbuf[hb][0].rearrange("p c n -> p (c n)"), None,
                            h["wg_bf"].rearrange("s p c n -> (s p) (c n)"),
                            IndirectOffsetOnAxis(idxg_i[:, ti, hb:hb + 1], 0),
                        ).then_inc(s_g[hb].h, 16)
                        s_g[hb].val += 16
                        g.indirect_dma_start(
                            gbuf[hb][1].rearrange("p c n -> p (c n)"), None,
                            h["wu_bf"].rearrange("s p c n -> (s p) (c n)"),
                            IndirectOffsetOnAxis(idxg_i[:, ti, hb:hb + 1], 0),
                        ).then_inc(s_u[hb].h, 16)
                        s_u[hb].val += 16

                @b.tensor
                def _(tt):
                    tt.wait_ge(s_x.h, B_x + 16)
                    for a in range(4):
                        if a >= 2:
                            tt.wait_ge(tac.h, B_ac + a - 1)
                        ins = None
                        for c in range(NCH):
                            ins = tt.transpose(psi[:, a % 2, c, :],
                                               xg_tm[:, a, c * 128:(c + 1) * 128],
                                               t["ident"][:, :])
                        tpe.inc(ins)

                @b.scalar
                def _(sc):
                    for a in range(4):
                        sc.wait_ge(tpe.h, B_pe + a + 1)
                        tac.inc(sc.activation(
                            xg_f[:, :, a * 128:(a + 1) * 128],
                            psi[:, a % 2, :, :], AF.Identity))

            if _TILES_PHASES < 2:
                continue
            with nc.Block() as b, \
                 nc.psum_tensor(_nm("ps_tg"), [128, 4, 512], F32) as psg, \
                 nc.psum_tensor(_nm("ps_tu"), [128, 4, 512], F32) as psu:
                b_gm, b_um, b_sl, b_h = fgm.val, fum.val, fsl.val, fh.val
                b_g = [s_g[0].val, s_g[1].val]
                b_u = [s_u[0].val, s_u[1].val]

                @b.gpsimd
                def _(g):
                    for hb in range(2, 8):
                        g.wait_ge(fgm.h, b_gm + hb - 1)
                        g.indirect_dma_start(
                            gbuf[hb % 2][0].rearrange("p c n -> p (c n)"), None,
                            h["wg_bf"].rearrange("s p c n -> (s p) (c n)"),
                            IndirectOffsetOnAxis(idxg_i[:, ti, hb:hb + 1], 0),
                        ).then_inc(s_g[hb % 2].h, 16)
                        s_g[hb % 2].val += 16
                        g.wait_ge(fum.h, b_um + hb - 1)
                        g.indirect_dma_start(
                            gbuf[hb % 2][1].rearrange("p c n -> p (c n)"), None,
                            h["wu_bf"].rearrange("s p c n -> (s p) (c n)"),
                            IndirectOffsetOnAxis(idxg_i[:, ti, hb:hb + 1], 0),
                        ).then_inc(s_u[hb % 2].h, 16)
                        s_u[hb % 2].val += 16

                @b.tensor
                def _(tt):
                    for hb in range(8):
                        tt.wait_ge(s_g[hb % 2].h, b_g[hb % 2] + 16 * (hb // 2))
                        if hb >= 1:
                            tt.wait_ge(fsl.h, b_sl + hb)
                        ins = None
                        for mc in range(4):
                            for c in range(NCH):
                                ins = tt.matmul(psg[:, mc, :],
                                                gbuf[hb % 2][0][:, c, mc * 128:(mc + 1) * 128],
                                                xg_f[:, c, :],
                                                start=(c == 0), stop=(c == NCH - 1))
                        fgm.inc(ins)
                        tt.wait_ge(s_u[hb % 2].h, b_u[hb % 2] + 16 * (hb // 2))
                        if hb >= 1:
                            tt.wait_ge(fh.h, b_h + hb)
                        ins = None
                        for mc in range(4):
                            for c in range(NCH):
                                ins = tt.matmul(psu[:, mc, :],
                                                gbuf[hb % 2][1][:, c, mc * 128:(mc + 1) * 128],
                                                xg_f[:, c, :],
                                                start=(c == 0), stop=(c == NCH - 1))
                        fum.inc(ins)

                @b.scalar
                def _(sc):
                    for hb in range(8):
                        sc.wait_ge(fgm.h, b_gm + hb + 1)
                        if hb >= 1:
                            sc.wait_ge(fh.h, b_h + hb)
                        fsl.inc(sc.activation(sg.rearrange("p c n -> p (c n)"),
                                              psg.rearrange("p c n -> p (c n)"),
                                              AF.Silu))

                @b.vector
                def _(v):
                    for hb in range(8):
                        v.wait_ge(fsl.h, b_sl + hb + 1)
                        v.wait_ge(fum.h, b_um + hb + 1)
                        fh.inc(v.tensor_tensor(
                            h_bf[:, hb * 4:(hb + 1) * 4, :].rearrange("p c n -> p (c n)"),
                            sg.rearrange("p c n -> p (c n)"),
                            psu.rearrange("p c n -> p (c n)"), ALU.mult))

            if _TILES_PHASES < 3:
                continue
            with nc.Block() as b, \
                 nc.psum_tensor(_nm("ps_td"), [128, 8, 512], F32) as psd:
                b_dm, b_ds = fdm.val, fds.val
                b_d = [s_d[0].val, s_d[1].val]

                @b.gpsimd
                def _(g):
                    for sl in range(8):
                        if sl >= 2 and _T3_MODE != "g":
                            g.wait_ge(fds.h, b_ds + sl - 1)
                        g.indirect_dma_start(
                            wds[sl % 2].rearrange("p c n -> p (c n)"), None,
                            h["wd_bf"].rearrange("s p c n -> (s p) (c n)"),
                            IndirectOffsetOnAxis(idxg_i[:, ti, sl:sl + 1], 0),
                        ).then_inc(s_d[sl % 2].h, 16)
                        s_d[sl % 2].val += 16

                @b.tensor
                def _(tt):
                    if _T3_MODE == "g":
                        return
                    for sl in range(8):
                        tt.wait_ge(s_d[sl % 2].h, b_d[sl % 2] + 16 * (sl // 2 + 1))
                        for fo in range(8):
                            ins = None
                            for c in range(4):
                                ins = tt.matmul(psd[:, fo, :],
                                                wds[sl % 2][:, c, fo * 128:(fo + 1) * 128],
                                                h_bf[:, sl * 4 + c, :],
                                                start=(sl == 0 and c == 0),
                                                stop=(sl == 7 and c == 3))
                            if sl == 7:
                                fdm.inc(ins)
                        if sl < 7:
                            fds.inc(ins)

                @b.scalar
                def _(sc):
                    if _T3_MODE in ("g", "gm"):
                        return
                    for fo in range(8):
                        sc.wait_ge(fdm.h, b_dm + fo + 1)
                        sc.activation(oslot_f[:, fo, :], psd[:, fo, :], AF.Identity)

            if _TILES_PHASES < 4:
                continue
            with nc.Block() as b, \
                 nc.psum_tensor(_nm("ps_to"), [128, 2, 8, 128], BF16) as pso:
                B_pe, B_ac, B_x = tpe.val, tac.val, s_x.val

                @b.tensor
                def _(tt):
                    for a in range(4):
                        if a >= 2:
                            tt.wait_ge(tac.h, B_ac + a - 1)
                        ins = None
                        for c in range(NCH):
                            ins = tt.transpose(pso[:, a % 2, c, :],
                                               oslot_f[:, c, a * 128:(a + 1) * 128],
                                               t["ident"][:, :])
                        tpe.inc(ins)

                @b.scalar
                def _(sc):
                    for a in range(4):
                        sc.wait_ge(tpe.h, B_pe + a + 1)
                        tac.inc(sc.activation(xg_tm[:, a, :],
                                              pso[:, a % 2, :, :].rearrange(
                                                  "p c n -> p (c n)"),
                                              AF.Identity))

                @b.gpsimd
                def _(g):
                    g.wait_ge(tac.h, B_ac + 4)
                    g.dma_start(h["os_dram"][ti * 512:(ti + 1) * 512].rearrange(
                        "(a p) n -> p a n", p=128),
                        xg_tm[:, :, :]).then_inc(s_x.h, 16)
                    s_x.val += 16
                    g.wait_ge(s_x.h, s_x.val)


def _ffn_combine(p, x_sb, spos_i, wdtm):
    nc, t, h = p.nc, p.t, p.h
    cdm = [p.sem("cdm0"), p.sem("cdm1")]
    cac_, cyb = p.sem("cac"), p.sem("cyb")
    cpe, cxa, cgp = p.sem("cpe"), p.sem("cxa"), p.sem("cgp")
    with nc.Block() as b, \
         nc.psum_tensor(_nm("ps_cb"), [128, 2, 8, 128], BF16) as psc, \
         nc.sbuf_tensor(_nm("c_gt"), [128, 2, 1024], BF16) as gtile, \
         nc.sbuf_tensor(_nm("c_y"), [128, 1024], F32) as y_tm, \
         nc.sbuf_tensor(_nm("c_yb"), [128, 1024], BF16) as y_bf:
        B_dm = [cdm[0].val, cdm[1].val]
        B_ac, B_yb = cac_.val, cyb.val
        B_pe, B_xa, B_gp = cpe.val, cxa.val, cgp.val

        @b.gpsimd
        def _(g):
            cgp.inc(g.memset(gtile[:, :, :], 0.0))
            g.wait_ge(cgp.h, cgp.val)
            k = 0
            for i in range(16):
                for e in range(8):
                    if k >= 2:
                        g.wait_ge(cac_.h, B_ac + k - 1)
                    g.indirect_dma_start(
                        gtile[:, k % 2, :], None,
                        h["os_dram"][:, :],
                        IndirectOffsetOnAxis(spos_i[:, i, e:e + 1], 0),
                        bounds_check=t["r_bc"], oob_is_err=False,
                    ).then_inc(cdm[k % 2].h, 16)
                    cdm[k % 2].val += 16
                    k += 1

        @b.vector
        def _(v):
            v = SerialEngine(p, v)
            k = 0
            for i in range(16):
                for e in range(8):
                    v.wait_ge(cdm[k % 2].h, B_dm[k % 2] + 16 * (k // 2 + 1))
                    wcol = wdtm[:, i, e:e + 1]
                    if e == 0:
                        ins = v.tensor_single_scalar(y_tm[:, :], gtile[:, k % 2, :],
                                                     wcol, ALU.mult)
                    else:
                        ins = v.scalar_tensor_tensor(y_tm[:, :], gtile[:, k % 2, :],
                                                     wcol, y_tm[:, :],
                                                     ALU.mult, ALU.add)
                    cac_.inc(ins)
                    k += 1
                cyb.inc(v.tensor_copy(y_bf[:, :], y_tm[:, :]))
                v.wait_ge(cpe.h, B_pe + i + 1)
                cxa.inc(v.tensor_tensor(
                    x_sb[:, :, i * 128:(i + 1) * 128],
                    x_sb[:, :, i * 128:(i + 1) * 128],
                    psc[:, i % 2, :, :], ALU.add))

        @b.tensor
        def _(tt):
            for i in range(16):
                tt.wait_ge(cyb.h, B_yb + i + 1)
                if i >= 2:
                    tt.wait_ge(cxa.h, B_xa + i - 1)
                ins = None
                for c in range(NCH):
                    ins = tt.transpose(psc[:, i % 2, c, :],
                                       y_bf[:, c * 128:(c + 1) * 128],
                                       t["ident"][:, :])
                cpe.inc(ins)


def _emit_ffn_call(p, x_sb):
    nc, t = p.nc, p.t
    xn_g = nc.sbuf_tensor(_nm("f_xn"), [128, NCH, S], BF16, side="right")
    xn_bf = xn_g.__enter__()
    with nc.sbuf_tensor(_nm("f_mu1"), [128, S], F32) as mu1, \
         nc.sbuf_tensor(_nm("f_rs1"), [128, S], F32) as rs1:
        emit_stats(p, x_sb, mu1, rs1, LN_EPS, "x")
        emit_ln_apply(p, x_sb, mu1, rs1, xn_bf)
    emit_ffn_sparse(p, x_sb, xn_bf, xn_g)



_CACHE = {}


def kernel(**inputs):
    weights = prep_inputs(inputs)
    x = np.asarray(inputs["L0_attn_a_x"], np.float32)
    xt = np.ascontiguousarray(x.transpose(0, 2, 1))

    key = ("full", NSEQ, _PARTS)
    if key not in _CACHE:
        _CACHE[key] = build_program(NSEQ, _PARTS)
    p = _CACHE[key]

    in_maps = []
    for core in range(NCORES):
        m = {"xin": np.ascontiguousarray(xt[core * NSEQ:(core + 1) * NSEQ])}
        m.update(weights)
        in_maps.append(m)
    res = run_bass_kernel_spmd(p.nc, in_maps, core_ids=list(range(NCORES)))
    out = np.concatenate([r["xout"] for r in res.results], axis=0)
    return np.ascontiguousarray(out.transpose(0, 2, 1))
